# version 27 (final)
# baseline (speedup 1.0000x reference)
import numpy as np
from contextlib import ExitStack
import concourse.bass as bass
import concourse.mybir as mybir
from concourse.bass_utils import run_bass_kernel_spmd

F32 = mybir.dt.float32
BF16 = mybir.dt.bfloat16
AF = mybir.ActivationFunctionType
ALU = mybir.AluOpType

T = 4096
D = 1024
DBG = {"nch": None}
C = 128
NEG = -1.0e30


ENGS = ("t", "v", "s", "g", "y")
DMA_SEM = {"t": None, "v": "dv", "s": "ds", "g": "dg", "y": "dy"}
ENG_SEM = {"t": "et", "v": "ev", "s": "es", "g": "eg", "y": None}


class Prog:
    def __init__(self, nc):
        self.nc = nc
        self.cnt = {n: 0 for n in ("et", "ev", "es", "eg", "dy", "dg", "ds", "dv")}
        self.sems = {}
        self.waited = {e: {} for e in ENGS}
        self.bounds = {n: [0] for n in ("dy", "dg", "ds", "dv")}
        self.th = None
        self.last = None
        self.dirty = None
        self.last_tok = {}
        self._need_drain = {e: False for e in ENGS}

    def open_sems(self, stack):
        for n in self.cnt:
            self.sems[n] = stack.enter_context(self.nc.semaphore(n))

    def begin(self):
        self.th = {e: [] for e in ENGS}
        self.last = {e: None for e in ENGS}
        self.last_comp = {e: None for e in ENGS}
        self.dirty = {e: False for e in ENGS}
        self._sq_tok = {}
        self._qser = None

    def finish_pass(self):
        self.barrier()
        with self.nc.Block() as block:
            self.run(block)

    def emit(self, e, fn, dr=True):
        if dr and e in ("v", "s", "g") and self._need_drain.get(e):
            self.th[e].append({"fn": lambda E: E.drain(), "incs": None})
            self._need_drain[e] = False
        item = {"fn": fn, "incs": []}
        self.th[e].append(item)
        self.last[e] = item
        self.last_comp[e] = item
        self.dirty[e] = True
        self._need_drain[e] = True
        return item

    def wait(self, e, tok):
        if tok is None:
            return
        sem, val = tok
        if sem in self.QSEMS and DMA_SEM.get(e) != sem:
            bs = self.bounds[sem]
            up = [b for b in bs if b >= val]
            if up:
                val = up[0]
            else:
                self.fence(self.QOWNER[sem])
                val = self.bounds[sem][-1]
        if self.waited[e].get(sem, 0) >= val:
            return
        self.waited[e][sem] = val
        h = self.sems[sem]
        self.th[e].append({"fn": lambda E, h=h, val=val: E.wait_ge(h, val), "incs": None})

    QSEMS = ("dy", "dg", "ds", "dv")
    QOWNER = {"dy": "y", "dg": "g", "ds": "s", "dv": "v"}

    def fence(self, q):
        sem = DMA_SEM[q]
        val = self.cnt[sem]
        if self.bounds[sem] and self.bounds[sem][-1] >= val:
            return (sem, val)
        h = self.sems[sem]
        self.th[q].append({"fn": lambda E, h=h, val=val: E.wait_ge(h, val), "incs": None})
        self.waited[q][sem] = max(self.waited[q].get(sem, 0), val)
        self.bounds[sem].append(val)
        return (sem, val)

    def waits(self, e, toks):
        for tk in toks:
            self.wait(e, tk)

    def inc(self, e):
        item = self.last_comp[e]
        sem = ENG_SEM[e]
        assert item is not None and sem is not None
        for s, v in item["incs"]:
            if s == sem:
                return (sem, v)
        self.cnt[sem] += 1
        item["incs"].append((sem, self.cnt[sem]))
        return (sem, self.cnt[sem])

    def dma(self, q, out, in_, deps=()):
        self.waits(q, deps)
        sem = DMA_SEM[q]
        self.cnt[sem] += 16
        item = {"fn": lambda E, out=out, in_=in_: E.dma_start(out=out, in_=in_),
                "incs": [(sem, 16)], "dma": True}
        self.th[q].append(item)
        self.last[q] = item
        self.dirty[q] = True
        tok = (sem, self.cnt[sem])
        self.last_tok[sem] = tok
        return tok

    def op(self, e, fn, deps=(), dr=True):
        self.waits(e, deps)
        return self.emit(e, fn, dr=dr)

    def mm(self, out, lhsT, rhs, start, stop, deps=()):
        self.op("t", lambda E, o=out, l=lhsT, r=rhs, a=start, b=stop:
                E.matmul(o, l, r, start=a, stop=b), deps)

    def tr(self, out, in_, ident, deps=()):
        self.op("t", lambda E, o=out, i=in_, d=ident: E.transpose(o, i, d), deps)

    def run(self, block):
        for e in ENGS:
            items = self.th[e]
            if not items:
                continue
            sec = getattr(block, {"t": "tensor", "v": "vector", "s": "scalar",
                                  "g": "gpsimd", "y": "sync"}[e])

            def body(E, items=items):
                for it in items:
                    ins = it["fn"](E)
                    if it["incs"]:
                        for s, _v in it["incs"]:
                            ins.then_inc(self.sems[s], 16 if it.get("dma") else 1)
            sec(body)

    def barrier(self):
        toks = []
        for e in ENGS:
            if self.dirty[e] and ENG_SEM[e] and self.last_comp[e] is not None:
                toks.append(self.inc(e))
        for sem in ("dy", "dg", "ds", "dv"):
            if self.cnt[sem] > 0:
                toks.append((sem, self.cnt[sem]))
        for e in ENGS:
            self.waits(e, toks)


def batch(item_fn, n):
    return [item_fn(i) for i in range(n)]


def load_consts(P, nc, ar, c_in, names, tag=""):
    out = {}
    for nm in names:
        shape = list(c_in[nm].shape)
        ap = ar.enter_context(nc.sbuf_tensor(tag + "k_" + nm, shape, F32)).ap()
        P.dma("y", ap, c_in[nm])
        out[nm] = ap
    tok = ("dy", P.cnt["dy"])
    for e in ("t", "v", "s", "g"):
        P.wait(e, tok)
    return out


class PsumPool:

    def __init__(self, P, arena, n=4, name="pp"):
        self.P = P
        self.free_tok = [None] * n
        self.aps = [arena.enter_context(P.nc.psum_tensor(f"{name}{i}", [128, 1024], F32)).ap()
                    for i in range(n)]
        self.i = 0
        self.n = n

    def acquire(self):
        i = self.i
        self.i = (self.i + 1) % self.n
        tok = self.free_tok[i]
        return self.aps[i], i, tok

    def release(self, idx, tok):
        self.free_tok[idx] = tok



def ln_rows(P, ps_sx, ps_sq, row, ndfeat, eps, extra_ln=False):
    Tc = ps_sx.shape[-1]
    rD = 1.0 / ndfeat

    def sl(i):
        return row[0:1, i * Tc:(i + 1) * Tc]
    mu, m2, t0, t1, s_, o_ = (sl(i) for i in range(6))
    if extra_ln:
        sf, of = sl(6), sl(7)
    v, s = "v", "s"
    P.op(v, lambda E: E.tensor_scalar_mul(mu, ps_sx, rD))
    P.op(v, lambda E: E.tensor_scalar_mul(m2, ps_sq, rD))
    P.op(v, lambda E: E.tensor_tensor(t0, mu, mu, ALU.mult))
    P.op(v, lambda E: E.tensor_tensor(t0, m2, t0, ALU.subtract))
    P.op(v, lambda E: E.tensor_scalar_add(t0, t0, eps))
    P.op(v, lambda E: E.reciprocal(t0, t0))
    tok = P.inc(v)
    P.op(s, lambda E: E.activation(s_, t0, AF.Sqrt), deps=[tok])
    tok_s = P.inc(s)
    P.op(v, lambda E: E.tensor_tensor(o_, mu, s_, ALU.mult), deps=[tok_s])
    P.op(v, lambda E: E.tensor_scalar_mul(o_, o_, -1.0))
    if not extra_ln:
        P.inc(v)
        return s_, o_
    P.op(v, lambda E: E.tensor_tensor(t1, s_, mu, ALU.mult))
    P.op(v, lambda E: E.tensor_tensor(mu, t1, o_, ALU.add))
    P.op(v, lambda E: E.tensor_scalar_mul(t1, t1, 2.0))
    P.op(v, lambda E: E.tensor_tensor(t1, t1, o_, ALU.add))
    P.op(v, lambda E: E.tensor_tensor(t1, t1, o_, ALU.mult))
    P.op(v, lambda E: E.tensor_tensor(t0, s_, s_, ALU.mult))
    P.op(v, lambda E: E.tensor_tensor(t0, t0, m2, ALU.mult))
    P.op(v, lambda E: E.tensor_tensor(t0, t0, t1, ALU.add))
    P.op(v, lambda E: E.tensor_tensor(t1, mu, mu, ALU.mult))
    P.op(v, lambda E: E.tensor_tensor(t0, t0, t1, ALU.subtract))
    P.op(v, lambda E: E.tensor_scalar_add(t0, t0, eps))
    P.op(v, lambda E: E.reciprocal(t0, t0))
    tok = P.inc(v)
    P.op(s, lambda E: E.activation(t1, t0, AF.Sqrt), deps=[tok])
    tok_s = P.inc(s)
    P.op(v, lambda E: E.tensor_tensor(sf, s_, t1, ALU.mult), deps=[tok_s])
    P.op(v, lambda E: E.tensor_tensor(of, o_, mu, ALU.subtract))
    P.op(v, lambda E: E.tensor_tensor(of, of, t1, ALU.mult))
    P.inc(v)
    return (s_, o_), (sf, of)


def emit_ln_stats(P, x_tiles, Tc, pool, sq_tiles, consts):
    nd = len(x_tiles)
    g = len(sq_tiles)
    ps, pi, ptok = pool.acquire()
    ps_sx = ps[0:1, 0:Tc]
    ps_sq = ps[0:1, 512:512 + Tc]
    if ptok:
        P.wait("t", ptok)
    for i in range(nd):
        P.mm(ps_sx, consts["ones_col"], x_tiles[i], start=(i == 0), stop=(i == nd - 1))
    for base in range(0, nd, g):
        grp = range(base, min(base + g, nd))
        tk = P._sq_tok.get("last")
        if tk is not None:
            P.wait("s", tk)
        for i in grp:
            P.op("s", lambda E, o=sq_tiles[i - base], x=x_tiles[i]:
                 E.activation(o, x, AF.Square))
        stok = P.inc("s")
        P.wait("t", stok)
        for i in grp:
            P.mm(ps_sq, consts["ones_col"], sq_tiles[i - base],
                 start=(i == 0), stop=(i == nd - 1))
        P._sq_tok["last"] = P.inc("t")
    ttok = P.inc("t")
    return ps, ps_sx, ps_sq, pi, ttok


def bcast_rows(P, pool, rows, Tc, consts, deps=()):
    ps, pi, ptok = pool.acquire()
    if ptok:
        P.wait("t", ptok)
    P.waits("t", deps)
    outs = []
    for j, r in enumerate(rows):
        dst = ps[:, j * Tc:(j + 1) * Tc]
        P.mm(dst, consts["ones_row"], r, start=True, stop=True)
        outs.append(dst)
    tok = P.inc("t")
    return outs, pi, tok



def build_program(dbg_passes=None):
    nc = bass.Bass("TRN2", target_bir_lowering=False)
    want = dbg_passes

    def on(name):
        return want is None or name in want

    ext_scratch = want is not None

    def din(name, shape):
        return nc.dram_tensor(name, list(shape), F32, kind="ExternalInput").ap()

    x0 = din("x", [D, T])
    wq, wk, wv = din("wq", [D, D]), din("wk", [D, D]), din("wv", [D, D])
    wba = din("wba", [D, 16])
    ln1w = din("ln1wb", [D, 2])
    wo = din("wo", [D, D])
    w1 = din("w1", [D, 4096])
    b1 = din("b1", [4096, 1])
    w2 = din("w2", [4096, D])
    w2s = din("w2s", [1024, 4096])
    b2 = din("b2", [D, 1])
    ln2w = din("ln2wb", [D, 2])
    wfcT = din("wfcT", [D, 4096])
    flnw = din("flnwb", [4096, 2])
    wprT = din("wprT", [4096, D])
    pqT, pkT, pvT = din("pqT", [D, D]), din("pkT", [D, D]), din("pvT", [D, D])
    pbT = din("pbT", [D, 4])
    qcw, kcw, vcw = din("qcw", [D, 4]), din("kcw", [D, 4]), din("vcw", [D, 4])
    firs = din("firs", [D, 3])
    firl = din("firl", [D, 31])
    fdiag = din("fdiag", [128, 3 * 31 * 128])
    fgw1T = din("fgw1T", [1792, D])
    fgw1s = din("fgw1s", [1024, 1792])
    fgb1 = din("fgb1", [D, 1])
    fgw2T = din("fgw2T", [D, 4])
    ltemp = din("ltemp", [1, 1])
    onw = din("onw", [D, 1])
    oprT = din("oprT", [D, D])
    oprs = din("oprs", [1024, 1024])
    c_in = {}
    for nm, shape in [("ones_col", (128, 1)), ("ones_row", (1, 128)),
                      ("ones128", (128, 128)), ("negtriu", (128, 128)),
                      ("tri01i_x8", (128, 1024)), ("maskb_x8", (128, 1024)),
                      ("ident", (128, 128)), ("identx8", (128, 1024)),
                      ("ident2x8", (128, 1024)),
                      ("mask01s_x4", (128, 512)), ("mask01i_x4", (128, 512)),
                      ("identx4", (128, 512)), ("ident2x4", (128, 512)),
                      ("tri_incl", (128, 128)), ("rev_strict", (128, 128)),
                      ("fgb2row", (128, 4))]:
        c_in[nm] = din("c_" + nm, shape)

    producer = {"s_xa": "P1", "s_x1": "P15", "s_x2": "P2", "s_hs": "P3a",
                "s_v": "P3a", "s_delta": "P3a", "s_x3": "P3b",
                "s_wfq": "P0", "s_wpq": "P0"}
    consumers = {"s_xa": ["P15"], "s_x1": ["P2"], "s_x2": ["P3a", "P3b"],
                 "s_hs": ["P3b"], "s_v": ["P3b"], "s_delta": ["P3b"],
                 "s_x3": ["P4"], "s_wfq": ["P2", "P4"], "s_wpq": ["P2", "P4"]}

    def scr(name, shape, dtype=F32):
        if not ext_scratch:
            kind = "Internal"
        elif producer[name] == "P0" or on(producer[name]):
            kind = "ExternalOutput"
        elif any(on(c) for c in consumers[name]):
            kind = "ExternalInput"
        else:
            kind = "Internal"
        return nc.dram_tensor(name, list(shape), dtype, kind=kind).ap()

    s_xa = scr("s_xa", [D, T])
    s_x1 = scr("s_x1", [D, T])
    s_x2 = scr("s_x2", [D, T])
    s_hs = scr("s_hs", [D, T])
    s_v = scr("s_v", [D, T])
    s_delta = scr("s_delta", [D, T])
    s_x3 = scr("s_x3", [D, T])
    s_wfq = scr("s_wfq", [D, 4096], BF16)
    s_wpq = scr("s_wpq", [4096, D], BF16)
    out = nc.dram_tensor("out", [D, T], F32, kind="ExternalOutput").ap()

    P = Prog(nc)
    with ExitStack() as stack:
        P.open_sems(stack)
        consts_sb = {}

        cst = ExitStack()
        stack.enter_context(cst)
        k_itc = cst.enter_context(nc.sbuf_tensor("k_itc", [128, 1], F32)).ap()

        P.begin()
        with ExitStack() as ar:
            def sb(name, shape, dtype=F32):
                return ar.enter_context(nc.sbuf_tensor(name, list(shape), dtype)).ap()
            pool = PsumPool(P, ar, 4)
            K0 = load_consts(P, nc, ar, c_in, ["ones_col", "ones_row"], tag="p0")
            trow = sb("trow", [1, 4])
            P.dma("y", trow[0:1, 0:1], ltemp)
            tok = ("dy", P.cnt["dy"])
            P.op("s", lambda E: E.activation(trow[0:1, 1:2], trow[0:1, 0:1], AF.Exp),
                 deps=[tok])
            P.op("s", lambda E: E.drain())
            P.op("s", lambda E: E.activation(trow[0:1, 2:3], trow[0:1, 1:2], AF.Ln, bias=1.0))
            stok = P.inc("s")
            P.op("v", lambda E: E.tensor_scalar_add(trow[0:1, 3:4], trow[0:1, 2:3], 1e-4),
                 deps=[stok])
            P.op("v", lambda E: E.reciprocal(trow[0:1, 3:4], trow[0:1, 3:4]))
            vtok = P.inc("v")
            ps, pi, _ = pool.acquire()
            P.waits("t", [vtok])
            P.mm(ps[:, 0:1], K0["ones_row"], trow[0:1, 3:4], start=True, stop=True)
            ttok = P.inc("t")
            P.op("v", lambda E, ps=ps: E.tensor_copy(k_itc, ps[:, 0:1]), deps=[ttok])
            pool.release(pi, P.inc("v"))

            wbuf = sb("q_w", [128, 4096])
            absb = sb("q_abs", [128, 4096])
            sgnb = sb("q_sgn", [128, 4096])
            m01 = sb("q_m01", [128, 4096])
            qout = sb("q_out", [128, 4096], BF16)
            acc = sb("q_acc", [128, 40])
            grow = sb("q_grow", [1, 8])
            gcol = sb("q_gcol", [128, 1])

            for (win, wout, nchunk, ncols, nelem) in [
                    (wfcT, s_wfq, 8, 4096, D * 4096), (wprT, s_wpq, 32, D, D * 4096)]:
                wt = win.rearrange("(c p) n -> c p n", p=128)
                qt = wout.rearrange("(c p) n -> c p n", p=128)
                vtoks = []
                for i in range(nchunk):
                    dtok = P.dma("y", wbuf[:, 0:ncols], wt[i],
                                 deps=[P._qser] if P._qser else [])
                    P.op("v", lambda E, i=i, nco=ncols: E.reduce_sum(
                        acc[:, i:i + 1], wbuf[:, 0:nco], axis=mybir.AxisListType.X,
                        apply_absolute_value=True), deps=[dtok])
                    vtoks.append(P.inc("v"))
                    if i + 1 < nchunk:
                        pass
                    P._qser = vtoks[-1]
                    if i + 1 < nchunk:
                        P.waits("y", [vtoks[-1]])
                P.op("v", lambda E, nck=nchunk: E.reduce_sum(
                    acc[:, 36:37], acc[:, 0:nck], axis=mybir.AxisListType.X))
                vtok = P.inc("v")
                ps, pi, ptok = pool.acquire()
                if ptok:
                    P.wait("t", ptok)
                P.waits("t", [vtok])
                P.mm(ps[0:1, 0:1], acc[:, 36:37], K0["ones_col"], start=True, stop=True)
                ttok = P.inc("t")
                P.op("v", lambda E, ps=ps, ne=nelem: E.tensor_scalar(
                    grow[0:1, 0:1], ps[0:1, 0:1], 1.0 / ne, 1e-6, ALU.mult, ALU.add),
                    deps=[ttok])
                P.op("v", lambda E: E.tensor_scalar_mul(grow[0:1, 1:2], grow[0:1, 0:1], 0.5))
                vtok = P.inc("v")
                pool.release(pi, vtok)
                ps2, pi2, ptok2 = pool.acquire()
                if ptok2:
                    P.wait("t", ptok2)
                P.waits("t", [vtok])
                P.mm(ps2[:, 0:1], K0["ones_row"], grow[0:1, 1:2], start=True, stop=True)
                ttok = P.inc("t")
                P.op("v", lambda E, ps2=ps2: E.tensor_copy(gcol, ps2[:, 0:1]), deps=[ttok])
                gtok = P.inc("v")
                pool.release(pi2, gtok)
                last_store = getattr(P, "_qstore", None)
                for i in range(nchunk):
                    dtok = P.dma("y", wbuf[:, 0:ncols], wt[i], deps=[P._qser])
                    P.op("s", lambda E, nco=ncols: E.activation(
                        absb[:, 0:nco], wbuf[:, 0:nco], AF.Abs), deps=[dtok, gtok])
                    P.op("s", lambda E, nco=ncols: E.activation(
                        sgnb[:, 0:nco], wbuf[:, 0:nco], AF.Sign))
                    stok = P.inc("s")
                    P.op("v", lambda E, nco=ncols: E.tensor_scalar(
                        m01[:, 0:nco], absb[:, 0:nco], gcol, None, ALU.is_gt),
                        deps=[stok])
                    P.op("v", lambda E, nco=ncols: E.tensor_tensor(
                        qout[:, 0:nco], sgnb[:, 0:nco], m01[:, 0:nco], ALU.mult),
                        deps=[last_store] if last_store else [])
                    vtok = P.inc("v")
                    P._qser = vtok
                    last_store = P.dma("y", qt[i], qout[:, 0:ncols], deps=[vtok])
                    P._qstore = last_store
            P.finish_pass()

        if on("P1"):
            emit_p1(P, nc, x0, s_xa, dict(wq=wq, wk=wk, wv=wv, wba=wba, ln1=ln1w, wo=wo),
                    c_in)

        if on("P15"):
            emit_p15(P, nc, x0, s_xa, s_x1,
                     dict(w1=w1, b1=b1, w2=w2, w2s=w2s, b2=b2, ln2=ln2w), c_in)

        if on("P2"):
            emit_ffn(P, nc, s_x1, s_x2, s_wfq, s_wpq, flnw, c_in, tag="f2")

        if on("P3a"):
            emit_p3a(P, nc, s_x2, s_hs, s_v, s_delta,
                     dict(pqT=pqT, pkT=pkT, pvT=pvT, pbT=pbT, qcw=qcw, kcw=kcw, vcw=vcw),
                     c_in)

        if on("P3b"):
            emit_p3b(P, nc, s_x2, s_hs, s_v, s_delta, s_x3,
                     dict(firs=firs, firl=firl, fdiag=fdiag, fgw1T=fgw1T,
                          fgw1s=fgw1s, fgb1=fgb1, fgw2T=fgw2T, onw=onw,
                          oprT=oprT, oprs=oprs),
                     c_in, k_itc)

        if on("P4"):
            emit_ffn(P, nc, s_x3, out, s_wfq, s_wpq, flnw, c_in, tag="f4")

        P.begin()
        P.finish_pass()

    return nc



def emit_ffn(P, nc, xin, xout, s_wfq, s_wpq, flnw, c_in, tag):
    Tc = 256
    NCH = T // Tc if DBG["nch"] is None else DBG["nch"]
    ND = D // 128
    NH = 4096 // 128
    P.begin()
    with ExitStack() as ar:
        def sb(name, shape, dtype=F32):
            return ar.enter_context(nc.sbuf_tensor(tag + name, list(shape), dtype)).ap()
        pool = PsumPool(P, ar, 3, name=tag + "pp")
        ps_ded = ar.enter_context(nc.psum_tensor(tag + "st2", [128, 1024], F32)).ap()
        K = load_consts(P, nc, ar, c_in, ["ones_col", "ones_row"], tag=tag)
        wfq = sb("wfq", [128, 8 * 4096], BF16)
        wpq = sb("wpq", [128, 32 * D], BF16)
        flw = sb("flw", [128, 32 * 2])
        sq = sb("sq", [128, 4 * Tc])
        xres_b = sb("xres", [128, 2 * Tc])
        nf = sb("nf", [128, Tc])
        h = sb("h", [128, 32 * Tc])
        hh = sb("hh", [128, 32 * Tc], BF16)
        hl = sb("hl", [128, 32 * Tc], BF16)
        row = sb("row", [1, 6 * Tc])
        nh = hh[:, 0:8 * Tc]
        nl = hl[:, 0:8 * Tc]
        nbig = h[:, 8 * Tc:16 * Tc]
        xtb = h[:, 24 * Tc:32 * Tc]
        wfq_t = s_wfq.rearrange("(c p) n -> c p n", p=128)
        wpq_t = s_wpq.rearrange("(c p) n -> c p n", p=128)
        for i in range(8):
            P.dma("y", wfq[:, i * 4096:(i + 1) * 4096], wfq_t[i])
        for i in range(32):
            P.dma("y", wpq[:, i * D:(i + 1) * D], wpq_t[i])
        fl_t = flnw.rearrange("(c p) n -> c p n", p=128)
        for i in range(32):
            P.dma("y", flw[:, i * 2:(i + 1) * 2], fl_t[i])
        wtok = ("dy", P.cnt["dy"])
        xin_t = xin.rearrange("(c p) n -> c p n", p=128)
        xout_t = xout.rearrange("(c p) n -> c p n", p=128)
        x_tiles = [xtb[:, i * Tc:(i + 1) * Tc] for i in range(ND)]
        st_tiles = [sq[:, 2 * Tc:3 * Tc], sq[:, 3 * Tc:4 * Tc]]

        def stage1a(ci, xfree_deps):
            c0 = ci * Tc
            dtoks = []
            for i in range(ND):
                dtoks.append(P.dma("y", x_tiles[i], xin_t[i][:, c0:c0 + Tc],
                                   deps=xfree_deps))
            for i in range(ND):
                P.wait("s", dtoks[i])
                P.wait("t", dtoks[i])
            ps_st, ps_sx, ps_sq, pi_st, sttok = emit_ln_stats(
                P, x_tiles, Tc, pool, st_tiles, K)
            P.wait("v", sttok)
            s_o = ln_rows(P, ps_sx, ps_sq, row, D, 1e-5, extra_ln=False)
            vtok = P.inc("v")
            pool.release(pi_st, vtok)
            bc, pi_bc, bctok = bcast_rows(P, pool, [s_o[0], s_o[1]], Tc, K,
                                          deps=[vtok])
            P.wait("v", bctok)
            for i in range(ND):
                P.op("v", lambda E, xi=x_tiles[i], b0=bc[0],
                     o=nbig[:, i * Tc:(i + 1) * Tc]:
                     E.tensor_tensor(o, xi, b0, ALU.mult), dr=(i == 0))
            for i in range(ND):
                P.op("v", lambda E, b1=bc[1], o=nbig[:, i * Tc:(i + 1) * Tc]:
                     E.tensor_tensor(o, o, b1, ALU.add), dr=(i == 0))
            napp = P.inc("v")
            pool.release(pi_bc, napp)
            return {"napp": napp, "sttok": sttok}

        def stage1b(st, wdep):
            for i in range(ND):
                P.op("s", lambda E, o=nh[:, i * Tc:(i + 1) * Tc],
                     i_=nbig[:, i * Tc:(i + 1) * Tc]:
                     E.activation(o, i_, AF.Copy),
                     deps=([st["napp"]] + ([wdep] if wdep else []))
                     if i == 0 else (), dr=(i == 0))
            sctok = P.inc("s")
            hilo_toks = []
            for i in range(ND):
                P.op("v", lambda E, o=nl[:, i * Tc:(i + 1) * Tc],
                     i_=nbig[:, i * Tc:(i + 1) * Tc],
                     hi=nh[:, i * Tc:(i + 1) * Tc]: E.tensor_tensor(
                    o, i_, hi, ALU.subtract),
                    deps=[sctok] if i == 0 else (), dr=(i == 0))
                hilo_toks.append(P.inc("v"))
            st["hilo"] = hilo_toks

        prev_store = [None] * 8
        prev_inner_rows = None
        xres_tok = [None, None]
        cur = stage1a(0, [])
        stage1b(cur, None)
        for ci in range(NCH):
            c0 = ci * Tc
            P.waits("t", [wtok] + cur["hilo"])
            h_tiles = [h[:, i * Tc:(i + 1) * Tc] for i in range(NH)]
            ps_sx = ps_ded[0:1, 0:Tc]
            ps_sq = ps_ded[0:1, 512:512 + Tc]
            sq_tiles = [sq[:, j * Tc:(j + 1) * Tc] for j in range(2)]
            pend = []

            def flush_stat():
                o0, ht, sqt, g0, gs0 = pend.pop(0)
                if o0 == 0 and prev_inner_rows is not None:
                    P.wait("t", prev_inner_rows)
                P.wait("t", g0)
                P.mm(ps_sx, K["ones_col"], ht, start=(o0 == 0), stop=(o0 == NH - 1))
                P.wait("t", gs0)
                P.mm(ps_sq, K["ones_col"], sqt, start=(o0 == 0), stop=(o0 == NH - 1))
                P._sq_tok["last"] = P.inc("t")

            for oc in range(NH):
                ps, pi, ptok = pool.acquire()
                if ptok:
                    P.wait("t", ptok)
                psl = ps[:, 0:Tc]
                for kc in range(ND):
                    lh = wfq[:, kc * 4096 + oc * 128: kc * 4096 + oc * 128 + 128]
                    P.mm(psl, lh, nh[:, kc * Tc:(kc + 1) * Tc], start=(kc == 0), stop=False)
                    P.mm(psl, lh, nl[:, kc * Tc:(kc + 1) * Tc], start=False,
                         stop=(kc == ND - 1))
                ttok = P.inc("t")
                P.op("s", lambda E, oc=oc, psl=psl: E.activation(
                    h_tiles[oc], psl, AF.Gelu), deps=[ttok])
                gtok = P.inc("s")
                pool.release(pi, gtok)
                tk = P._sq_tok.get("last")
                P.op("g", lambda E, o=sq_tiles[oc % 2], x=h_tiles[oc]:
                     E.tensor_tensor(o, x, x, ALU.mult),
                     deps=[gtok] + ([tk] if tk is not None else []), dr=True)
                gstok = P.inc("g")
                pend.append((oc, h_tiles[oc], sq_tiles[oc % 2], gtok, gstok))
                if len(pend) == 2:
                    flush_stat()
            while pend:
                flush_stat()
            sttok = P._sq_tok["last"]
            P.wait("v", sttok)
            s_o = ln_rows(P, ps_sx, ps_sq, row, 4096, 1e-5, extra_ln=False)
            vtok = P.inc("v")
            prev_inner_rows = vtok
            bc, pi_bc, bctok = bcast_rows(P, pool, [s_o[0], s_o[1]], Tc, K, deps=[vtok])
            P.wait("v", bctok)
            for i in range(NH):
                P.op("v", lambda E, o=h_tiles[i], b0=bc[0]: E.tensor_tensor(
                    o, o, b0, ALU.mult), dr=(i == 0))
            for i in range(NH):
                P.op("v", lambda E, o=h_tiles[i], b1=bc[1]: E.tensor_tensor(
                    o, o, b1, ALU.add), dr=(i == 0))
            for i in range(NH):
                P.op("v", lambda E, o=h_tiles[i], w_=flw[:, i * 2:i * 2 + 1],
                     b_=flw[:, i * 2 + 1:i * 2 + 2]: E.tensor_scalar(
                    o, o, w_, b_, ALU.mult, ALU.add), dr=(i == 0))
            happ = P.inc("v")
            for i in range(NH):
                P.op("s", lambda E, o=hh[:, i * Tc:(i + 1) * Tc], i_=h_tiles[i]:
                     E.activation(o, i_, AF.Copy),
                     deps=[happ] if i == 0 else (), dr=(i == 0))
            sctok2 = P.inc("s")
            h2_toks = []
            for i in range(NH):
                P.op("v", lambda E, o=hl[:, i * Tc:(i + 1) * Tc], i_=h_tiles[i],
                     hi=hh[:, i * Tc:(i + 1) * Tc]: E.tensor_tensor(
                    o, i_, hi, ALU.subtract),
                    deps=[sctok2] if i == 0 else (), dr=(i == 0))
                h2_toks.append(P.inc("v"))
            pool.release(pi_bc, h2_toks[-1])
            P.waits("t", h2_toks)
            nxt = None
            if ci + 1 < NCH:
                nxt = stage1a(ci + 1, [cur["napp"], cur["sttok"],
                                       h2_toks[-1], sctok2])
            xres = [xres_b[:, 0:Tc], xres_b[:, Tc:2 * Tc]]
            for oc in range(ND):
                ps, pi, ptok = pool.acquire()
                if ptok:
                    P.wait("t", ptok)
                psl = ps[:, 0:Tc]
                rtok = P.dma("y", xres[oc % 2], xin_t[oc][:, c0:c0 + Tc],
                             deps=[xres_tok[oc % 2]] if xres_tok[oc % 2] else [])
                for kc in range(NH):
                    lh = wpq[:, kc * D + oc * 128: kc * D + oc * 128 + 128]
                    P.mm(psl, lh, hh[:, kc * Tc:(kc + 1) * Tc], start=(kc == 0), stop=False)
                    P.mm(psl, lh, hl[:, kc * Tc:(kc + 1) * Tc], start=False,
                         stop=(kc == NH - 1))
                ttok = P.inc("t")
                P.op("v", lambda E, oc=oc, psl=psl, xr=xres[oc % 2]: E.tensor_tensor(
                    nf, psl, xr, ALU.add), deps=[ttok, rtok])
                vtok = P.inc("v")
                xres_tok[oc % 2] = vtok
                prev_store[oc] = P.dma("y", xout_t[oc][:, c0:c0 + Tc], nf, deps=[vtok])
                P.wait("v", prev_store[oc])
                pool.release(pi, vtok)
            wp_tok = ttok
            if nxt is not None:
                stage1b(nxt, wp_tok)
                cur = nxt
        P.finish_pass()




def emit_p1(P, nc, x0, s_xa, W, c_in):
    Tc = 256
    NCH = (T // Tc) if DBG["nch"] is None else DBG["nch"]
    ND = D // 128
    NH = 8
    P.begin()
    with ExitStack() as ar:
        def sb(name, shape, dtype=F32):
            return ar.enter_context(nc.sbuf_tensor("a" + name, list(shape), dtype)).ap()
        pool = PsumPool(P, ar, 4, name="app")
        K = load_consts(P, nc, ar, c_in,
                        ["ones_col", "ones_row", "ones128", "negtriu", "tri01i_x8",
                         "maskb_x8", "ident", "identx8", "ident2x8",
                         "tri_incl", "rev_strict"], tag="a")
        identx8_bf = sb("idbf", [128, 1024], BF16)
        P.op("v", lambda E: E.tensor_copy(identx8_bf, K["identx8"]))
        P.inc("v")
        wba = sb("wba", [128, 8 * 16])
        ln1 = sb("ln1", [128, 8 * 2])
        for i in range(8):
            P.dma("y", wba[:, i * 16:(i + 1) * 16],
                  W["wba"].rearrange("(c p) n -> c p n", p=128)[i])
            P.dma("y", ln1[:, i * 2:(i + 1) * 2],
                  W["ln1"].rearrange("(c p) n -> c p n", p=128)[i])
        wpan = sb("wpan", [128, 2 * 1024])
        wopan = sb("wopan", [128, 2 * 1024])
        xt = sb("xt", [128, ND * Tc])
        sq = sb("sq", [128, 2 * Tc])
        row = sb("row", [1, 8 * Tc])
        tmp8 = sb("tmp8", [128, 8])
        n0 = sb("n0", [128, ND * Tc])
        n1 = sb("n1", [128, ND * Tc])
        q_t = sb("q_t", [128, 2 * 1024])
        k_t = sb("k_t", [128, 2 * 1024])
        v_t = sb("v_t", [128, 2 * 1024])
        q_f = sb("q_f", [128, 2 * 1024])
        k_f = sb("k_f", [128, 2 * 1024])
        sq2 = sb("sq2", [128, 1024])
        rcol = sb("rcol", [128, 8])
        rcol2 = sb("rcol2", [128, 8])
        bl = sb("bl", [128, 2 * 16])
        gcols = sb("gcols", [128, 4 * 8])
        rep = sb("rep", [128, 1024])
        dst = sb("dst", [128, 1024])
        dinc = sb("dinc", [128, 1024])
        mt = sb("mt", [128, 1024])
        at = sb("at", [128, 1024])
        atf = sb("atf", [128, 1024])
        at_bf = sb("at_bf", [128, 1024], BF16)
        xbuf = sb("xbuf", [128, 1024], BF16)
        xtbuf = sb("xtbuf", [128, 1024], BF16)
        qbuf = sb("qbuf", [128, 1024], BF16)
        x5f = sb("x5f", [128, 1024])
        xt5f = sb("xt5f", [128, 1024])
        q6f = sb("q6f", [128, 1024])
        x6f = x5f
        xt6f = xt5f
        xt7 = xt5f
        rwg = sb("rwg", [128, 1024])
        kpr = sb("kpr", [128, 1024])
        wfn = rep
        ub = dinc
        o1b = dst
        otm = mt
        s0 = [sb("s0a", [128, 1024]), sb("s0b", [128, 1024])]
        o_f = sb("o_f", [128, 8 * 256])
        xa_o = sb("xa_o", [128, Tc])
        P.op("g", lambda E: E.memset(s0[0], 0.0))
        memtok = P.inc("g")
        P._ppan = {}
        P._wopan = {}
        P._ofdone = None
        P._stok = memtok
        P._wo_ttok = None
        P._o1read = None

        x0_t = x0.rearrange("(c p) n -> c p n", p=128)
        xa_t = s_xa.rearrange("(c p) n -> c p n", p=128)
        wsrcs = {"q": W["wq"].rearrange("(c p) n -> c p n", p=128),
                 "k": W["wk"].rearrange("(c p) n -> c p n", p=128),
                 "v": W["wv"].rearrange("(c p) n -> c p n", p=128)}
        wo_t = W["wo"].rearrange("(c p) n -> c p n", p=128)
        wtok0 = ("dy", P.cnt["dy"])
        for e in ("t", "s", "v"):
            P.wait(e, wtok0)

        prev_xa_store = [None] * ND
        prev_vtok = None
        for ci in range(NCH):
            c0 = ci * Tc
            x_tiles = [xt[:, i * Tc:(i + 1) * Tc] for i in range(ND)]
            if prev_vtok is not None:
                for e in ("t", "v", "s"):
                    P.wait(e, prev_vtok)
                    P.wait(e, P._wo_ttok)
            dt = []
            for i in range(ND):
                dt.append(P.dma("y", x_tiles[i], x0_t[i][:, c0:c0 + Tc],
                                deps=[prev_vtok] if prev_vtok else []))
            for i in range(ND):
                P.wait("s", dt[i]); P.wait("t", dt[i]); P.wait("v", dt[i])
            sq_tiles = [sq[:, 0:Tc], sq[:, Tc:2 * Tc]]
            ps_st, ps_sx, ps_sq, pi_st, sttok = emit_ln_stats(
                P, x_tiles, Tc, pool, sq_tiles, K)
            P.wait("v", sttok)
            (s0r, o0r), (s1r, o1r) = ln_rows(P, ps_sx, ps_sq, row, D, 1e-5,
                                             extra_ln=True)
            vtok = P.inc("v")
            pool.release(pi_st, vtok)
            bc, pi_bc, bctok = bcast_rows(P, pool, [s0r, o0r, s1r, o1r], Tc, K,
                                          deps=[vtok])
            P.wait("v", bctok)
            for i in range(ND):
                P.op("v", lambda E, xi=x_tiles[i], b=bc[0],
                     o=n0[:, i * Tc:(i + 1) * Tc]: E.tensor_tensor(o, xi, b, ALU.mult),
                     dr=(i == 0))
            for i in range(ND):
                P.op("v", lambda E, o=n0[:, i * Tc:(i + 1) * Tc], b=bc[1]:
                     E.tensor_tensor(o, o, b, ALU.add), dr=(i == 0))
            for i in range(ND):
                P.op("v", lambda E, xi=x_tiles[i], b=bc[2],
                     o=n1[:, i * Tc:(i + 1) * Tc]: E.tensor_tensor(o, xi, b, ALU.mult),
                     dr=(i == 0))
            for i in range(ND):
                P.op("v", lambda E, o=n1[:, i * Tc:(i + 1) * Tc], b=bc[3]:
                     E.tensor_tensor(o, o, b, ALU.add), dr=(i == 0))
            ntoks = []
            for i in range(ND):
                P.op("v", lambda E, o=n1[:, i * Tc:(i + 1) * Tc],
                     w_=ln1[:, i * 2:i * 2 + 1], b_=ln1[:, i * 2 + 1:i * 2 + 2]:
                     E.tensor_scalar(o, o, w_, b_, ALU.mult, ALU.add), dr=(i == 0))
                ntoks.append(P.inc("v"))
            pool.release(pi_bc, ntoks[-1])
            P.waits("t", ntoks)
            for ti, nm in enumerate(["q", "k", "v"]):
                dstt = {"q": q_t, "k": k_t, "v": v_t}[nm]
                pr = [pool.acquire(), pool.acquire()]
                for (_, _, ptk) in pr:
                    if ptk:
                        P.wait("t", ptk)
                for kc in range(ND):
                    slot = wpan[:, (kc % 2) * 1024:(kc % 2) * 1024 + 1024]
                    ptok = P.dma("y", slot, wsrcs[nm][kc],
                                 deps=[P._ppan[kc % 2]] if P._ppan.get(kc % 2) else [])
                    P.wait("t", ptok)
                    for tg in range(2):
                        for hf in range(2):
                            P.mm(pr[tg][0][:, hf * 512:(hf + 1) * 512],
                                 n1[:, kc * Tc + tg * 128: kc * Tc + tg * 128 + 128],
                                 slot[:, hf * 512:(hf + 1) * 512],
                                 start=(kc == 0), stop=(kc == ND - 1))
                    P._ppan[kc % 2] = P.inc("t")
                ttok = P.inc("t")
                for tg in range(2):
                    P.op("s", lambda E, o=dstt[:, tg * 1024:(tg + 1) * 1024],
                         p=pr[tg][0]: E.activation(o, p, AF.Silu), deps=[ttok])
                    stok = P.inc("s")
                    pool.release(pr[tg][1], stok)
                P._silu = stok
            for tg in range(2):
                psb, pib, pbt = pool.acquire()
                if pbt:
                    P.wait("t", pbt)
                for kc in range(ND):
                    P.mm(psb[:, 0:16],
                         n1[:, kc * Tc + tg * 128: kc * Tc + tg * 128 + 128],
                         wba[:, kc * 16:(kc + 1) * 16],
                         start=(kc == 0), stop=(kc == ND - 1))
                ttok = P.inc("t")
                P.op("s", lambda E, o=bl[:, tg * 16:tg * 16 + 8], p=psb[0:128, 0:8]:
                     E.activation(o, p, AF.Sigmoid), deps=[ttok])
                P.op("s", lambda E, p=psb[0:128, 8:16]:
                     E.activation(tmp8, p, AF.Sigmoid))
                P.op("s", lambda E, o=bl[:, tg * 16 + 8:tg * 16 + 16]:
                     E.activation(o, tmp8, AF.Ln))
                stok = P.inc("s")
                pool.release(pib, stok)
            bltok = stok
            for (tt, ff) in [(q_t, q_f), (k_t, k_f)]:
                for tg in range(2):
                    tile = tt[:, tg * 1024:(tg + 1) * 1024]
                    P.wait("s", P._silu)
                    P.op("s", lambda E, i_=tile: E.activation(sq2, i_, AF.Square))
                    stok = P.inc("s")
                    P.op("v", lambda E: E.reduce_sum(
                        rcol, sq2.rearrange("p (h k) -> p h k", h=8),
                        axis=mybir.AxisListType.X), deps=[stok])
                    P.op("v", lambda E: E.tensor_scalar_add(rcol, rcol, 1e-6))
                    P.op("v", lambda E: E.reciprocal(rcol, rcol))
                    vtok = P.inc("v")
                    P.op("s", lambda E: E.activation(rcol2, rcol, AF.Sqrt),
                         deps=[vtok])
                    stok = P.inc("s")
                    P.op("v", lambda E, o=tile: E.tensor_tensor(
                        o, o, rcol2.unsqueeze(2).broadcast_to(
                            (128, 8, 128)), ALU.mult), deps=[stok])
                    ltok = P.inc("v")
                    pst, pit, ptt = pool.acquire()
                    if ptt:
                        P.wait("t", ptt)
                    P.wait("t", ltok)
                    for h in range(NH):
                        P.tr(pst[:, h * 128:(h + 1) * 128],
                             tile[:, h * 128:(h + 1) * 128], K["ident"])
                    ttok = P.inc("t")
                    P.op("v", lambda E, o=ff[:, tg * 1024:(tg + 1) * 1024], p=pst:
                         E.tensor_copy(o, p), deps=[ttok])
                    vtok = P.inc("v")
                    pool.release(pit, vtok)
            P.waits("t", [vtok, bltok])
            for tg in range(2):
                cc = ci * 2 + tg
                _gdn_chunk(P, nc, pool, K, identx8_bf, cc, tg,
                           q_t, k_t, v_t, q_f, k_f, bl, gcols, rep, dst, dinc,
                           mt, at, atf, at_bf, xbuf, xtbuf, qbuf, x5f, xt5f,
                           q6f, x6f, xt6f, xt7, rwg, kpr, wfn, ub, o1b, otm,
                           s0, o_f)
            wops = [pool.acquire() for _ in range(4)]
            for (_, _, ptk) in wops:
                if ptk:
                    P.wait("t", ptk)
            P.waits("t", [P._ofdone])

            def wo_region(oc):
                ps = wops[oc // 2][0]
                return ps[:, (oc % 2) * 512:(oc % 2) * 512 + 256]
            for kc in range(ND):
                slot = wopan[:, (kc % 2) * 1024:(kc % 2) * 1024 + 1024]
                wtok = P.dma("y", slot, wo_t[kc],
                             deps=[P._wopan[kc % 2]] if P._wopan.get(kc % 2) else [])
                P.wait("t", wtok)
                for oc in range(ND):
                    P.mm(wo_region(oc), slot[:, oc * 128:(oc + 1) * 128],
                         o_f[:, kc * 256:(kc + 1) * 256],
                         start=(kc == 0), stop=(kc == ND - 1))
                P._wopan[kc % 2] = P.inc("t")
            P._wo_ttok = P.inc("t")
            for oc in range(ND):
                deps = [P._wo_ttok]
                if prev_xa_store[oc]:
                    deps.append(prev_xa_store[oc])
                P.op("v", lambda E, p=wo_region(oc), nn=n0[:, oc * Tc:(oc + 1) * Tc]:
                     E.tensor_tensor(xa_o, p, nn, ALU.add), deps=deps)
                vtok = P.inc("v")
                prev_xa_store[oc] = P.dma("y", xa_t[oc][:, c0:c0 + Tc], xa_o,
                                          deps=[vtok])
                P.wait("v", prev_xa_store[oc])
            vtok = P.inc("v")
            for (_, pi, _) in wops:
                pool.release(pi, vtok)
            prev_vtok = vtok
        P.finish_pass()


def _gdn_chunk(P, nc, pool, K, identx8_bf, cc, tg,
               q_t, k_t, v_t, q_f, k_f, bl, gcols, rep, dst, dinc,
               mt, at, atf, at_bf, xbuf, xtbuf, qbuf, x5f, xt5f, q6f,
               x6f, xt6f, xt7, rwg, kpr, wfn, ub, o1b, otm, s0, o_f):
    NH = 8
    la = bl[:, tg * 16 + 8:tg * 16 + 16]
    beta = bl[:, tg * 16:tg * 16 + 8]
    ktT = k_t[:, tg * 1024:(tg + 1) * 1024]
    vtT = v_t[:, tg * 1024:(tg + 1) * 1024]
    qfT = q_f[:, tg * 1024:(tg + 1) * 1024]
    kfT = k_f[:, tg * 1024:(tg + 1) * 1024]
    Sin = s0[cc % 2]
    Sout = s0[(cc + 1) % 2]
    gcol = gcols[:, 0:8]
    ecol = gcols[:, 8:16]
    gC = gcols[:, 16:24]
    edb = gcols[:, 24:32]
    labc = la.unsqueeze(2).broadcast_to((128, 8, 128))

    if P._stok is not None:
        P.wait("t", P._stok)
        P.wait("v", P._stok)

    psc, pic, ptok = pool.acquire()
    if ptok:
        P.wait("t", ptok)
    P.mm(psc[:, 0:8], K["tri_incl"], la, start=True, stop=True)
    P.mm(psc[:, 512:520], K["rev_strict"], la, start=True, stop=True)
    ttok = P.inc("t")
    P.op("s", lambda E: E.activation(gcol, psc[0:128, 0:8], AF.Exp), deps=[ttok])
    P.op("s", lambda E: E.activation(ecol, psc[0:128, 512:520], AF.Exp))
    stok = P.inc("s")
    P.op("v", lambda E: E.tensor_tensor(gC, ecol, gcol, ALU.mult), deps=[stok])
    P.op("v", lambda E: E.tensor_tensor(edb, ecol, beta, ALU.mult))
    P.op("v", lambda E: E.tensor_tensor(
        rep, labc, K["tri01i_x8"].rearrange("p (h t) -> p h t", h=8), ALU.mult))
    gtok = P.inc("v")
    pool.release(pic, gtok)
    psd, pid, ptok = pool.acquire()
    if ptok:
        P.wait("t", ptok)
    P.waits("t", [gtok])
    for half in range(2):
        sl = slice(half * 512, (half + 1) * 512)
        P.mm(psd[:, sl], K["ones128"], rep[:, sl], start=True, stop=False)
        P.mm(psd[:, sl], K["negtriu"], labc[:, half * 4:(half + 1) * 4, :],
             start=False, stop=False)
        P.mm(psd[:, sl], K["ident"], K["maskb_x8"][:, sl], start=False, stop=True)
    ttok = P.inc("t")
    dstdeps = [ttok]
    if getattr(P, "_o1read", None) is not None:
        dstdeps.append(P._o1read)
    P.op("s", lambda E: E.activation(dst, psd, AF.Exp), deps=dstdeps)
    stok = P.inc("s")
    P.op("v", lambda E: E.tensor_tensor(dinc, dst, K["identx8"], ALU.add),
         deps=[stok])
    dtok = P.inc("v")
    pool.release(pid, dtok)
    pskk, pikk, ptok = pool.acquire()
    if ptok:
        P.wait("t", ptok)
    pskq, pikq, ptok2 = pool.acquire()
    if ptok2:
        P.wait("t", ptok2)
    for h in range(NH):
        hs = slice(h * 128, (h + 1) * 128)
        P.mm(pskk[:, hs], kfT[:, hs], kfT[:, hs], start=True, stop=True)
        P.mm(pskq[:, hs], kfT[:, hs], qfT[:, hs], start=True, stop=True)
    ttok = P.inc("t")
    P.wait("v", ttok)
    P.wait("v", dtok)
    for h in range(NH):
        hs = slice(h * 128, (h + 1) * 128)
        P.op("v", lambda E, hs=hs, h=h: E.scalar_tensor_tensor(
            mt[:, hs], pskk[:, hs], beta[:, h:h + 1], dst[:, hs],
            ALU.mult, ALU.mult), dr=(h == 0))
        P.op("v", lambda E, hs=hs, h=h: E.scalar_tensor_tensor(
            at[:, hs], pskq[:, hs], beta[:, h:h + 1], dinc[:, hs],
            ALU.mult, ALU.mult), dr=False)
    mtok = P.inc("v")
    pool.release(pikk, mtok)
    pool.release(pikq, mtok)
    P.op("v", lambda E: E.tensor_tensor(atf, mt, K["identx8"], ALU.add))
    P.op("v", lambda E: E.tensor_copy(at_bf, atf))
    atok = P.inc("v")
    P.waits("t", [atok])

    def batch_mm(psx, lhs, rhs):
        for h in range(NH):
            hs = slice(h * 128, (h + 1) * 128)
            P.mm(psx[:, hs], lhs[:, hs], rhs[:, hs], start=True, stop=True)
        return P.inc("t")

    def evac(dstb, psx, dep, stt=False, eng="v"):
        if stt:
            P.op(eng, lambda E: E.scalar_tensor_tensor(
                dstb, psx, -1.0, K["ident2x8"], ALU.mult, ALU.add), deps=[dep])
        elif eng == "s":
            P.op(eng, lambda E: E.activation(dstb, psx, AF.Copy), deps=[dep])
        else:
            P.op(eng, lambda E: E.tensor_copy(dstb, psx), deps=[dep])
        return P.inc(eng)

    def newton_iter(lhs_at, idq, xin, xtin, xout, xtout, qb, skip_x=False):
        psp, pip, pt = pool.acquire()
        if pt:
            P.wait("t", pt)
        t1 = batch_mm(psp, lhs_at, xin)
        qtok = evac(qb, psp, t1, stt=True)
        pool.release(pip, qtok)
        P.waits("t", [qtok])
        toks = []
        if not skip_x:
            psx, pix, pt = pool.acquire()
            if pt:
                P.wait("t", pt)
            t2 = batch_mm(psx, xtin, qb)
            vt = evac(xout, psx, t2, eng="s")
            pool.release(pix, vt)
            toks.append(vt)
        psxt, pixt, pt = pool.acquire()
        if pt:
            P.wait("t", pt)
        t3 = batch_mm(psxt, qb, xtin)
        vt2 = evac(xtout, psxt, t3)
        pool.release(pixt, vt2)
        toks.append(vt2)
        P.waits("t", toks)
        P._nt = toks

    psp, pip, pt = pool.acquire()
    if pt:
        P.wait("t", pt)
    t1 = batch_mm(psp, at_bf, identx8_bf)
    vtok = evac(xbuf, psp, t1, stt=True)
    pool.release(pip, vtok)
    P.waits("t", [vtok])
    psxt, pixt, pt = pool.acquire()
    if pt:
        P.wait("t", pt)
    t2 = batch_mm(psxt, xbuf, identx8_bf)
    vtok = evac(xtbuf, psxt, t2)
    pool.release(pixt, vtok)
    P.waits("t", [vtok])
    for _ in range(4):
        newton_iter(at_bf, identx8_bf, xbuf, xtbuf, xbuf, xtbuf, qbuf)
    P.op("v", lambda E: E.tensor_copy(x5f, xbuf), deps=P._nt)
    vtok = P.inc("v")
    pstr, pitr, pt = pool.acquire()
    if pt:
        P.wait("t", pt)
    P.waits("t", [vtok])
    for h in range(NH):
        hs = slice(h * 128, (h + 1) * 128)
        P.tr(pstr[:, hs], x5f[:, hs], K["ident"])
    t3 = P.inc("t")
    vtok = evac(xt5f, pstr, t3)
    pool.release(pitr, vtok)
    P.waits("t", [vtok])
    newton_iter(atf, None, x5f, xt5f, x6f, xt6f, q6f)
    newton_iter(atf, None, x6f, xt6f, None, xt7, q6f, skip_x=True)
    P.op("v", lambda E: E.tensor_tensor(
        rwg, ktT, gcol.unsqueeze(2).broadcast_to((128, 8, 128)),
        ALU.mult))
    P.op("v", lambda E: E.tensor_tensor(
        kpr, ktT, edb.unsqueeze(2).broadcast_to((128, 8, 128)),
        ALU.mult))
    sctok = P.inc("v")
    P.waits("t", [sctok])
    psw, piw, pt = pool.acquire()
    if pt:
        P.wait("t", pt)
    t4 = batch_mm(psw, rwg, xt7)
    P.op("v", lambda E, p=psw: E.tensor_scalar_mul(wfn, p, -1.0), deps=[t4])
    wtok = P.inc("v")
    pool.release(piw, wtok)
    P.waits("t", [wtok])
    psu, piu, pt = pool.acquire()
    if pt:
        P.wait("t", pt)
    for h in range(NH):
        hs = slice(h * 128, (h + 1) * 128)
        P.mm(psu[:, hs], xt7[:, hs], vtT[:, hs], start=True, stop=False)
        P.mm(psu[:, hs], wfn[:, hs], Sin[:, hs], start=False, stop=True)
    t5 = P.inc("t")
    utok = evac(ub, psu, t5)
    pool.release(piu, utok)
    P.waits("t", [utok])
    pso1, pio1, pt = pool.acquire()
    if pt:
        P.wait("t", pt)
    t6 = batch_mm(pso1, qfT, Sin)
    P.op("v", lambda E, p=pso1: E.tensor_tensor(
        o1b, p, gcol.unsqueeze(2).broadcast_to((128, 8, 128)),
        ALU.mult), deps=[t6])
    o1tok = P.inc("v")
    pool.release(pio1, o1tok)
    pso2, pio2, pt = pool.acquire()
    if pt:
        P.wait("t", pt)
    pss, pis, pt2 = pool.acquire()
    if pt2:
        P.wait("t", pt2)
    for h in range(NH):
        hs = slice(h * 128, (h + 1) * 128)
        P.mm(pso2[:, hs], at[:, hs], ub[:, hs], start=True, stop=True)
        P.mm(pss[:, hs], kpr[:, hs], ub[:, hs], start=True, stop=True)
    t7 = P.inc("t")
    P.op("v", lambda E, p=pso2: E.tensor_tensor(otm, p, o1b, ALU.add),
         deps=[t7, o1tok])
    otok = P.inc("v")
    P._o1read = otok
    pool.release(pio2, otok)
    for h in range(NH):
        hs = slice(h * 128, (h + 1) * 128)
        P.op("v", lambda E, hs=hs, h=h, p=pss: E.scalar_tensor_tensor(
            Sout[:, hs], Sin[:, hs], gC[:, h:h + 1], p[:, hs],
            ALU.mult, ALU.add), deps=[t7] if h == 0 else (), dr=(h == 0))
    stok_v = P.inc("v")
    pool.release(pis, stok_v)
    P._stok = stok_v
    psof, piof, pt = pool.acquire()
    if pt:
        P.wait("t", pt)
    P.waits("t", [otok])
    for h in range(NH):
        hs = slice(h * 128, (h + 1) * 128)
        P.tr(psof[:, hs], otm[:, hs], K["ident"])
    t8 = P.inc("t")
    deps = [t8]
    if P._wo_ttok is not None:
        deps.append(P._wo_ttok)
    P.op("v", lambda E: E.tensor_copy(
        o_f.rearrange("p (h t) -> p h t", h=8)[:, :, tg * 128:(tg + 1) * 128],
        psof.rearrange("p (h t) -> p h t", h=8)), deps=deps)
    oftok = P.inc("v")
    pool.release(piof, oftok)
    P._ofdone = oftok


def emit_p15(P, nc, x0, s_xa, s_x1, W, c_in):
    Tc = 512
    NCH = (T // Tc) if DBG["nch"] is None else DBG["nch"]
    ND = D // 128
    NH = 4096 // 128
    P.begin()
    with ExitStack() as ar:
        def sb(name, shape, dtype=F32):
            return ar.enter_context(nc.sbuf_tensor("f1" + name, list(shape), dtype)).ap()
        pool = PsumPool(P, ar, 4, name="f1pp")
        K = load_consts(P, nc, ar, c_in, ["ones_col", "ones_row"], tag="f1")
        ln2 = sb("ln2", [128, 8 * 2])
        b1c = sb("b1c", [128, 32])
        b2c = sb("b2c", [128, 8])
        for i in range(8):
            P.dma("y", ln2[:, i * 2:(i + 1) * 2],
                  W["ln2"].rearrange("(c p) n -> c p n", p=128)[i])
            P.dma("y", b2c[:, i:i + 1],
                  W["b2"].rearrange("(c p) n -> c p n", p=128)[i])
        for i in range(32):
            P.dma("y", b1c[:, i:i + 1],
                  W["b1"].rearrange("(c p) n -> c p n", p=128)[i])
        xat = sb("xat", [128, ND * Tc])
        x0r = sb("x0r", [128, 2 * Tc])
        xar = sb("xar", [128, 2 * Tc])
        n2 = sb("n2", [128, ND * Tc])
        h = sb("h", [128, NH * Tc])
        sq = sb("sq", [128, 2 * Tc])
        row = sb("row", [1, 8 * Tc])
        nf = sb("nf", [128, Tc])
        w1p = sb("w1p", [128, 2 * 4096])
        w2p = sb("w2p", [128, 2 * 4096])
        wtok0 = ("dy", P.cnt["dy"])
        for e in ("t", "v", "s"):
            P.wait(e, wtok0)
        xa_t = s_xa.rearrange("(c p) n -> c p n", p=128)
        x0_t = x0.rearrange("(c p) n -> c p n", p=128)
        x1_t = s_x1.rearrange("(c p) n -> c p n", p=128)
        w1_t = W["w1"].rearrange("(c p) n -> c p n", p=128)
        w2_t = W["w2"].rearrange("(c p) n -> c p n", p=128)
        w2s_t = W["w2s"].rearrange("(c p) n -> c p n", p=128)
        P._p15pan = {}
        prev_store = [None] * ND
        prev_vtok = None
        prev_ntoks = None
        prev_w1tok = None
        x0r_tok = [None, None]
        for ci in range(NCH):
            c0 = ci * Tc
            xa_tiles = [xat[:, i * Tc:(i + 1) * Tc] for i in range(ND)]
            dt = []
            for i in range(ND):
                pv = [prev_ntoks] if prev_ntoks else []
                dt.append(P.dma("y", xa_tiles[i], xa_t[i][:, c0:c0 + Tc], deps=pv))
            for tk in dt:
                P.wait("s", tk); P.wait("t", tk); P.wait("v", tk)
            sq_tiles = [sq[:, 0:Tc], sq[:, Tc:2 * Tc]]
            ps_st, ps_sx, ps_sq, pi_st, sttok = emit_ln_stats(
                P, xa_tiles, Tc, pool, sq_tiles, K)
            P.wait("v", sttok)
            s_o = ln_rows(P, ps_sx, ps_sq, row, D, 1e-5, extra_ln=False)
            vtok = P.inc("v")
            pool.release(pi_st, vtok)
            bc, pi_bc, bctok = bcast_rows(P, pool, [s_o[0], s_o[1]], Tc, K,
                                          deps=[vtok])
            P.wait("v", bctok)
            if prev_w1tok is not None:
                P.wait("v", prev_w1tok)
            for i in range(ND):
                P.op("v", lambda E, xi=xa_tiles[i], b=bc[0],
                     o=n2[:, i * Tc:(i + 1) * Tc]: E.tensor_tensor(o, xi, b, ALU.mult),
                     dr=(i == 0))
            for i in range(ND):
                P.op("v", lambda E, o=n2[:, i * Tc:(i + 1) * Tc], b=bc[1]:
                     E.tensor_tensor(o, o, b, ALU.add), dr=(i == 0))
            ntoks = []
            for i in range(ND):
                P.op("v", lambda E, o=n2[:, i * Tc:(i + 1) * Tc],
                     w_=ln2[:, i * 2:i * 2 + 1], b_=ln2[:, i * 2 + 1:i * 2 + 2]:
                     E.tensor_scalar(o, o, w_, b_, ALU.mult, ALU.add), dr=(i == 0))
                ntoks.append(P.inc("v"))
            pool.release(pi_bc, ntoks[-1])
            prev_ntoks = ntoks[-1]
            P.waits("t", ntoks)
            for og in range(NH // 4):
                slot = w1p[:, (og % 2) * 4096:(og % 2) * 4096 + 4096]
                ptoks = []
                for kc in range(ND):
                    dep = [P._p15pan[og % 2]] if P._p15pan.get(og % 2) else []
                    ptoks.append(P.dma("y", slot[:, kc * 512:(kc + 1) * 512],
                                       w1_t[kc][:, og * 512:(og + 1) * 512],
                                       deps=dep))
                P.wait("t", ptoks[-1])
                for j in range(4):
                    oc = og * 4 + j
                    ps, pi, ptk = pool.acquire()
                    if ptk:
                        P.wait("t", ptk)
                    psl = ps[:, 0:Tc]
                    for kc in range(ND):
                        P.mm(psl, slot[:, kc * 512 + j * 128: kc * 512 + j * 128 + 128],
                             n2[:, kc * Tc:(kc + 1) * Tc],
                             start=(kc == 0), stop=(kc == ND - 1))
                    mmt = P.inc("t")
                    if j == 3:
                        P._p15pan[og % 2] = mmt
                    P.op("s", lambda E, o=h[:, oc * Tc:(oc + 1) * Tc], p=psl,
                         b_=b1c[:, oc:oc + 1]: E.activation(o, p, AF.Gelu, bias=b_),
                         deps=[mmt])
                    gtok = P.inc("s")
                    pool.release(pi, gtok)
            prev_w1tok = P._p15pan[(NH // 4 - 1) % 2]
            P.wait("t", gtok)
            for oc in range(ND):
                slot = w2p[:, (oc % 2) * 4096:(oc % 2) * 4096 + 4096]
                dep = [P._p15pan[2 + oc % 2]] if P._p15pan.get(2 + oc % 2) else []
                ptok = P.dma("y", slot, w2s_t[oc], deps=dep)
                ps, pi, ptk = pool.acquire()
                if ptk:
                    P.wait("t", ptk)
                P.wait("t", ptok)
                psl = ps[:, 0:Tc]
                for kc in range(NH):
                    P.mm(psl, slot[:, kc * 128:(kc + 1) * 128],
                         h[:, kc * Tc:(kc + 1) * Tc],
                         start=(kc == 0), stop=(kc == NH - 1))
                mmt = P.inc("t")
                P._p15pan[2 + oc % 2] = mmt
                x0tok = P.dma("y", x0r[:, (oc % 2) * Tc:(oc % 2) * Tc + Tc],
                              x0_t[oc][:, c0:c0 + Tc],
                              deps=[x0r_tok[oc % 2]] if x0r_tok[oc % 2] else [])
                xatok = P.dma("y", xar[:, (oc % 2) * Tc:(oc % 2) * Tc + Tc],
                              xa_t[oc][:, c0:c0 + Tc],
                              deps=[x0r_tok[oc % 2]] if x0r_tok[oc % 2] else [])
                deps = [mmt, x0tok, xatok]
                if prev_store[oc]:
                    deps.append(prev_store[oc])
                P.op("v", lambda E, p=psl,
                     xa_=xar[:, (oc % 2) * Tc:(oc % 2) * Tc + Tc],
                     b_=b2c[:, oc:oc + 1]:
                     E.scalar_tensor_tensor(nf, p, b_, xa_, ALU.add, ALU.add),
                     deps=deps)
                P.op("v", lambda E, x0_=x0r[:, (oc % 2) * Tc:(oc % 2) * Tc + Tc]:
                     E.tensor_tensor(nf, nf, x0_, ALU.add))
                vtok = P.inc("v")
                x0r_tok[oc % 2] = vtok
                prev_store[oc] = P.dma("y", x1_t[oc][:, c0:c0 + Tc], nf,
                                       deps=[vtok])
                P.wait("v", prev_store[oc])
                pool.release(pi, vtok)
            prev_vtok = vtok
        P.finish_pass()


def emit_p3a(P, nc, s_x2, s_hs, s_v, s_delta, W, c_in):
    Tc = 256
    NCH = (T // Tc) if DBG["nch"] is None else DBG["nch"]
    ND = D // 128
    NH = 4
    P.begin()
    with ExitStack() as ar:
        def sb(name, shape, dtype=F32):
            return ar.enter_context(nc.sbuf_tensor("b" + name, list(shape), dtype)).ap()
        pool = PsumPool(P, ar, 4, name="bpp")
        K = load_consts(P, nc, ar, c_in,
                        ["ones_col", "ones_row", "ident", "mask01s_x4",
                         "mask01i_x4", "identx4", "ident2x4"], tag="b")
        identx4_bf = sb("id4bf", [128, 512], BF16)
        P.op("v", lambda E: E.tensor_copy(identx4_bf, K["identx4"]))
        P.inc("v")
        wq = sb("wq", [128, 8 * 1024])
        wk = sb("wk", [128, 8 * 1024])
        wv = sb("wv", [128, 8 * 1024])
        wb4 = sb("wb4", [128, 8 * 4])
        cw = sb("cw", [128, 3 * 8 * 4])
        for i in range(8):
            P.dma("y", wq[:, i * 1024:(i + 1) * 1024],
                  W["pqT"].rearrange("(c p) n -> c p n", p=128)[i])
            P.dma("y", wk[:, i * 1024:(i + 1) * 1024],
                  W["pkT"].rearrange("(c p) n -> c p n", p=128)[i])
            P.dma("y", wv[:, i * 1024:(i + 1) * 1024],
                  W["pvT"].rearrange("(c p) n -> c p n", p=128)[i])
            P.dma("y", wb4[:, i * 4:(i + 1) * 4],
                  W["pbT"].rearrange("(c p) n -> c p n", p=128)[i])
            for ti, wn in enumerate(["qcw", "kcw", "vcw"]):
                P.dma("y", cw[:, ti * 32 + i * 4: ti * 32 + (i + 1) * 4],
                      W[wn].rearrange("(c p) n -> c p n", p=128)[i])
        CB = 3 + Tc
        cbuf = sb("cb", [128, 8 * CB])
        tails = sb("tails", [128, 3 * 8 * 3])
        P.op("g", lambda E: E.memset(tails, 0.0))
        cbtok = P.inc("g")
        xt = sb("xt", [128, 8 * Tc])
        hs = sb("hs", [128, 8 * Tc])
        k_tm = hs[:, 0:1024]
        v_tm = hs[:, 1024:2048]
        sq = sb("sq", [128, 2 * Tc])
        sq2 = sb("sq2", [128, 1024])
        row = sb("row", [1, 8 * Tc])
        acc = sb("acc", [128, 2 * Tc])
        acc2 = sq
        q_fm = sb("q_fm", [128, 8 * Tc])
        k_fm = sb("k_fm", [128, 8 * Tc])
        v_fm = sb("v_fm", [128, 8 * Tc])
        bl = sb("bl", [128, 2 * 4])

        mt = sb("mt", [128, 512])
        at = sb("at", [128, 512])
        atf = sb("atf", [128, 512])
        at_bf = sb("at_bf", [128, 512], BF16)
        xbuf = sb("xbuf", [128, 512], BF16)
        xtbuf = sb("xtbuf", [128, 512], BF16)
        qbuf = sb("qbuf", [128, 512], BF16)
        x5f = sb("x5f", [128, 512])
        xt5f = sb("xt5f", [128, 512])
        q6f = sb("q6f", [128, 512])
        x6f, xt6f, xt7 = x5f, xt5f, xt5f
        wfn = sb("wfn", [128, 1024])
        kpr = sq2
        ub = xt[:, 0:1024]
        otm = xt[:, 1024:2048]
        s0 = [sb("s0a", [128, 2048]), sb("s0b", [128, 2048])]
        delta_fm = cbuf[:, 0:8 * Tc]
        P.op("g", lambda E: E.memset(s0[0], 0.0))
        memtok = P.inc("g")
        P._stok3 = memtok
        P._dstore = [None] * 8
        P._vstore = [None] * 8
        P._hstore = [None] * 8

        x2_t = s_x2.rearrange("(c p) n -> c p n", p=128)
        hs_t = s_hs.rearrange("(c p) n -> c p n", p=128)
        v_t_d = s_v.rearrange("(c p) n -> c p n", p=128)
        d_t = s_delta.rearrange("(c p) n -> c p n", p=128)
        wtok0 = ("dy", P.cnt["dy"])
        for e in ("t", "v", "s", "g"):
            P.wait(e, wtok0)
        P.wait("v", cbtok)
        P.wait("s", cbtok)

        prev_vtok = None
        for ci in range(NCH):
            c0 = ci * Tc
            x_tiles = [xt[:, i * Tc:(i + 1) * Tc] for i in range(ND)]
            hs_tiles = [hs[:, i * Tc:(i + 1) * Tc] for i in range(ND)]
            if prev_vtok is not None:
                spill = ("dy", P.cnt["dy"])
                for e in ("t", "v", "s", "g"):
                    P.wait(e, prev_vtok)
                    P.wait(e, spill)
            dt = []
            for i in range(ND):
                dt.append(P.dma("y", x_tiles[i], x2_t[i][:, c0:c0 + Tc],
                                deps=[prev_vtok] if prev_vtok else []))
            for tk in dt:
                P.wait("s", tk); P.wait("t", tk); P.wait("v", tk)
            sq_tiles = [sq[:, 0:Tc], sq[:, Tc:2 * Tc]]
            ps_st, ps_sx, ps_sq, pi_st, sttok = emit_ln_stats(
                P, x_tiles, Tc, pool, sq_tiles, K)
            P.wait("v", sttok)
            s_o = ln_rows(P, ps_sx, ps_sq, row, D, 1e-5, extra_ln=False)
            vtok = P.inc("v")
            pool.release(pi_st, vtok)
            bc, pi_bc, bctok = bcast_rows(P, pool, [s_o[0], s_o[1]], Tc, K,
                                          deps=[vtok])
            ntoks = []
            P.wait("v", bctok)
            for i in range(ND):
                P.op("v", lambda E, xi=x_tiles[i], b=bc[0], o=hs_tiles[i]:
                     E.tensor_tensor(o, xi, b, ALU.mult), dr=(i == 0))
            for i in range(ND):
                P.op("v", lambda E, o=hs_tiles[i], b=bc[1]:
                     E.tensor_tensor(o, o, b, ALU.add), dr=(i == 0))
                tk = P.inc("v")
                ntoks.append(tk)
                P._hstore[i] = P.dma("y", hs_t[i][:, c0:c0 + Tc], hs_tiles[i],
                                     deps=[tk])
            pool.release(pi_bc, ntoks[-1])
            P.waits("t", ntoks)
            P._hs_spill = ("dy", P.cnt["dy"])
            for ti, (wsb, dstf) in enumerate([(wq, q_fm), (wk, k_fm), (wv, v_fm)]):
                ttoks = []
                for oc in range(ND):
                    cb = cbuf[:, oc * CB:(oc + 1) * CB]
                    tl = tails[:, (ti * 8 + oc) * 3:(ti * 8 + oc) * 3 + 3]
                    dep = [P._convdone] if getattr(P, "_convdone", None) else []
                    P.op("v", lambda E, cb=cb, tl=tl: E.tensor_copy(cb[:, 0:3], tl),
                         deps=dep)
                    ttoks.append(P.inc("v"))
                for oc in range(ND):
                    cb = cbuf[:, oc * CB:(oc + 1) * CB]
                    ps, pi, ptk = pool.acquire()
                    if ptk:
                        P.wait("t", ptk)
                    psl = ps[:, 0:Tc]
                    for kc in range(ND):
                        P.mm(psl, wsb[:, kc * 1024 + oc * 128: kc * 1024 + oc * 128 + 128],
                             hs_tiles[kc], start=(kc == 0), stop=(kc == ND - 1))
                    mtk = P.inc("t")
                    P.op("s", lambda E, o=cb[:, 3:3 + Tc], p=psl:
                         E.activation(o, p, AF.Copy), deps=[mtk, ttoks[oc]])
                    stok = P.inc("s")
                    pool.release(pi, stok)
                    tl = tails[:, (ti * 8 + oc) * 3:(ti * 8 + oc) * 3 + 3]
                    P.op("v", lambda E, cb=cb, tl=tl: E.tensor_copy(
                        tl, cb[:, Tc:Tc + 3]), deps=[stok])
                    eng = "v" if (oc % 4 != 3) else "g"
                    a0 = acc[:, 0:Tc] if eng == "v" else acc2[:, 0:Tc]
                    a1 = acc[:, Tc:2 * Tc] if eng == "v" else acc2[:, Tc:2 * Tc]
                    wc0 = cw[:, ti * 32 + oc * 4: ti * 32 + oc * 4 + 1]
                    if eng == "v":
                        P.op(eng, lambda E, cb=cb, w_=wc0, o=a0: E.tensor_scalar(
                            o, cb[:, 0:Tc], w_, None, ALU.mult), deps=[stok])
                        for j in range(1, 4):
                            src_sl = (a0, a1) if j % 2 == 1 else (a1, a0)
                            P.op(eng, lambda E, cb=cb, j=j,
                                 w_=cw[:, ti * 32 + oc * 4 + j: ti * 32 + oc * 4 + j + 1],
                                 i0=src_sl[0], o=src_sl[1]: E.scalar_tensor_tensor(
                                 o, cb[:, j:j + Tc], w_, i0, ALU.mult, ALU.add))
                    else:
                        P.op(eng, lambda E, cb=cb, w_=wc0, o=a1: E.tensor_tensor(
                            o, cb[:, 0:Tc], w_.broadcast_to((128, Tc)), ALU.mult),
                            deps=[stok])
                        for j in range(1, 4):
                            P.op(eng, lambda E, cb=cb, j=j, o=a0,
                                 w_=cw[:, ti * 32 + oc * 4 + j: ti * 32 + oc * 4 + j + 1]:
                                 E.tensor_tensor(o, cb[:, j:j + Tc],
                                                 w_.broadcast_to((128, Tc)), ALU.mult))
                            P.op(eng, lambda E, o=a1, a=a0: E.tensor_tensor(
                                o, o, a, ALU.add))
                    etok = P.inc(eng)
                    P.op("s", lambda E, o=dstf[:, oc * Tc:(oc + 1) * Tc], i_=a1:
                         E.activation(o, i_, AF.Silu), deps=[etok])
                    P._convdone = P.inc("s")
            for tg in range(2):
                psb, pib, pbt = pool.acquire()
                if pbt:
                    P.wait("t", pbt)
                for kc in range(ND):
                    P.mm(psb[:, 0:4],
                         hs_tiles[kc][:, tg * 128:(tg + 1) * 128],
                         wb4[:, kc * 4:(kc + 1) * 4],
                         start=(kc == 0), stop=(kc == ND - 1))
                ttk = P.inc("t")
                P.op("s", lambda E, o=bl[:, tg * 4:(tg + 1) * 4], p=psb[0:128, 0:4]:
                     E.activation(o, p, AF.Sigmoid), deps=[ttk])
                btok = P.inc("s")
                pool.release(pib, btok)
            for (ff,) in [(q_fm,), (k_fm,)]:
                pss_, pis_, ptk = pool.acquire()
                if ptk:
                    P.wait("t", ptk)
                sqtoks = []
                for pc in range(ND):
                    P.op("s", lambda E, i_=ff[:, pc * Tc:(pc + 1) * Tc],
                         o=sq2[:, (pc % 4) * 256:(pc % 4) * 256 + 256]:
                         E.activation(o, i_, AF.Square),
                         deps=[P._convdone, btok] + (
                             [sqtoks[pc - 4]] if pc >= 4 else []))
                    stk = P.inc("s")
                    sqtoks.append(stk)
                    h = pc // 2
                    P.wait("t", stk)
                    P.mm(pss_[0:1, h * 256:(h + 1) * 256], K["ones_col"],
                         sq2[:, (pc % 4) * 256:(pc % 4) * 256 + 256],
                         start=(pc % 2 == 0), stop=(pc % 2 == 1))
                    sqtoks[pc] = P.inc("t")
                sstok = P.inc("t")
                P.op("v", lambda E, p=pss_[0:1, 0:1024]: E.tensor_scalar_add(
                    row[0:1, 0:1024], p, 1e-12), deps=[sstok])
                P.op("v", lambda E: E.reciprocal(row[0:1, 0:1024], row[0:1, 0:1024]))
                vtk = P.inc("v")
                pool.release(pis_, vtk)
                P.op("s", lambda E: E.activation(row[0:1, 1024:2048],
                                                 row[0:1, 0:1024], AF.Sqrt),
                     deps=[vtk])
                stk = P.inc("s")
                psb2, pib2, ptk2 = pool.acquire()
                if ptk2:
                    P.wait("t", ptk2)
                P.waits("t", [stk])
                for h in range(NH):
                    P.mm(psb2[:, h * 256:(h + 1) * 256], K["ones_row"],
                         row[0:1, 1024 + h * 256:1024 + (h + 1) * 256],
                         start=True, stop=True)
                btk = P.inc("t")
                sc_toks = []
                for pc in range(ND):
                    P.op("v", lambda E, o=ff[:, pc * Tc:(pc + 1) * Tc],
                         b=psb2[:, (pc // 2) * 256:(pc // 2) * 256 + 256]:
                         E.tensor_tensor(o, o, b, ALU.mult), deps=[btk])
                    sc_toks.append(P.inc("v"))
                pool.release(pib2, sc_toks[-1])
            l2tok = sc_toks[-1]
            P.waits("t", [l2tok])
            for tg in range(2):
                _delta_chunk(P, nc, pool, K, identx4_bf, ci * 2 + tg, tg, Tc,
                             q_fm, k_fm, v_fm, bl, k_tm, v_tm, mt, at, atf,
                             at_bf, xbuf, xtbuf, qbuf, x5f, xt5f, q6f,
                             x6f, xt6f, xt7, kpr, wfn, ub, otm, s0, delta_fm)
            for i in range(ND):
                dep = [P._convdone] + ([P._vstore[i]] if P._vstore[i] else [])
                P._vstore[i] = P.dma("y", v_t_d[i][:, c0:c0 + Tc],
                                     v_fm[:, i * Tc:(i + 1) * Tc], deps=dep[:1])
                dep = [P._ddone] + ([P._dstore[i]] if P._dstore[i] else [])
                P._dstore[i] = P.dma("y", d_t[i][:, c0:c0 + Tc],
                                     delta_fm[:, i * Tc:(i + 1) * Tc], deps=dep[:1])
            prev_vtok = P._ddone
        P.finish_pass()


def _delta_chunk(P, nc, pool, K, identx4_bf, cc, tg, Tc,
                 q_fm, k_fm, v_fm, bl, k_tm, v_tm, mt, at, atf,
                 at_bf, xbuf, xtbuf, qbuf, x5f, xt5f, q6f,
                 x6f, xt6f, xt7, kpr, wfn, ub, otm, s0, delta_fm):
    NH = 4
    beta = bl[:, tg * 4:(tg + 1) * 4]
    Sin = s0[cc % 2]
    Sout = s0[(cc + 1) % 2]

    if P._stok3 is not None:
        P.wait("t", P._stok3)
        P.wait("v", P._stok3)

    for (src, dst) in [(k_fm, k_tm), (v_fm, v_tm)]:
        pst, pit, ptk = pool.acquire()
        if ptk:
            P.wait("t", ptk)
        for pc in range(8):
            P.tr(pst[:, pc * 128:(pc + 1) * 128],
                 src[:, pc * Tc + tg * 128: pc * Tc + (tg + 1) * 128], K["ident"])
        ttk = P.inc("t")
        P.op("v", lambda E, o=dst, p=pst: E.tensor_copy(o, p),
             deps=[ttk, P._hs_spill])
        vtk = P.inc("v")
        pool.release(pit, vtk)
    P.waits("t", [vtk])
    pskk, pikk, ptk = pool.acquire()
    if ptk:
        P.wait("t", ptk)
    pskq, pikq, ptk2 = pool.acquire()
    if ptk2:
        P.wait("t", ptk2)
    for h in range(NH):
        hs_ = slice(h * 128, (h + 1) * 128)
        for half in range(2):
            pc = 2 * h + half
            lh = k_fm[:, pc * Tc + tg * 128: pc * Tc + (tg + 1) * 128]
            P.mm(pskk[:, hs_], lh, lh, start=(half == 0), stop=(half == 1))
            P.mm(pskq[:, hs_], lh,
                 q_fm[:, pc * Tc + tg * 128: pc * Tc + (tg + 1) * 128],
                 start=(half == 0), stop=(half == 1))
    ttok = P.inc("t")
    P.wait("v", ttok)
    for h in range(NH):
        hs_ = slice(h * 128, (h + 1) * 128)
        P.op("v", lambda E, hs_=hs_, h=h: E.scalar_tensor_tensor(
            mt[:, hs_], pskk[:, hs_], beta[:, h:h + 1], K["mask01s_x4"][:, hs_],
            ALU.mult, ALU.mult), dr=(h == 0))
        P.op("v", lambda E, hs_=hs_, h=h: E.scalar_tensor_tensor(
            at[:, hs_], pskq[:, hs_], beta[:, h:h + 1], K["mask01i_x4"][:, hs_],
            ALU.mult, ALU.mult), dr=False)
    mtok = P.inc("v")
    pool.release(pikk, mtok)
    pool.release(pikq, mtok)
    P.op("v", lambda E: E.tensor_tensor(atf, mt, K["identx4"], ALU.add))
    P.op("v", lambda E: E.tensor_copy(at_bf, atf))
    atok = P.inc("v")
    P.waits("t", [atok])

    def batch_mm(psx, lhs, rhs):
        for h in range(NH):
            hs_ = slice(h * 128, (h + 1) * 128)
            P.mm(psx[:, hs_], lhs[:, hs_], rhs[:, hs_], start=True, stop=True)
        return P.inc("t")

    def evac(dstb, psx, dep, stt=False, eng="v"):
        if stt:
            P.op(eng, lambda E: E.scalar_tensor_tensor(
                dstb, psx, -1.0, K["ident2x4"], ALU.mult, ALU.add), deps=[dep])
        elif eng == "s":
            P.op(eng, lambda E: E.activation(dstb, psx, AF.Copy), deps=[dep])
        else:
            P.op(eng, lambda E: E.tensor_copy(dstb, psx), deps=[dep])
        return P.inc(eng)

    def newton_iter(lhs_at, xin, xtin, xout, xtout, qb, skip_x=False):
        psp, pip, pt = pool.acquire()
        if pt:
            P.wait("t", pt)
        t1 = batch_mm(psp[:, 0:512], lhs_at, xin)
        qtok = evac(qb, psp[:, 0:512], t1, stt=True)
        pool.release(pip, qtok)
        P.waits("t", [qtok])
        toks = []
        if not skip_x:
            psx, pix, pt = pool.acquire()
            if pt:
                P.wait("t", pt)
            t2 = batch_mm(psx[:, 0:512], xtin, qb)
            vt = evac(xout, psx[:, 0:512], t2, eng="s")
            pool.release(pix, vt)
            toks.append(vt)
        psxt, pixt, pt = pool.acquire()
        if pt:
            P.wait("t", pt)
        t3 = batch_mm(psxt[:, 0:512], qb, xtin)
        vt2 = evac(xtout, psxt[:, 0:512], t3)
        pool.release(pixt, vt2)
        toks.append(vt2)
        P.waits("t", toks)
        P._nt = toks

    psp, pip, pt = pool.acquire()
    if pt:
        P.wait("t", pt)
    t1 = batch_mm(psp[:, 0:512], at_bf, identx4_bf)
    vtok = evac(xbuf, psp[:, 0:512], t1, stt=True)
    pool.release(pip, vtok)
    P.waits("t", [vtok])
    psxt, pixt, pt = pool.acquire()
    if pt:
        P.wait("t", pt)
    t2 = batch_mm(psxt[:, 0:512], xbuf, identx4_bf)
    vtok = evac(xtbuf, psxt[:, 0:512], t2)
    pool.release(pixt, vtok)
    P.waits("t", [vtok])
    for _ in range(4):
        newton_iter(at_bf, xbuf, xtbuf, xbuf, xtbuf, qbuf)
    P.op("v", lambda E: E.tensor_copy(x5f, xbuf), deps=P._nt)
    vtok = P.inc("v")
    pstr, pitr, pt = pool.acquire()
    if pt:
        P.wait("t", pt)
    P.waits("t", [vtok])
    for h in range(NH):
        hs_ = slice(h * 128, (h + 1) * 128)
        P.tr(pstr[:, hs_], x5f[:, hs_], K["ident"])
    t3 = P.inc("t")
    vtok = evac(xt5f, pstr[:, 0:512], t3)
    pool.release(pitr, vtok)
    P.waits("t", [vtok])
    newton_iter(atf, x5f, xt5f, x6f, xt6f, q6f)
    newton_iter(atf, x6f, xt6f, None, xt7, q6f, skip_x=True)
    P.op("v", lambda E: E.tensor_tensor(
        kpr, k_tm, beta.unsqueeze(2).broadcast_to((128, 4, 256)), ALU.mult))
    sctok = P.inc("v")
    P.waits("t", [sctok])
    psw, piw, pt = pool.acquire()
    if pt:
        P.wait("t", pt)
    for h in range(NH):
        for half in range(2):
            P.mm(psw[:, h * 256 + half * 128: h * 256 + (half + 1) * 128],
                 k_tm[:, h * 256 + half * 128: h * 256 + (half + 1) * 128],
                 xt7[:, h * 128:(h + 1) * 128], start=True, stop=True)
    t4 = P.inc("t")
    P.op("v", lambda E, p=psw: E.tensor_scalar_mul(wfn, p, -1.0), deps=[t4])
    wtok = P.inc("v")
    pool.release(piw, wtok)
    P.waits("t", [wtok])
    psu, piu, pt = pool.acquire()
    if pt:
        P.wait("t", pt)
    for h in range(NH):
        hv = slice(h * 256, (h + 1) * 256)
        P.mm(psu[:, hv], xt7[:, h * 128:(h + 1) * 128], v_tm[:, hv],
             start=True, stop=False)
        P.mm(psu[:, hv], wfn[:, h * 256:h * 256 + 128],
             Sin[:, h * 512:h * 512 + 256], start=False, stop=False)
        P.mm(psu[:, hv], wfn[:, h * 256 + 128:h * 256 + 256],
             Sin[:, h * 512 + 256:h * 512 + 512], start=False, stop=True)
    t5 = P.inc("t")
    utok = evac(ub, psu, t5)
    pool.release(piu, utok)
    P.waits("t", [utok])
    pso, pio, pt = pool.acquire()
    if pt:
        P.wait("t", pt)
    for h in range(NH):
        hv = slice(h * 256, (h + 1) * 256)
        for half in range(2):
            pc = 2 * h + half
            P.mm(pso[:, hv],
                 q_fm[:, pc * Tc + tg * 128: pc * Tc + (tg + 1) * 128],
                 Sin[:, h * 512 + half * 256: h * 512 + (half + 1) * 256],
                 start=(half == 0), stop=False)
        P.mm(pso[:, hv], at[:, h * 128:(h + 1) * 128], ub[:, hv],
             start=False, stop=True)
    t6 = P.inc("t")
    otok = evac(otm, pso, t6)
    pool.release(pio, otok)
    pss1, pis1, pt = pool.acquire()
    if pt:
        P.wait("t", pt)
    pss2, pis2, pt2 = pool.acquire()
    if pt2:
        P.wait("t", pt2)
    for h in range(NH):
        hv = slice(h * 256, (h + 1) * 256)
        psx = pss1 if h < 2 else pss2
        base = (h % 2) * 512
        for half in range(2):
            P.mm(psx[:, base + half * 256: base + (half + 1) * 256],
                 kpr[:, h * 256 + half * 128: h * 256 + (half + 1) * 128],
                 ub[:, hv], start=True, stop=True)
    t7 = P.inc("t")
    P.op("v", lambda E, p=pss1: E.tensor_tensor(
        Sout[:, 0:1024], Sin[:, 0:1024], p, ALU.add), deps=[t7])
    P.op("v", lambda E, p=pss2: E.tensor_tensor(
        Sout[:, 1024:2048], Sin[:, 1024:2048], p, ALU.add))
    stok_v = P.inc("v")
    pool.release(pis1, stok_v)
    pool.release(pis2, stok_v)
    P._stok3 = stok_v
    psof, piof, pt = pool.acquire()
    if pt:
        P.wait("t", pt)
    P.waits("t", [otok])
    for pc in range(8):
        P.tr(psof[:, pc * 128:(pc + 1) * 128], otm[:, pc * 128:(pc + 1) * 128],
             K["ident"])
    t8 = P.inc("t")
    deps = [t8]
    if getattr(P, "_ddone", None) is not None and tg == 0:
        pass
    P.op("v", lambda E: E.tensor_copy(
        delta_fm.rearrange("p (c t) -> p c t", c=8)[:, :, tg * 128:(tg + 1) * 128],
        psof.rearrange("p (c t) -> p c t", c=8)), deps=deps)
    dftok = P.inc("v")
    pool.release(piof, dftok)
    P._ddone = dftok


def emit_p3b(P, nc, s_x2, s_hs, s_v, s_delta, s_x3, W, c_in, k_itc):
    Tc = 512
    NCH = (T // Tc) if DBG["nch"] is None else DBG["nch"]
    ND = D // 128
    PAD = 30
    CB = PAD + Tc
    P.begin()
    with ExitStack() as ar:
        def sb(name, shape, dtype=F32):
            return ar.enter_context(nc.sbuf_tensor("c" + name, list(shape), dtype)).ap()
        pool = PsumPool(P, ar, 4, name="cpp")
        K = load_consts(P, nc, ar, c_in, ["ones_col", "ones_row", "ident",
                                          "fgb2row"], tag="c")
        fsw = sb("fsw", [128, 8 * 3])
        flw_ = sb("flw", [128, 8 * 31])
        fgb1c = sb("fgb1c", [128, 8])
        fw2 = sb("fw2", [128, 8 * 4])
        onwc = sb("onwc", [128, 8])
        for i in range(8):
            P.dma("y", fsw[:, i * 3:(i + 1) * 3],
                  W["firs"].rearrange("(c p) n -> c p n", p=128)[i])
            P.dma("y", flw_[:, i * 31:(i + 1) * 31],
                  W["firl"].rearrange("(c p) n -> c p n", p=128)[i])
            P.dma("y", fgb1c[:, i:i + 1],
                  W["fgb1"].rearrange("(c p) n -> c p n", p=128)[i])
            P.dma("y", fw2[:, i * 4:(i + 1) * 4],
                  W["fgw2T"].rearrange("(c p) n -> c p n", p=128)[i])
            P.dma("y", onwc[:, i:i + 1],
                  W["onw"].rearrange("(c p) n -> c p n", p=128)[i])
        cbuf = sb("cb", [128, 8 * CB])
        tails = sb("tails", [128, 8 * PAD])
        P.op("g", lambda E: E.memset(tails, 0.0))
        tailtok = P.inc("g")
        delta = sb("delta", [128, 8 * Tc])
        hsb = sb("hs", [128, 8 * Tc])
        short = sb("short", [128, 8 * Tc])
        longb = sb("long", [128, 8 * Tc])
        ghid = sb("ghid", [128, 8 * Tc])
        accv = sb("accv", [128, 2 * Tc])
        accg = sb("accg", [128, 2 * Tc])
        gscr = sb("gscr", [128, 2 * Tc])
        fdg = sb("fdg", [128, 3 * 31 * 128])
        P.dma("y", fdg, W["fdiag"])
        xres = sb("xres", [128, 2 * Tc])
        nf = sb("nf", [128, Tc])
        lgt = sb("lgt", [128, 16])
        wgt = sb("wgt", [128, 16])
        scol = sb("scol", [128, 2])
        wrow = sb("wrow", [1, 4 * Tc])
        rrow = sb("rrow", [1, 4 * Tc])
        fgp = sb("fgp", [128, 2 * 14 * 128])
        opp = sb("opp", [128, 2 * 1024])
        sq2 = ghid

        x2_t = s_x2.rearrange("(c p) n -> c p n", p=128)
        hs_t = s_hs.rearrange("(c p) n -> c p n", p=128)
        v_t = s_v.rearrange("(c p) n -> c p n", p=128)
        d_t = s_delta.rearrange("(c p) n -> c p n", p=128)
        x3_t = s_x3.rearrange("(c p) n -> c p n", p=128)
        fg_t = W["fgw1T"].rearrange("(c p) n -> c p n", p=128)
        fgs_t = W["fgw1s"].rearrange("(c p) n -> c p n", p=128)
        ops_t = W["oprs"].rearrange("(c p) n -> c p n", p=128)
        op_t = W["oprT"].rearrange("(c p) n -> c p n", p=128)
        wtok0 = ("dy", P.cnt["dy"])
        for e in ("t", "v", "s", "g"):
            P.wait(e, wtok0)
            P.wait(e, tailtok)
        P._fgpan = {}
        P._oppan = {}
        prev_vtok = None
        prev_store = [None] * ND
        for ci in range(NCH):
            c0 = ci * Tc
            if prev_vtok is not None:
                spill = ("dy", P.cnt["dy"])
                for e in ("t", "v", "s", "g"):
                    P.wait(e, prev_vtok)
                    P.wait(e, spill)
            pdep = [prev_vtok] if prev_vtok else []
            dts = []
            for i in range(ND):
                cb = cbuf[:, i * CB:(i + 1) * CB]
                tl = tails[:, i * PAD:(i + 1) * PAD]
                P.op("v", lambda E, cb=cb, tl=tl: E.tensor_copy(cb[:, 0:PAD], tl),
                     deps=pdep)
                vtk = P.inc("v")
                dts.append(P.dma("y", cb[:, PAD:PAD + Tc], v_t[i][:, c0:c0 + Tc],
                                 deps=[vtk]))
                dts.append(P.dma("y", delta[:, i * Tc:(i + 1) * Tc],
                                 d_t[i][:, c0:c0 + Tc], deps=pdep))
                dts.append(P.dma("y", hsb[:, i * Tc:(i + 1) * Tc],
                                 hs_t[i][:, c0:c0 + Tc], deps=pdep))
            ldtok = ("dy", P.cnt["dy"])
            for e in ("t", "v", "s", "g"):
                P.wait(e, ldtok)
            cbs = [cbuf[:, i * CB:(i + 1) * CB] for i in range(ND)]
            for i in range(ND):
                P.op("v", lambda E, cb=cbs[i], tl=tails[:, i * PAD:(i + 1) * PAD]:
                     E.tensor_copy(tl, cb[:, Tc:Tc + PAD]), dr=(i == 0))
            NPE = 3
            ps_a, pi_a, ptk = pool.acquire()
            if ptk:
                P.wait("t", ptk)
            ps_b, pi_b, ptk2 = pool.acquire()
            if ptk2:
                P.wait("t", ptk2)
            pe_toks = []
            for b in range(NPE):
                psl = (ps_a if b < 2 else ps_b)[:, (b % 2) * 512:(b % 2) * 512 + 512]
                for j in range(31):
                    P.mm(psl, fdg[:, (b * 31 + j) * 128:(b * 31 + j + 1) * 128],
                         cbs[b][:, j:j + Tc], start=(j == 0), stop=(j == 30))
                pe_toks.append(P.inc("t"))
                P.op("s", lambda E, o=longb[:, b * Tc:(b + 1) * Tc], p=psl:
                     E.activation(o, p, AF.Copy), deps=[pe_toks[-1]])
            evac_tok = P.inc("s")
            pool.release(pi_a, evac_tok)
            pool.release(pi_b, evac_tok)
            for i in range(NPE, ND):
                P.op("v", lambda E, cb=cbs[i], w_=flw_[:, i * 31:i * 31 + 1],
                     o=longb[:, i * Tc:(i + 1) * Tc]:
                     E.tensor_scalar(o, cb[:, 0:Tc], w_, None, ALU.mult),
                     dr=(i == NPE))
            for j in range(1, 31):
                for i in range(NPE, ND):
                    P.op("v", lambda E, cb=cbs[i], j=j,
                         o=longb[:, i * Tc:(i + 1) * Tc],
                         w_=flw_[:, i * 31 + j:i * 31 + j + 1]:
                         E.scalar_tensor_tensor(o, cb[:, j:j + Tc], w_, o,
                                                ALU.mult, ALU.add), dr=(i == NPE))
            dve_tok = P.inc("v")
            for i in range(ND):
                P.op("g", lambda E, cb=cbs[i], o=short[:, i * Tc:(i + 1) * Tc],
                     w_=fsw[:, i * 3:i * 3 + 1]:
                     E.tensor_tensor(o, cb[:, PAD - 2:PAD - 2 + Tc],
                                     w_.broadcast_to((128, Tc)), ALU.mult),
                     dr=True)
                P.op("g", lambda E, cb=cbs[i], o=gscr[:, 0:Tc],
                     w_=fsw[:, i * 3 + 1:i * 3 + 2]:
                     E.tensor_tensor(o, cb[:, PAD - 1:PAD - 1 + Tc],
                                     w_.broadcast_to((128, Tc)), ALU.mult),
                     dr=False)
                P.op("g", lambda E, cb=cbs[i], o=gscr[:, Tc:2 * Tc],
                     w_=fsw[:, i * 3 + 2:i * 3 + 3]:
                     E.tensor_tensor(o, cb[:, PAD:PAD + Tc],
                                     w_.broadcast_to((128, Tc)), ALU.mult),
                     dr=False)
                P.op("g", lambda E, o=short[:, i * Tc:(i + 1) * Tc], a=gscr[:, 0:Tc]:
                     E.tensor_tensor(o, o, a, ALU.add))
                P.op("g", lambda E, o=short[:, i * Tc:(i + 1) * Tc],
                     a=gscr[:, Tc:2 * Tc]:
                     E.tensor_tensor(o, o, a, ALU.add))
            convtok = dve_tok
            mean_dst = {"sm": [accv[:, 0:Tc], accv[:, Tc:2 * Tc]],
                        "lm": [accg[:, 0:Tc], accg[:, Tc:2 * Tc]],
                        "dm": [xres[:, 0:Tc], xres[:, Tc:2 * Tc]]}
            srcs = {"sm": short, "lm": longb, "dm": delta}
            mtriples = []
            for nm in ["sm", "lm", "dm"]:
                ssrc = srcs[nm]
                for half in range(2):
                    tiles = [ssrc[:, (2 * h + half) * Tc:(2 * h + half + 1) * Tc]
                             for h in range(4)]
                    mtriples.append((mean_dst[nm][half], tiles))
            for k, (o, tl) in enumerate(mtriples):
                P.op("g", lambda E, o=o, a=tl[0], b=tl[1]:
                     E.tensor_tensor(o, a, b, ALU.add),
                     deps=[convtok, evac_tok] if k == 0 else (), dr=(k == 0))
            for ph in (2, 3):
                for k, (o, tl) in enumerate(mtriples):
                    P.op("g", lambda E, o=o, a=tl[ph]:
                         E.tensor_tensor(o, o, a, ALU.add), dr=(k == 0))
            for k, (o, tl) in enumerate(mtriples):
                P.op("g", lambda E, o=o: E.tensor_scalar_mul(o, o, 0.25),
                     dr=(k == 0))
            meantok = P.inc("g")
            P.wait("v", meantok)
            P.waits("t", [meantok])
            gate_in = ([hsb[:, i * Tc:(i + 1) * Tc] for i in range(8)]
                       + mean_dst["sm"] + mean_dst["lm"] + mean_dst["dm"])
            for oc in range(ND):
                slot = fgp[:, (oc % 2) * 14 * 128:(oc % 2) * 14 * 128 + 14 * 128]
                dep = [P._fgpan[oc % 2]] if P._fgpan.get(oc % 2) else []
                ptok = P.dma("y", slot, fgs_t[oc], deps=dep)
                ps, pi, ptk = pool.acquire()
                if ptk:
                    P.wait("t", ptk)
                P.wait("t", ptok)
                psl = ps[:, 0:Tc]
                for kc in range(14):
                    P.mm(psl, slot[:, kc * 128:(kc + 1) * 128], gate_in[kc],
                         start=(kc == 0), stop=(kc == 13))
                P._fgpan[oc % 2] = P.inc("t")
                P.op("s", lambda E, o=ghid[:, oc * Tc:(oc + 1) * Tc], p=psl,
                     b_=fgb1c[:, oc:oc + 1]: E.activation(o, p, AF.Gelu, bias=b_),
                     deps=[P._fgpan[oc % 2]])
                gtok = P.inc("s")
                pool.release(pi, gtok)
            P.wait("t", gtok)
            psl4, pil4, ptk = pool.acquire()
            if ptk:
                P.wait("t", ptk)
            pswr, piwr, ptk2 = pool.acquire()
            if ptk2:
                P.wait("t", ptk2)
            pswr2, piwr2, ptk3 = pool.acquire()
            if ptk3:
                P.wait("t", ptk3)
            for tg in range(4):
                reg = psl4[:, tg * 256:tg * 256 + 4]
                for kc in range(ND):
                    P.mm(reg, ghid[:, kc * Tc + tg * 128: kc * Tc + (tg + 1) * 128],
                         fw2[:, kc * 4:(kc + 1) * 4],
                         start=(kc == 0), stop=(kc == ND - 1))
                ltk = P.inc("t")
                lg = lgt[:, tg * 4:(tg + 1) * 4]
                wg = wgt[:, tg * 4:(tg + 1) * 4]
                P.op("v", lambda E, lg=lg, p=reg: E.tensor_tensor(
                    lg, p, K["fgb2row"], ALU.add), deps=[ltk])
                P.op("v", lambda E, lg=lg: E.tensor_scalar(
                    lg, lg, k_itc, None, ALU.mult))
                P.op("v", lambda E, lg=lg: E.reduce_max(
                    scol[:, 0:1], lg, axis=mybir.AxisListType.X))
                P.op("v", lambda E, lg=lg: E.tensor_scalar(
                    lg, lg, scol[:, 0:1], None, ALU.subtract))
                vtk = P.inc("v")
                P.op("s", lambda E, lg=lg: E.activation(lg, lg, AF.Exp),
                     deps=[vtk])
                stk = P.inc("s")
                P.op("v", lambda E, lg=lg: E.reduce_sum(
                    scol[:, 0:1], lg, axis=mybir.AxisListType.X), deps=[stk])
                P.op("v", lambda E: E.reciprocal(scol[:, 0:1], scol[:, 0:1]))
                P.op("v", lambda E, lg=lg, wg=wg: E.tensor_scalar(
                    wg, lg, scol[:, 0:1], None, ALU.mult))
                P.op("v", lambda E, wg=wg: E.tensor_scalar(
                    scol[:, 1:2], wg[:, 3:4], 0.05, None, ALU.mult))
                P.op("v", lambda E, wg=wg: E.tensor_tensor(
                    wg[:, 0:2], wg[:, 0:2],
                    scol[:, 1:2].broadcast_to((128, 2)), ALU.max))
                P.op("v", lambda E, wg=wg: E.reduce_sum(
                    scol[:, 0:1], wg, axis=mybir.AxisListType.X))
                P.op("v", lambda E: E.reciprocal(scol[:, 0:1], scol[:, 0:1]))
                P.op("v", lambda E, wg=wg: E.tensor_scalar(
                    wg, wg, scol[:, 0:1], None, ALU.mult))
                wtk = P.inc("v")
                P.waits("t", [wtk])
                for kk in range(4):
                    psdst = pswr if kk < 2 else pswr2
                    P.tr(psdst[0:1, (kk % 2) * 512 + tg * 128:
                               (kk % 2) * 512 + (tg + 1) * 128],
                         wg[:, kk:kk + 1], K["ident"])
            trtok = P.inc("t")
            pool.release(pil4, trtok)
            P.op("v", lambda E, p=pswr: E.tensor_copy(wrow[0:1, 0:1024],
                                                      p[0:1, 0:1024]),
                 deps=[trtok])
            P.op("v", lambda E, p=pswr2: E.tensor_copy(wrow[0:1, 1024:2048],
                                                       p[0:1, 0:1024]))
            wrtok = P.inc("v")
            pool.release(piwr, wrtok)
            pool.release(piwr2, wrtok)
            psb1, pib1, ptk = pool.acquire()
            if ptk:
                P.wait("t", ptk)
            psb2, pib2, ptk2 = pool.acquire()
            if ptk2:
                P.wait("t", ptk2)
            P.waits("t", [wrtok])
            wb = []
            for kk in range(4):
                psdst = psb1 if kk < 2 else psb2
                reg = psdst[:, (kk % 2) * 512:(kk % 2) * 512 + 512]
                P.mm(reg, K["ones_row"], wrow[0:1, kk * 512:(kk + 1) * 512],
                     start=True, stop=True)
                wb.append(reg)
            bctok = P.inc("t")
            for i in range(ND):
                P.op("v", lambda E, o=short[:, i * Tc:(i + 1) * Tc], b=wb[0]:
                     E.tensor_tensor(o, o, b, ALU.mult),
                     deps=[bctok] if i == 0 else (), dr=(i == 0))
            for i in range(ND):
                P.op("v", lambda E, o=longb[:, i * Tc:(i + 1) * Tc], b=wb[1]:
                     E.tensor_tensor(o, o, b, ALU.mult), dr=(i == 0))
            for i in range(ND):
                P.op("v", lambda E, o=short[:, i * Tc:(i + 1) * Tc],
                     a=longb[:, i * Tc:(i + 1) * Tc]:
                     E.tensor_tensor(o, o, a, ALU.add), dr=(i == 0))
            for i in range(ND):
                P.op("v", lambda E, o=delta[:, i * Tc:(i + 1) * Tc], b=wb[2]:
                     E.tensor_tensor(o, o, b, ALU.mult), dr=(i == 0))
            for i in range(ND):
                P.op("v", lambda E, o=short[:, i * Tc:(i + 1) * Tc],
                     a=delta[:, i * Tc:(i + 1) * Tc]:
                     E.tensor_tensor(o, o, a, ALU.add), dr=(i == 0))
            for i in range(ND):
                P.op("v", lambda E, o=delta[:, i * Tc:(i + 1) * Tc],
                     cb=cbs[i], b=wb[3]:
                     E.tensor_tensor(o, cb[:, PAD:PAD + Tc], b, ALU.mult),
                     dr=(i == 0))
            ws_toks = []
            for i in range(ND):
                P.op("v", lambda E, o=short[:, i * Tc:(i + 1) * Tc],
                     a=delta[:, i * Tc:(i + 1) * Tc]:
                     E.tensor_tensor(o, o, a, ALU.add), dr=(i == 0))
                ws_toks.append(P.inc("v"))
            pool.release(pib1, ws_toks[-1])
            pool.release(pib2, ws_toks[-1])
            pss1, pis1, ptk = pool.acquire()
            if ptk:
                P.wait("t", ptk)
            pss2, pis2, ptk2 = pool.acquire()
            if ptk2:
                P.wait("t", ptk2)
            sq_toks = []
            for i in range(ND):
                P.op("s", lambda E, i_=short[:, i * Tc:(i + 1) * Tc],
                     o=sq2[:, (i % 4) * Tc:(i % 4) * Tc + Tc]:
                     E.activation(o, i_, AF.Square),
                     deps=[ws_toks[i]] + ([sq_toks[i - 4]] if i >= 4 else []))
                stk = P.inc("s")
                P.wait("t", stk)
                h = i // 2
                psdst = pss1 if h < 2 else pss2
                P.mm(psdst[0:1, (h % 2) * 512:(h % 2) * 512 + 512], K["ones_col"],
                     sq2[:, (i % 4) * Tc:(i % 4) * Tc + Tc],
                     start=(i % 2 == 0), stop=(i % 2 == 1))
                sq_toks.append(P.inc("t"))
            sstok = P.inc("t")
            P.op("v", lambda E, p=pss1: E.tensor_scalar(
                rrow[0:1, 0:1024], p[0:1, 0:1024], 1.0 / 256, 1e-5,
                ALU.mult, ALU.add), deps=[sstok])
            P.op("v", lambda E, p=pss2: E.tensor_scalar(
                rrow[0:1, 1024:2048], p[0:1, 0:1024], 1.0 / 256, 1e-5,
                ALU.mult, ALU.add))
            P.op("v", lambda E: E.reciprocal(rrow[0:1, 0:2048], rrow[0:1, 0:2048]))
            vtk = P.inc("v")
            pool.release(pis1, vtk)
            pool.release(pis2, vtk)
            P.op("s", lambda E: E.activation(wrow[0:1, 0:2048],
                                             rrow[0:1, 0:2048], AF.Sqrt),
                 deps=[vtk])
            rstok = P.inc("s")
            psb1, pib1, ptk = pool.acquire()
            if ptk:
                P.wait("t", ptk)
            psb2, pib2, ptk2 = pool.acquire()
            if ptk2:
                P.wait("t", ptk2)
            P.waits("t", [rstok])
            rb = []
            for h in range(4):
                psdst = psb1 if h < 2 else psb2
                reg = psdst[:, (h % 2) * 512:(h % 2) * 512 + 512]
                P.mm(reg, K["ones_row"], wrow[0:1, h * 512:(h + 1) * 512],
                     start=True, stop=True)
                rb.append(reg)
            rbtok = P.inc("t")
            rm_toks = []
            for i in range(ND):
                P.op("v", lambda E, o=short[:, i * Tc:(i + 1) * Tc],
                     b=rb[i // 2]: E.tensor_tensor(o, o, b, ALU.mult),
                     deps=[rbtok] if i == 0 else (), dr=(i == 0))
            for i in range(ND):
                P.op("v", lambda E, o=short[:, i * Tc:(i + 1) * Tc],
                     w_=onwc[:, i:i + 1]: E.tensor_scalar(
                     o, o, w_, None, ALU.mult), dr=(i == 0))
                rm_toks.append(P.inc("v"))
            pool.release(pib1, rm_toks[-1])
            pool.release(pib2, rm_toks[-1])
            P.waits("t", rm_toks)
            xres_tok = [trtok, trtok]
            for oc in range(ND):
                slot = opp[:, (oc % 2) * 1024:(oc % 2) * 1024 + 1024]
                dep = [P._oppan[oc % 2]] if P._oppan.get(oc % 2) else []
                ptok = P.dma("y", slot, ops_t[oc], deps=dep)
                xrtok = P.dma("y", xres[:, (oc % 2) * Tc:(oc % 2) * Tc + Tc],
                              x2_t[oc][:, c0:c0 + Tc],
                              deps=[xres_tok[oc % 2]])
                ps, pi, ptk = pool.acquire()
                if ptk:
                    P.wait("t", ptk)
                P.wait("t", ptok)
                psl = ps[:, 0:Tc]
                for kc in range(ND):
                    P.mm(psl, slot[:, kc * 128:(kc + 1) * 128],
                         short[:, kc * Tc:(kc + 1) * Tc],
                         start=(kc == 0), stop=(kc == ND - 1))
                P._oppan[oc % 2] = P.inc("t")
                deps = [P._oppan[oc % 2], xrtok]
                if prev_store[oc]:
                    deps.append(prev_store[oc])
                P.op("v", lambda E, p=psl, xr=xres[:, (oc % 2) * Tc:(oc % 2) * Tc + Tc]:
                     E.tensor_tensor(nf, p, xr, ALU.add), deps=deps)
                vtk = P.inc("v")
                xres_tok[oc % 2] = vtk
                prev_store[oc] = P.dma("y", x3_t[oc][:, c0:c0 + Tc], nf,
                                       deps=[vtk])
                P.wait("v", prev_store[oc])
                pool.release(pi, vtk)
            prev_vtok = vtk
        P.finish_pass()



def make_inputs(args):
    f32 = np.float32
    g = {k: np.asarray(v, dtype=f32) for k, v in args.items()}
    m = {}
    m["wq"] = np.ascontiguousarray(g["L0_attn_a_Wq"])
    m["wk"] = np.ascontiguousarray(g["L0_attn_a_Wk"])
    m["wv"] = np.ascontiguousarray(g["L0_attn_a_Wv"])
    m["wba"] = np.ascontiguousarray(np.concatenate(
        [g["L0_attn_a_Wb"], g["L0_attn_a_Wa"]], axis=1))
    m["ln1wb"] = np.ascontiguousarray(np.stack(
        [g["L0_attn_a_ln1_w"], g["L0_attn_a_ln1_b"]], axis=1))
    m["wo"] = np.ascontiguousarray(g["L0_attn_a_Wo"])
    m["w1"] = np.ascontiguousarray(g["L0_attn_a_W1"])
    m["b1"] = np.ascontiguousarray(g["L0_attn_a_b1"][:, None])
    m["w2"] = np.ascontiguousarray(g["L0_attn_a_W2"])
    m["w2s"] = np.ascontiguousarray(
        g["L0_attn_a_W2"].reshape(32, 128, 8, 128).transpose(2, 1, 0, 3)
        .reshape(1024, 4096))
    m["b2"] = np.ascontiguousarray(g["L0_attn_a_b2"][:, None])
    m["ln2wb"] = np.ascontiguousarray(np.stack(
        [g["L0_attn_a_ln2_w"], g["L0_attn_a_ln2_b"]], axis=1))
    m["wfcT"] = np.ascontiguousarray(g["L0_ffn_W_fc"].T)
    m["flnwb"] = np.ascontiguousarray(np.stack(
        [g["L0_ffn_ln_w"], g["L0_ffn_ln_b"]], axis=1))
    m["wprT"] = np.ascontiguousarray(g["L0_ffn_W_proj"].T)
    m["pqT"] = np.ascontiguousarray(g["L0_attn_b_q_proj_w"].T)
    m["pkT"] = np.ascontiguousarray(g["L0_attn_b_k_proj_w"].T)
    m["pvT"] = np.ascontiguousarray(g["L0_attn_b_v_proj_w"].T)
    m["pbT"] = np.ascontiguousarray(g["L0_attn_b_b_proj_w"].T)
    m["qcw"] = np.ascontiguousarray(g["L0_attn_b_q_conv_w"])
    m["kcw"] = np.ascontiguousarray(g["L0_attn_b_k_conv_w"])
    m["vcw"] = np.ascontiguousarray(g["L0_attn_b_v_conv_w"])
    m["firs"] = np.ascontiguousarray(g["L0_attn_b_fir_short_w"].reshape(D, 3))
    m["firl"] = np.ascontiguousarray(g["L0_attn_b_fir_long_w"].reshape(D, 31))
    fl = m["firl"]
    fd = np.zeros((128, 3 * 31 * 128), np.float32)
    arp = np.arange(128)
    for b_ in range(3):
        for j_ in range(31):
            fd[arp, (b_ * 31 + j_) * 128 + arp] = fl[b_ * 128 + arp, j_]
    m["fdiag"] = fd
    m["fgw1T"] = np.ascontiguousarray(g["L0_attn_b_fg_w1"].T)
    m["fgw1s"] = np.ascontiguousarray(
        g["L0_attn_b_fg_w1"].T.reshape(14, 128, 8, 128).transpose(2, 1, 0, 3)
        .reshape(1024, 1792))
    m["fgb1"] = np.ascontiguousarray(g["L0_attn_b_fg_b1"][:, None])
    m["fgw2T"] = np.ascontiguousarray(g["L0_attn_b_fg_w2"].T)
    m["ltemp"] = np.ascontiguousarray(g["L0_attn_b_logit_temp"][:, None])
    m["onw"] = np.ascontiguousarray(np.tile(g["L0_attn_b_o_norm_w"], 4)[:, None])
    m["oprT"] = np.ascontiguousarray(g["L0_attn_b_o_proj_w"].T)
    m["oprs"] = np.ascontiguousarray(
        g["L0_attn_b_o_proj_w"].T.reshape(8, 128, 8, 128).transpose(2, 1, 0, 3)
        .reshape(1024, 1024))
    one = np.ones
    tri_i = np.tril(one((128, 128), f32), 0)
    m["c_ones_col"] = one((128, 1), f32)
    m["c_ones_row"] = one((1, 128), f32)
    m["c_ones128"] = one((128, 128), f32)
    m["c_negtriu"] = -np.triu(one((128, 128), f32), 0)
    m["c_tri01i_x8"] = np.tile(np.triu(one((128, 128), f32), 0), (1, 8))
    maskb = np.where(np.triu(one((128, 128), bool), 1), 0.0, NEG).astype(f32)
    m["c_maskb_x8"] = np.tile(maskb, (1, 8))
    m["c_ident"] = np.eye(128, dtype=f32)
    m["c_identx8"] = np.tile(np.eye(128, dtype=f32), (1, 8))
    m["c_ident2x8"] = 2.0 * m["c_identx8"]
    m["c_mask01s_x4"] = np.tile(np.triu(one((128, 128), f32), 1), (1, 4))
    m["c_mask01i_x4"] = np.tile(np.triu(one((128, 128), f32), 0), (1, 4))
    m["c_identx4"] = np.tile(np.eye(128, dtype=f32), (1, 4))
    m["c_ident2x4"] = 2.0 * m["c_identx4"]
    m["c_tri_incl"] = np.tril(one((128, 128), f32), 0).T
    m["c_rev_strict"] = np.tril(one((128, 128), f32), -1)
    m["c_fgb2row"] = np.tile(g["L0_attn_b_fg_b2"][None, :], (128, 1))
    return m


def kernel(**inputs):
    nc = build_program()
    shared = make_inputs(inputs)
    x = np.asarray(inputs["L0_attn_a_x"], dtype=np.float32)
    B = x.shape[0]
    in_maps = []
    for b in range(B):
        im = dict(shared)
        im["x"] = np.ascontiguousarray(x[b].T)
        in_maps.append(im)
    res = run_bass_kernel_spmd(nc, in_maps, core_ids=list(range(B)))
    outs = [np.ascontiguousarray(r["out"].T) for r in res.results]
    return np.stack(outs, axis=0)


# version 28
# speedup vs baseline: 1.0012x; 1.0012x over previous
import numpy as np
from contextlib import ExitStack
import concourse.bass as bass
import concourse.mybir as mybir
from concourse.bass_utils import run_bass_kernel_spmd

F32 = mybir.dt.float32
BF16 = mybir.dt.bfloat16
AF = mybir.ActivationFunctionType
ALU = mybir.AluOpType

T = 4096
D = 1024
DBG = {"nch": None}
C = 128
NEG = -1.0e30


ENGS = ("t", "v", "s", "g", "y")
DMA_SEM = {"t": None, "v": "dv", "s": "ds", "g": "dg", "y": "dy"}
ENG_SEM = {"t": "et", "v": "ev", "s": "es", "g": "eg", "y": None}


class Prog:
    def __init__(self, nc):
        self.nc = nc
        self.cnt = {n: 0 for n in ("et", "ev", "es", "eg", "dy", "dg", "ds", "dv")}
        self.sems = {}
        self.waited = {e: {} for e in ENGS}
        self.bounds = {n: [0] for n in ("dy", "dg", "ds", "dv")}
        self.th = None
        self.last = None
        self.dirty = None
        self.last_tok = {}
        self._need_drain = {e: False for e in ENGS}

    def open_sems(self, stack):
        for n in self.cnt:
            self.sems[n] = stack.enter_context(self.nc.semaphore(n))

    def begin(self):
        self.th = {e: [] for e in ENGS}
        self.last = {e: None for e in ENGS}
        self.last_comp = {e: None for e in ENGS}
        self.dirty = {e: False for e in ENGS}
        self._sq_tok = {}
        self._qser = None

    def finish_pass(self):
        self.barrier()
        with self.nc.Block() as block:
            self.run(block)

    def emit(self, e, fn, dr=True):
        if dr and e in ("v", "s", "g") and self._need_drain.get(e):
            self.th[e].append({"fn": lambda E: E.drain(), "incs": None})
            self._need_drain[e] = False
        item = {"fn": fn, "incs": []}
        self.th[e].append(item)
        self.last[e] = item
        self.last_comp[e] = item
        self.dirty[e] = True
        self._need_drain[e] = True
        return item

    def wait(self, e, tok):
        if tok is None:
            return
        sem, val = tok
        if sem in self.QSEMS and DMA_SEM.get(e) != sem:
            bs = self.bounds[sem]
            up = [b for b in bs if b >= val]
            if up:
                val = up[0]
            else:
                self.fence(self.QOWNER[sem])
                val = self.bounds[sem][-1]
        if self.waited[e].get(sem, 0) >= val:
            return
        self.waited[e][sem] = val
        h = self.sems[sem]
        self.th[e].append({"fn": lambda E, h=h, val=val: E.wait_ge(h, val), "incs": None})

    QSEMS = ("dy", "dg", "ds", "dv")
    QOWNER = {"dy": "y", "dg": "g", "ds": "s", "dv": "v"}

    def fence(self, q):
        sem = DMA_SEM[q]
        val = self.cnt[sem]
        if self.bounds[sem] and self.bounds[sem][-1] >= val:
            return (sem, val)
        h = self.sems[sem]
        self.th[q].append({"fn": lambda E, h=h, val=val: E.wait_ge(h, val), "incs": None})
        self.waited[q][sem] = max(self.waited[q].get(sem, 0), val)
        self.bounds[sem].append(val)
        return (sem, val)

    def waits(self, e, toks):
        for tk in toks:
            self.wait(e, tk)

    def inc(self, e):
        item = self.last_comp[e]
        sem = ENG_SEM[e]
        assert item is not None and sem is not None
        for s, v in item["incs"]:
            if s == sem:
                return (sem, v)
        self.cnt[sem] += 1
        item["incs"].append((sem, self.cnt[sem]))
        return (sem, self.cnt[sem])

    def dma(self, q, out, in_, deps=()):
        self.waits(q, deps)
        sem = DMA_SEM[q]
        self.cnt[sem] += 16
        item = {"fn": lambda E, out=out, in_=in_: E.dma_start(out=out, in_=in_),
                "incs": [(sem, 16)], "dma": True}
        self.th[q].append(item)
        self.last[q] = item
        self.dirty[q] = True
        tok = (sem, self.cnt[sem])
        self.last_tok[sem] = tok
        return tok

    def op(self, e, fn, deps=(), dr=True):
        self.waits(e, deps)
        return self.emit(e, fn, dr=dr)

    def mm(self, out, lhsT, rhs, start, stop, deps=()):
        self.op("t", lambda E, o=out, l=lhsT, r=rhs, a=start, b=stop:
                E.matmul(o, l, r, start=a, stop=b), deps)

    def tr(self, out, in_, ident, deps=()):
        self.op("t", lambda E, o=out, i=in_, d=ident: E.transpose(o, i, d), deps)

    def run(self, block):
        for e in ENGS:
            items = self.th[e]
            if not items:
                continue
            sec = getattr(block, {"t": "tensor", "v": "vector", "s": "scalar",
                                  "g": "gpsimd", "y": "sync"}[e])

            def body(E, items=items):
                for it in items:
                    ins = it["fn"](E)
                    if it["incs"]:
                        for s, _v in it["incs"]:
                            ins.then_inc(self.sems[s], 16 if it.get("dma") else 1)
            sec(body)

    def barrier(self):
        toks = []
        for e in ENGS:
            if self.dirty[e] and ENG_SEM[e] and self.last_comp[e] is not None:
                toks.append(self.inc(e))
        for sem in ("dy", "dg", "ds", "dv"):
            if self.cnt[sem] > 0:
                toks.append((sem, self.cnt[sem]))
        for e in ENGS:
            self.waits(e, toks)


def batch(item_fn, n):
    return [item_fn(i) for i in range(n)]


def load_consts(P, nc, ar, c_in, names, tag=""):
    out = {}
    for nm in names:
        shape = list(c_in[nm].shape)
        ap = ar.enter_context(nc.sbuf_tensor(tag + "k_" + nm, shape, F32)).ap()
        P.dma("y", ap, c_in[nm])
        out[nm] = ap
    tok = ("dy", P.cnt["dy"])
    for e in ("t", "v", "s", "g"):
        P.wait(e, tok)
    return out


class PsumPool:

    def __init__(self, P, arena, n=4, name="pp"):
        self.P = P
        self.free_tok = [None] * n
        self.aps = [arena.enter_context(P.nc.psum_tensor(f"{name}{i}", [128, 1024], F32)).ap()
                    for i in range(n)]
        self.i = 0
        self.n = n

    def acquire(self):
        i = self.i
        self.i = (self.i + 1) % self.n
        tok = self.free_tok[i]
        return self.aps[i], i, tok

    def release(self, idx, tok):
        self.free_tok[idx] = tok



def ln_rows(P, ps_sx, ps_sq, row, ndfeat, eps, extra_ln=False):
    Tc = ps_sx.shape[-1]
    rD = 1.0 / ndfeat

    def sl(i):
        return row[0:1, i * Tc:(i + 1) * Tc]
    mu, m2, t0, t1, s_, o_ = (sl(i) for i in range(6))
    if extra_ln:
        sf, of = sl(6), sl(7)
    v, s = "v", "s"
    P.op(v, lambda E: E.tensor_scalar_mul(mu, ps_sx, rD))
    P.op(v, lambda E: E.tensor_scalar_mul(m2, ps_sq, rD))
    P.op(v, lambda E: E.tensor_tensor(t0, mu, mu, ALU.mult))
    P.op(v, lambda E: E.tensor_tensor(t0, m2, t0, ALU.subtract))
    P.op(v, lambda E: E.tensor_scalar_add(t0, t0, eps))
    P.op(v, lambda E: E.reciprocal(t0, t0))
    tok = P.inc(v)
    P.op(s, lambda E: E.activation(s_, t0, AF.Sqrt), deps=[tok])
    tok_s = P.inc(s)
    P.op(v, lambda E: E.tensor_tensor(o_, mu, s_, ALU.mult), deps=[tok_s])
    P.op(v, lambda E: E.tensor_scalar_mul(o_, o_, -1.0))
    if not extra_ln:
        P.inc(v)
        return s_, o_
    P.op(v, lambda E: E.tensor_tensor(t1, s_, mu, ALU.mult))
    P.op(v, lambda E: E.tensor_tensor(mu, t1, o_, ALU.add))
    P.op(v, lambda E: E.tensor_scalar_mul(t1, t1, 2.0))
    P.op(v, lambda E: E.tensor_tensor(t1, t1, o_, ALU.add))
    P.op(v, lambda E: E.tensor_tensor(t1, t1, o_, ALU.mult))
    P.op(v, lambda E: E.tensor_tensor(t0, s_, s_, ALU.mult))
    P.op(v, lambda E: E.tensor_tensor(t0, t0, m2, ALU.mult))
    P.op(v, lambda E: E.tensor_tensor(t0, t0, t1, ALU.add))
    P.op(v, lambda E: E.tensor_tensor(t1, mu, mu, ALU.mult))
    P.op(v, lambda E: E.tensor_tensor(t0, t0, t1, ALU.subtract))
    P.op(v, lambda E: E.tensor_scalar_add(t0, t0, eps))
    P.op(v, lambda E: E.reciprocal(t0, t0))
    tok = P.inc(v)
    P.op(s, lambda E: E.activation(t1, t0, AF.Sqrt), deps=[tok])
    tok_s = P.inc(s)
    P.op(v, lambda E: E.tensor_tensor(sf, s_, t1, ALU.mult), deps=[tok_s])
    P.op(v, lambda E: E.tensor_tensor(of, o_, mu, ALU.subtract))
    P.op(v, lambda E: E.tensor_tensor(of, of, t1, ALU.mult))
    P.inc(v)
    return (s_, o_), (sf, of)


def emit_ln_stats(P, x_tiles, Tc, pool, sq_tiles, consts):
    nd = len(x_tiles)
    g = len(sq_tiles)
    ps, pi, ptok = pool.acquire()
    ps_sx = ps[0:1, 0:Tc]
    ps_sq = ps[0:1, 512:512 + Tc]
    if ptok:
        P.wait("t", ptok)
    for i in range(nd):
        P.mm(ps_sx, consts["ones_col"], x_tiles[i], start=(i == 0), stop=(i == nd - 1))
    for base in range(0, nd, g):
        grp = range(base, min(base + g, nd))
        tk = P._sq_tok.get("last")
        if tk is not None:
            P.wait("g", tk)
        for i in grp:
            P.op("g", lambda E, o=sq_tiles[i - base], x=x_tiles[i]:
                 E.tensor_tensor(o, x, x, ALU.mult), dr=(i == base))
        stok = P.inc("g")
        P.wait("t", stok)
        for i in grp:
            P.mm(ps_sq, consts["ones_col"], sq_tiles[i - base],
                 start=(i == 0), stop=(i == nd - 1))
        P._sq_tok["last"] = P.inc("t")
    ttok = P.inc("t")
    return ps, ps_sx, ps_sq, pi, ttok


def bcast_rows(P, pool, rows, Tc, consts, deps=()):
    ps, pi, ptok = pool.acquire()
    if ptok:
        P.wait("t", ptok)
    P.waits("t", deps)
    outs = []
    for j, r in enumerate(rows):
        dst = ps[:, j * Tc:(j + 1) * Tc]
        P.mm(dst, consts["ones_row"], r, start=True, stop=True)
        outs.append(dst)
    tok = P.inc("t")
    return outs, pi, tok



def build_program(dbg_passes=None):
    nc = bass.Bass("TRN2", target_bir_lowering=False)
    want = dbg_passes

    def on(name):
        return want is None or name in want

    ext_scratch = want is not None

    def din(name, shape):
        return nc.dram_tensor(name, list(shape), F32, kind="ExternalInput").ap()

    x0 = din("x", [D, T])
    wq, wk, wv = din("wq", [D, D]), din("wk", [D, D]), din("wv", [D, D])
    wba = din("wba", [D, 16])
    ln1w = din("ln1wb", [D, 2])
    wo = din("wo", [D, D])
    w1 = din("w1", [D, 4096])
    b1 = din("b1", [4096, 1])
    w2 = din("w2", [4096, D])
    w2s = din("w2s", [1024, 4096])
    b2 = din("b2", [D, 1])
    ln2w = din("ln2wb", [D, 2])
    wfcT = din("wfcT", [D, 4096])
    flnw = din("flnwb", [4096, 2])
    wprT = din("wprT", [4096, D])
    pqT, pkT, pvT = din("pqT", [D, D]), din("pkT", [D, D]), din("pvT", [D, D])
    pbT = din("pbT", [D, 4])
    qcw, kcw, vcw = din("qcw", [D, 4]), din("kcw", [D, 4]), din("vcw", [D, 4])
    firs = din("firs", [D, 3])
    firl = din("firl", [D, 31])
    fdiag = din("fdiag", [128, 3 * 31 * 128])
    fgw1T = din("fgw1T", [1792, D])
    fgw1s = din("fgw1s", [1024, 1792])
    fgb1 = din("fgb1", [D, 1])
    fgw2T = din("fgw2T", [D, 4])
    ltemp = din("ltemp", [1, 1])
    onw = din("onw", [D, 1])
    oprT = din("oprT", [D, D])
    oprs = din("oprs", [1024, 1024])
    c_in = {}
    for nm, shape in [("ones_col", (128, 1)), ("ones_row", (1, 128)),
                      ("ones128", (128, 128)), ("negtriu", (128, 128)),
                      ("tri01i_x8", (128, 1024)), ("maskb_x8", (128, 1024)),
                      ("ident", (128, 128)), ("identx8", (128, 1024)),
                      ("ident2x8", (128, 1024)),
                      ("mask01s_x4", (128, 512)), ("mask01i_x4", (128, 512)),
                      ("identx4", (128, 512)), ("ident2x4", (128, 512)),
                      ("tri_incl", (128, 128)), ("rev_strict", (128, 128)),
                      ("fgb2row", (128, 4))]:
        c_in[nm] = din("c_" + nm, shape)

    producer = {"s_xa": "P1", "s_x1": "P15", "s_x2": "P2", "s_hs": "P3a",
                "s_v": "P3a", "s_delta": "P3a", "s_x3": "P3b",
                "s_wfq": "P0", "s_wpq": "P0"}
    consumers = {"s_xa": ["P15"], "s_x1": ["P2"], "s_x2": ["P3a", "P3b"],
                 "s_hs": ["P3b"], "s_v": ["P3b"], "s_delta": ["P3b"],
                 "s_x3": ["P4"], "s_wfq": ["P2", "P4"], "s_wpq": ["P2", "P4"]}

    def scr(name, shape, dtype=F32):
        if not ext_scratch:
            kind = "Internal"
        elif producer[name] == "P0" or on(producer[name]):
            kind = "ExternalOutput"
        elif any(on(c) for c in consumers[name]):
            kind = "ExternalInput"
        else:
            kind = "Internal"
        return nc.dram_tensor(name, list(shape), dtype, kind=kind).ap()

    s_xa = scr("s_xa", [D, T])
    s_x1 = scr("s_x1", [D, T])
    s_x2 = scr("s_x2", [D, T])
    s_hs = scr("s_hs", [D, T])
    s_v = scr("s_v", [D, T])
    s_delta = scr("s_delta", [D, T])
    s_x3 = scr("s_x3", [D, T])
    s_wfq = scr("s_wfq", [D, 4096], BF16)
    s_wpq = scr("s_wpq", [4096, D], BF16)
    out = nc.dram_tensor("out", [D, T], F32, kind="ExternalOutput").ap()

    P = Prog(nc)
    with ExitStack() as stack:
        P.open_sems(stack)
        consts_sb = {}

        cst = ExitStack()
        stack.enter_context(cst)
        k_itc = cst.enter_context(nc.sbuf_tensor("k_itc", [128, 1], F32)).ap()

        P.begin()
        with ExitStack() as ar:
            def sb(name, shape, dtype=F32):
                return ar.enter_context(nc.sbuf_tensor(name, list(shape), dtype)).ap()
            pool = PsumPool(P, ar, 4)
            K0 = load_consts(P, nc, ar, c_in, ["ones_col", "ones_row"], tag="p0")
            trow = sb("trow", [1, 4])
            P.dma("y", trow[0:1, 0:1], ltemp)
            tok = ("dy", P.cnt["dy"])
            P.op("s", lambda E: E.activation(trow[0:1, 1:2], trow[0:1, 0:1], AF.Exp),
                 deps=[tok])
            P.op("s", lambda E: E.drain())
            P.op("s", lambda E: E.activation(trow[0:1, 2:3], trow[0:1, 1:2], AF.Ln, bias=1.0))
            stok = P.inc("s")
            P.op("v", lambda E: E.tensor_scalar_add(trow[0:1, 3:4], trow[0:1, 2:3], 1e-4),
                 deps=[stok])
            P.op("v", lambda E: E.reciprocal(trow[0:1, 3:4], trow[0:1, 3:4]))
            vtok = P.inc("v")
            ps, pi, _ = pool.acquire()
            P.waits("t", [vtok])
            P.mm(ps[:, 0:1], K0["ones_row"], trow[0:1, 3:4], start=True, stop=True)
            ttok = P.inc("t")
            P.op("v", lambda E, ps=ps: E.tensor_copy(k_itc, ps[:, 0:1]), deps=[ttok])
            pool.release(pi, P.inc("v"))

            wbuf = sb("q_w", [128, 4096])
            absb = sb("q_abs", [128, 4096])
            sgnb = sb("q_sgn", [128, 4096])
            m01 = sb("q_m01", [128, 4096])
            qout = sb("q_out", [128, 4096], BF16)
            acc = sb("q_acc", [128, 40])
            grow = sb("q_grow", [1, 8])
            gcol = sb("q_gcol", [128, 1])

            for (win, wout, nchunk, ncols, nelem) in [
                    (wfcT, s_wfq, 8, 4096, D * 4096), (wprT, s_wpq, 32, D, D * 4096)]:
                wt = win.rearrange("(c p) n -> c p n", p=128)
                qt = wout.rearrange("(c p) n -> c p n", p=128)
                vtoks = []
                for i in range(nchunk):
                    dtok = P.dma("y", wbuf[:, 0:ncols], wt[i],
                                 deps=[P._qser] if P._qser else [])
                    P.op("v", lambda E, i=i, nco=ncols: E.reduce_sum(
                        acc[:, i:i + 1], wbuf[:, 0:nco], axis=mybir.AxisListType.X,
                        apply_absolute_value=True), deps=[dtok])
                    vtoks.append(P.inc("v"))
                    if i + 1 < nchunk:
                        pass
                    P._qser = vtoks[-1]
                    if i + 1 < nchunk:
                        P.waits("y", [vtoks[-1]])
                P.op("v", lambda E, nck=nchunk: E.reduce_sum(
                    acc[:, 36:37], acc[:, 0:nck], axis=mybir.AxisListType.X))
                vtok = P.inc("v")
                ps, pi, ptok = pool.acquire()
                if ptok:
                    P.wait("t", ptok)
                P.waits("t", [vtok])
                P.mm(ps[0:1, 0:1], acc[:, 36:37], K0["ones_col"], start=True, stop=True)
                ttok = P.inc("t")
                P.op("v", lambda E, ps=ps, ne=nelem: E.tensor_scalar(
                    grow[0:1, 0:1], ps[0:1, 0:1], 1.0 / ne, 1e-6, ALU.mult, ALU.add),
                    deps=[ttok])
                P.op("v", lambda E: E.tensor_scalar_mul(grow[0:1, 1:2], grow[0:1, 0:1], 0.5))
                vtok = P.inc("v")
                pool.release(pi, vtok)
                ps2, pi2, ptok2 = pool.acquire()
                if ptok2:
                    P.wait("t", ptok2)
                P.waits("t", [vtok])
                P.mm(ps2[:, 0:1], K0["ones_row"], grow[0:1, 1:2], start=True, stop=True)
                ttok = P.inc("t")
                P.op("v", lambda E, ps2=ps2: E.tensor_copy(gcol, ps2[:, 0:1]), deps=[ttok])
                gtok = P.inc("v")
                pool.release(pi2, gtok)
                last_store = getattr(P, "_qstore", None)
                for i in range(nchunk):
                    dtok = P.dma("y", wbuf[:, 0:ncols], wt[i], deps=[P._qser])
                    P.op("s", lambda E, nco=ncols: E.activation(
                        absb[:, 0:nco], wbuf[:, 0:nco], AF.Abs), deps=[dtok, gtok])
                    P.op("s", lambda E, nco=ncols: E.activation(
                        sgnb[:, 0:nco], wbuf[:, 0:nco], AF.Sign))
                    stok = P.inc("s")
                    P.op("v", lambda E, nco=ncols: E.tensor_scalar(
                        m01[:, 0:nco], absb[:, 0:nco], gcol, None, ALU.is_gt),
                        deps=[stok])
                    P.op("v", lambda E, nco=ncols: E.tensor_tensor(
                        qout[:, 0:nco], sgnb[:, 0:nco], m01[:, 0:nco], ALU.mult),
                        deps=[last_store] if last_store else [])
                    vtok = P.inc("v")
                    P._qser = vtok
                    last_store = P.dma("y", qt[i], qout[:, 0:ncols], deps=[vtok])
                    P._qstore = last_store
            P.finish_pass()

        if on("P1"):
            emit_p1(P, nc, x0, s_xa, dict(wq=wq, wk=wk, wv=wv, wba=wba, ln1=ln1w, wo=wo),
                    c_in)

        if on("P15"):
            emit_p15(P, nc, x0, s_xa, s_x1,
                     dict(w1=w1, b1=b1, w2=w2, w2s=w2s, b2=b2, ln2=ln2w), c_in)

        if on("P2"):
            emit_ffn(P, nc, s_x1, s_x2, s_wfq, s_wpq, flnw, c_in, tag="f2")

        if on("P3a"):
            emit_p3a(P, nc, s_x2, s_hs, s_v, s_delta,
                     dict(pqT=pqT, pkT=pkT, pvT=pvT, pbT=pbT, qcw=qcw, kcw=kcw, vcw=vcw),
                     c_in)

        if on("P3b"):
            emit_p3b(P, nc, s_x2, s_hs, s_v, s_delta, s_x3,
                     dict(firs=firs, firl=firl, fdiag=fdiag, fgw1T=fgw1T,
                          fgw1s=fgw1s, fgb1=fgb1, fgw2T=fgw2T, onw=onw,
                          oprT=oprT, oprs=oprs),
                     c_in, k_itc)

        if on("P4"):
            emit_ffn(P, nc, s_x3, out, s_wfq, s_wpq, flnw, c_in, tag="f4")

        P.begin()
        P.finish_pass()

    return nc



def emit_ffn(P, nc, xin, xout, s_wfq, s_wpq, flnw, c_in, tag):
    Tc = 256
    NCH = T // Tc if DBG["nch"] is None else DBG["nch"]
    ND = D // 128
    NH = 4096 // 128
    P.begin()
    with ExitStack() as ar:
        def sb(name, shape, dtype=F32):
            return ar.enter_context(nc.sbuf_tensor(tag + name, list(shape), dtype)).ap()
        pool = PsumPool(P, ar, 3, name=tag + "pp")
        ps_ded = ar.enter_context(nc.psum_tensor(tag + "st2", [128, 1024], F32)).ap()
        K = load_consts(P, nc, ar, c_in, ["ones_col", "ones_row"], tag=tag)
        wfq = sb("wfq", [128, 8 * 4096], BF16)
        wpq = sb("wpq", [128, 32 * D], BF16)
        flw = sb("flw", [128, 32 * 2])
        sq = sb("sq", [128, 4 * Tc])
        xres_b = sb("xres", [128, 2 * Tc])
        nf = sb("nf", [128, Tc])
        h = sb("h", [128, 32 * Tc])
        hh = sb("hh", [128, 32 * Tc], BF16)
        hl = sb("hl", [128, 32 * Tc], BF16)
        row = sb("row", [1, 6 * Tc])
        nh = hh[:, 0:8 * Tc]
        nl = hl[:, 0:8 * Tc]
        nbig = h[:, 8 * Tc:16 * Tc]
        xtb = h[:, 24 * Tc:32 * Tc]
        wfq_t = s_wfq.rearrange("(c p) n -> c p n", p=128)
        wpq_t = s_wpq.rearrange("(c p) n -> c p n", p=128)
        for i in range(8):
            P.dma("y", wfq[:, i * 4096:(i + 1) * 4096], wfq_t[i])
        for i in range(32):
            P.dma("y", wpq[:, i * D:(i + 1) * D], wpq_t[i])
        fl_t = flnw.rearrange("(c p) n -> c p n", p=128)
        for i in range(32):
            P.dma("y", flw[:, i * 2:(i + 1) * 2], fl_t[i])
        wtok = ("dy", P.cnt["dy"])
        xin_t = xin.rearrange("(c p) n -> c p n", p=128)
        xout_t = xout.rearrange("(c p) n -> c p n", p=128)
        x_tiles = [xtb[:, i * Tc:(i + 1) * Tc] for i in range(ND)]
        st_tiles = [sq[:, 2 * Tc:3 * Tc], sq[:, 3 * Tc:4 * Tc]]

        def stage1a(ci, xfree_deps):
            c0 = ci * Tc
            dtoks = []
            for i in range(ND):
                dtoks.append(P.dma("y", x_tiles[i], xin_t[i][:, c0:c0 + Tc],
                                   deps=xfree_deps))
            for i in range(ND):
                P.wait("s", dtoks[i])
                P.wait("t", dtoks[i])
                P.wait("g", dtoks[i])
            ps_st, ps_sx, ps_sq, pi_st, sttok = emit_ln_stats(
                P, x_tiles, Tc, pool, st_tiles, K)
            P.wait("v", sttok)
            s_o = ln_rows(P, ps_sx, ps_sq, row, D, 1e-5, extra_ln=False)
            vtok = P.inc("v")
            pool.release(pi_st, vtok)
            bc, pi_bc, bctok = bcast_rows(P, pool, [s_o[0], s_o[1]], Tc, K,
                                          deps=[vtok])
            P.wait("v", bctok)
            for i in range(ND):
                P.op("v", lambda E, xi=x_tiles[i], b0=bc[0],
                     o=nbig[:, i * Tc:(i + 1) * Tc]:
                     E.tensor_tensor(o, xi, b0, ALU.mult), dr=(i == 0))
            for i in range(ND):
                P.op("v", lambda E, b1=bc[1], o=nbig[:, i * Tc:(i + 1) * Tc]:
                     E.tensor_tensor(o, o, b1, ALU.add), dr=(i == 0))
            napp = P.inc("v")
            pool.release(pi_bc, napp)
            return {"napp": napp, "sttok": sttok}

        def stage1b(st, wdep):
            for i in range(ND):
                P.op("s", lambda E, o=nh[:, i * Tc:(i + 1) * Tc],
                     i_=nbig[:, i * Tc:(i + 1) * Tc]:
                     E.activation(o, i_, AF.Copy),
                     deps=([st["napp"]] + ([wdep] if wdep else []))
                     if i == 0 else (), dr=(i == 0))
            sctok = P.inc("s")
            hilo_toks = []
            for i in range(ND):
                P.op("v", lambda E, o=nl[:, i * Tc:(i + 1) * Tc],
                     i_=nbig[:, i * Tc:(i + 1) * Tc],
                     hi=nh[:, i * Tc:(i + 1) * Tc]: E.tensor_tensor(
                    o, i_, hi, ALU.subtract),
                    deps=[sctok] if i == 0 else (), dr=(i == 0))
                hilo_toks.append(P.inc("v"))
            st["hilo"] = hilo_toks

        prev_store = [None] * 8
        prev_inner_rows = None
        xres_tok = [None, None]
        cur = stage1a(0, [])
        stage1b(cur, None)
        for ci in range(NCH):
            c0 = ci * Tc
            P.waits("t", [wtok] + cur["hilo"])
            h_tiles = [h[:, i * Tc:(i + 1) * Tc] for i in range(NH)]
            ps_sx = ps_ded[0:1, 0:Tc]
            ps_sq = ps_ded[0:1, 512:512 + Tc]
            sq_tiles = [sq[:, j * Tc:(j + 1) * Tc] for j in range(2)]
            pend = []

            def flush_stat():
                o0, ht, sqt, g0, gs0 = pend.pop(0)
                if o0 == 0 and prev_inner_rows is not None:
                    P.wait("t", prev_inner_rows)
                P.wait("t", g0)
                P.mm(ps_sx, K["ones_col"], ht, start=(o0 == 0), stop=(o0 == NH - 1))
                P.wait("t", gs0)
                P.mm(ps_sq, K["ones_col"], sqt, start=(o0 == 0), stop=(o0 == NH - 1))
                P._sq_tok["last"] = P.inc("t")

            for oc in range(NH):
                ps, pi, ptok = pool.acquire()
                if ptok:
                    P.wait("t", ptok)
                psl = ps[:, 0:Tc]
                for kc in range(ND):
                    lh = wfq[:, kc * 4096 + oc * 128: kc * 4096 + oc * 128 + 128]
                    P.mm(psl, lh, nh[:, kc * Tc:(kc + 1) * Tc], start=(kc == 0), stop=False)
                    P.mm(psl, lh, nl[:, kc * Tc:(kc + 1) * Tc], start=False,
                         stop=(kc == ND - 1))
                ttok = P.inc("t")
                P.op("s", lambda E, oc=oc, psl=psl: E.activation(
                    h_tiles[oc], psl, AF.Gelu), deps=[ttok])
                gtok = P.inc("s")
                pool.release(pi, gtok)
                tk = P._sq_tok.get("last")
                P.op("g", lambda E, o=sq_tiles[oc % 2], x=h_tiles[oc]:
                     E.tensor_tensor(o, x, x, ALU.mult),
                     deps=[gtok] + ([tk] if tk is not None else []), dr=True)
                gstok = P.inc("g")
                pend.append((oc, h_tiles[oc], sq_tiles[oc % 2], gtok, gstok))
                if len(pend) == 2:
                    flush_stat()
            while pend:
                flush_stat()
            sttok = P._sq_tok["last"]
            P.wait("v", sttok)
            s_o = ln_rows(P, ps_sx, ps_sq, row, 4096, 1e-5, extra_ln=False)
            vtok = P.inc("v")
            prev_inner_rows = vtok
            bc, pi_bc, bctok = bcast_rows(P, pool, [s_o[0], s_o[1]], Tc, K, deps=[vtok])
            P.wait("v", bctok)
            for i in range(NH):
                P.op("v", lambda E, o=h_tiles[i], b0=bc[0]: E.tensor_tensor(
                    o, o, b0, ALU.mult), dr=(i == 0))
            for i in range(NH):
                P.op("v", lambda E, o=h_tiles[i], b1=bc[1]: E.tensor_tensor(
                    o, o, b1, ALU.add), dr=(i == 0))
            for i in range(NH):
                P.op("v", lambda E, o=h_tiles[i], w_=flw[:, i * 2:i * 2 + 1],
                     b_=flw[:, i * 2 + 1:i * 2 + 2]: E.tensor_scalar(
                    o, o, w_, b_, ALU.mult, ALU.add), dr=(i == 0))
            happ = P.inc("v")
            for i in range(NH):
                P.op("s", lambda E, o=hh[:, i * Tc:(i + 1) * Tc], i_=h_tiles[i]:
                     E.activation(o, i_, AF.Copy),
                     deps=[happ] if i == 0 else (), dr=(i == 0))
            sctok2 = P.inc("s")
            h2_toks = []
            for i in range(NH):
                P.op("v", lambda E, o=hl[:, i * Tc:(i + 1) * Tc], i_=h_tiles[i],
                     hi=hh[:, i * Tc:(i + 1) * Tc]: E.tensor_tensor(
                    o, i_, hi, ALU.subtract),
                    deps=[sctok2] if i == 0 else (), dr=(i == 0))
                h2_toks.append(P.inc("v"))
            pool.release(pi_bc, h2_toks[-1])
            P.waits("t", h2_toks)
            nxt = None
            if ci + 1 < NCH:
                nxt = stage1a(ci + 1, [cur["napp"], cur["sttok"],
                                       h2_toks[-1], sctok2])
            xres = [xres_b[:, 0:Tc], xres_b[:, Tc:2 * Tc]]
            for oc in range(ND):
                ps, pi, ptok = pool.acquire()
                if ptok:
                    P.wait("t", ptok)
                psl = ps[:, 0:Tc]
                rtok = P.dma("y", xres[oc % 2], xin_t[oc][:, c0:c0 + Tc],
                             deps=[xres_tok[oc % 2]] if xres_tok[oc % 2] else [])
                for kc in range(NH):
                    lh = wpq[:, kc * D + oc * 128: kc * D + oc * 128 + 128]
                    P.mm(psl, lh, hh[:, kc * Tc:(kc + 1) * Tc], start=(kc == 0), stop=False)
                    P.mm(psl, lh, hl[:, kc * Tc:(kc + 1) * Tc], start=False,
                         stop=(kc == NH - 1))
                ttok = P.inc("t")
                P.op("v", lambda E, oc=oc, psl=psl, xr=xres[oc % 2]: E.tensor_tensor(
                    nf, psl, xr, ALU.add), deps=[ttok, rtok])
                vtok = P.inc("v")
                xres_tok[oc % 2] = vtok
                prev_store[oc] = P.dma("y", xout_t[oc][:, c0:c0 + Tc], nf, deps=[vtok])
                P.wait("v", prev_store[oc])
                pool.release(pi, vtok)
            wp_tok = ttok
            if nxt is not None:
                stage1b(nxt, wp_tok)
                cur = nxt
        P.finish_pass()




def emit_p1(P, nc, x0, s_xa, W, c_in):
    Tc = 256
    NCH = (T // Tc) if DBG["nch"] is None else DBG["nch"]
    ND = D // 128
    NH = 8
    P.begin()
    with ExitStack() as ar:
        def sb(name, shape, dtype=F32):
            return ar.enter_context(nc.sbuf_tensor("a" + name, list(shape), dtype)).ap()
        pool = PsumPool(P, ar, 4, name="app")
        K = load_consts(P, nc, ar, c_in,
                        ["ones_col", "ones_row", "ones128", "negtriu", "tri01i_x8",
                         "maskb_x8", "ident", "identx8", "ident2x8",
                         "tri_incl", "rev_strict"], tag="a")
        identx8_bf = sb("idbf", [128, 1024], BF16)
        P.op("v", lambda E: E.tensor_copy(identx8_bf, K["identx8"]))
        P.inc("v")
        wba = sb("wba", [128, 8 * 16])
        ln1 = sb("ln1", [128, 8 * 2])
        for i in range(8):
            P.dma("y", wba[:, i * 16:(i + 1) * 16],
                  W["wba"].rearrange("(c p) n -> c p n", p=128)[i])
            P.dma("y", ln1[:, i * 2:(i + 1) * 2],
                  W["ln1"].rearrange("(c p) n -> c p n", p=128)[i])
        wpan = sb("wpan", [128, 2 * 1024])
        wopan = sb("wopan", [128, 2 * 1024])
        xt = sb("xt", [128, ND * Tc])
        sq = sb("sq", [128, 2 * Tc])
        row = sb("row", [1, 8 * Tc])
        tmp8 = sb("tmp8", [128, 8])
        n0 = sb("n0", [128, ND * Tc])
        n1 = sb("n1", [128, ND * Tc])
        q_t = sb("q_t", [128, 2 * 1024])
        k_t = sb("k_t", [128, 2 * 1024])
        v_t = sb("v_t", [128, 2 * 1024])
        q_f = sb("q_f", [128, 2 * 1024])
        k_f = sb("k_f", [128, 2 * 1024])
        sq2 = sb("sq2", [128, 1024])
        rcol = sb("rcol", [128, 8])
        rcol2 = sb("rcol2", [128, 8])
        bl = sb("bl", [128, 2 * 16])
        gcols = sb("gcols", [128, 4 * 8])
        rep = sb("rep", [128, 1024])
        dst = sb("dst", [128, 1024])
        dinc = sb("dinc", [128, 1024])
        mt = sb("mt", [128, 1024])
        at = sb("at", [128, 1024])
        atf = sb("atf", [128, 1024])
        at_bf = sb("at_bf", [128, 1024], BF16)
        xbuf = sb("xbuf", [128, 1024], BF16)
        xtbuf = sb("xtbuf", [128, 1024], BF16)
        qbuf = sb("qbuf", [128, 1024], BF16)
        x5f = sb("x5f", [128, 1024])
        xt5f = sb("xt5f", [128, 1024])
        q6f = sb("q6f", [128, 1024])
        x6f = x5f
        xt6f = xt5f
        xt7 = xt5f
        rwg = sb("rwg", [128, 1024])
        kpr = sb("kpr", [128, 1024])
        wfn = rep
        ub = dinc
        o1b = dst
        otm = mt
        s0 = [sb("s0a", [128, 1024]), sb("s0b", [128, 1024])]
        o_f = sb("o_f", [128, 8 * 256])
        xa_o = sb("xa_o", [128, Tc])
        P.op("g", lambda E: E.memset(s0[0], 0.0))
        memtok = P.inc("g")
        P._ppan = {}
        P._wopan = {}
        P._ofdone = None
        P._stok = memtok
        P._wo_ttok = None
        P._o1read = None

        x0_t = x0.rearrange("(c p) n -> c p n", p=128)
        xa_t = s_xa.rearrange("(c p) n -> c p n", p=128)
        wsrcs = {"q": W["wq"].rearrange("(c p) n -> c p n", p=128),
                 "k": W["wk"].rearrange("(c p) n -> c p n", p=128),
                 "v": W["wv"].rearrange("(c p) n -> c p n", p=128)}
        wo_t = W["wo"].rearrange("(c p) n -> c p n", p=128)
        wtok0 = ("dy", P.cnt["dy"])
        for e in ("t", "s", "v"):
            P.wait(e, wtok0)

        prev_xa_store = [None] * ND
        prev_vtok = None
        for ci in range(NCH):
            c0 = ci * Tc
            x_tiles = [xt[:, i * Tc:(i + 1) * Tc] for i in range(ND)]
            if prev_vtok is not None:
                for e in ("t", "v", "s"):
                    P.wait(e, prev_vtok)
                    P.wait(e, P._wo_ttok)
            dt = []
            for i in range(ND):
                dt.append(P.dma("y", x_tiles[i], x0_t[i][:, c0:c0 + Tc],
                                deps=[prev_vtok] if prev_vtok else []))
            for i in range(ND):
                P.wait("s", dt[i]); P.wait("t", dt[i]); P.wait("v", dt[i])
                P.wait("g", dt[i])
            sq_tiles = [sq[:, 0:Tc], sq[:, Tc:2 * Tc]]
            ps_st, ps_sx, ps_sq, pi_st, sttok = emit_ln_stats(
                P, x_tiles, Tc, pool, sq_tiles, K)
            P.wait("v", sttok)
            (s0r, o0r), (s1r, o1r) = ln_rows(P, ps_sx, ps_sq, row, D, 1e-5,
                                             extra_ln=True)
            vtok = P.inc("v")
            pool.release(pi_st, vtok)
            bc, pi_bc, bctok = bcast_rows(P, pool, [s0r, o0r, s1r, o1r], Tc, K,
                                          deps=[vtok])
            P.wait("v", bctok)
            for i in range(ND):
                P.op("v", lambda E, xi=x_tiles[i], b=bc[0],
                     o=n0[:, i * Tc:(i + 1) * Tc]: E.tensor_tensor(o, xi, b, ALU.mult),
                     dr=(i == 0))
            for i in range(ND):
                P.op("v", lambda E, o=n0[:, i * Tc:(i + 1) * Tc], b=bc[1]:
                     E.tensor_tensor(o, o, b, ALU.add), dr=(i == 0))
            for i in range(ND):
                P.op("v", lambda E, xi=x_tiles[i], b=bc[2],
                     o=n1[:, i * Tc:(i + 1) * Tc]: E.tensor_tensor(o, xi, b, ALU.mult),
                     dr=(i == 0))
            for i in range(ND):
                P.op("v", lambda E, o=n1[:, i * Tc:(i + 1) * Tc], b=bc[3]:
                     E.tensor_tensor(o, o, b, ALU.add), dr=(i == 0))
            ntoks = []
            for i in range(ND):
                P.op("v", lambda E, o=n1[:, i * Tc:(i + 1) * Tc],
                     w_=ln1[:, i * 2:i * 2 + 1], b_=ln1[:, i * 2 + 1:i * 2 + 2]:
                     E.tensor_scalar(o, o, w_, b_, ALU.mult, ALU.add), dr=(i == 0))
                ntoks.append(P.inc("v"))
            pool.release(pi_bc, ntoks[-1])
            P.waits("t", ntoks)
            for ti, nm in enumerate(["q", "k", "v"]):
                dstt = {"q": q_t, "k": k_t, "v": v_t}[nm]
                pr = [pool.acquire(), pool.acquire()]
                for (_, _, ptk) in pr:
                    if ptk:
                        P.wait("t", ptk)
                for kc in range(ND):
                    slot = wpan[:, (kc % 2) * 1024:(kc % 2) * 1024 + 1024]
                    ptok = P.dma("y", slot, wsrcs[nm][kc],
                                 deps=[P._ppan[kc % 2]] if P._ppan.get(kc % 2) else [])
                    P.wait("t", ptok)
                    for tg in range(2):
                        for hf in range(2):
                            P.mm(pr[tg][0][:, hf * 512:(hf + 1) * 512],
                                 n1[:, kc * Tc + tg * 128: kc * Tc + tg * 128 + 128],
                                 slot[:, hf * 512:(hf + 1) * 512],
                                 start=(kc == 0), stop=(kc == ND - 1))
                    P._ppan[kc % 2] = P.inc("t")
                ttok = P.inc("t")
                for tg in range(2):
                    P.op("s", lambda E, o=dstt[:, tg * 1024:(tg + 1) * 1024],
                         p=pr[tg][0]: E.activation(o, p, AF.Silu), deps=[ttok])
                    stok = P.inc("s")
                    pool.release(pr[tg][1], stok)
                P._silu = stok
            for tg in range(2):
                psb, pib, pbt = pool.acquire()
                if pbt:
                    P.wait("t", pbt)
                for kc in range(ND):
                    P.mm(psb[:, 0:16],
                         n1[:, kc * Tc + tg * 128: kc * Tc + tg * 128 + 128],
                         wba[:, kc * 16:(kc + 1) * 16],
                         start=(kc == 0), stop=(kc == ND - 1))
                ttok = P.inc("t")
                P.op("s", lambda E, o=bl[:, tg * 16:tg * 16 + 8], p=psb[0:128, 0:8]:
                     E.activation(o, p, AF.Sigmoid), deps=[ttok])
                P.op("s", lambda E, p=psb[0:128, 8:16]:
                     E.activation(tmp8, p, AF.Sigmoid))
                P.op("s", lambda E, o=bl[:, tg * 16 + 8:tg * 16 + 16]:
                     E.activation(o, tmp8, AF.Ln))
                stok = P.inc("s")
                pool.release(pib, stok)
            bltok = stok
            l2prev = None
            for (tt, ff) in [(q_t, q_f), (k_t, k_f)]:
                for tg in range(2):
                    tile = tt[:, tg * 1024:(tg + 1) * 1024]
                    P.op("g", lambda E, i_=tile: E.tensor_tensor(
                        sq2, i_, i_, ALU.mult),
                        deps=[P._silu] + ([l2prev] if l2prev else []), dr=True)
                    stok = P.inc("g")
                    P.op("v", lambda E: E.reduce_sum(
                        rcol, sq2.rearrange("p (h k) -> p h k", h=8),
                        axis=mybir.AxisListType.X), deps=[stok])
                    P.op("v", lambda E: E.tensor_scalar_add(rcol, rcol, 1e-6))
                    P.op("v", lambda E: E.reciprocal(rcol, rcol))
                    vtok = P.inc("v")
                    l2prev = vtok
                    P.op("s", lambda E: E.activation(rcol2, rcol, AF.Sqrt),
                         deps=[vtok])
                    stok = P.inc("s")
                    P.op("v", lambda E, o=tile: E.tensor_tensor(
                        o, o, rcol2.unsqueeze(2).broadcast_to(
                            (128, 8, 128)), ALU.mult), deps=[stok])
                    ltok = P.inc("v")
                    pst, pit, ptt = pool.acquire()
                    if ptt:
                        P.wait("t", ptt)
                    P.wait("t", ltok)
                    for h in range(NH):
                        P.tr(pst[:, h * 128:(h + 1) * 128],
                             tile[:, h * 128:(h + 1) * 128], K["ident"])
                    ttok = P.inc("t")
                    P.op("v", lambda E, o=ff[:, tg * 1024:(tg + 1) * 1024], p=pst:
                         E.tensor_copy(o, p), deps=[ttok])
                    vtok = P.inc("v")
                    pool.release(pit, vtok)
            P.waits("t", [vtok, bltok])
            for tg in range(2):
                cc = ci * 2 + tg
                _gdn_chunk(P, nc, pool, K, identx8_bf, cc, tg,
                           q_t, k_t, v_t, q_f, k_f, bl, gcols, rep, dst, dinc,
                           mt, at, atf, at_bf, xbuf, xtbuf, qbuf, x5f, xt5f,
                           q6f, x6f, xt6f, xt7, rwg, kpr, wfn, ub, o1b, otm,
                           s0, o_f)
            wops = [pool.acquire() for _ in range(4)]
            for (_, _, ptk) in wops:
                if ptk:
                    P.wait("t", ptk)
            P.waits("t", [P._ofdone])

            def wo_region(oc):
                ps = wops[oc // 2][0]
                return ps[:, (oc % 2) * 512:(oc % 2) * 512 + 256]
            for kc in range(ND):
                slot = wopan[:, (kc % 2) * 1024:(kc % 2) * 1024 + 1024]
                wtok = P.dma("y", slot, wo_t[kc],
                             deps=[P._wopan[kc % 2]] if P._wopan.get(kc % 2) else [])
                P.wait("t", wtok)
                for oc in range(ND):
                    P.mm(wo_region(oc), slot[:, oc * 128:(oc + 1) * 128],
                         o_f[:, kc * 256:(kc + 1) * 256],
                         start=(kc == 0), stop=(kc == ND - 1))
                P._wopan[kc % 2] = P.inc("t")
            P._wo_ttok = P.inc("t")
            for oc in range(ND):
                deps = [P._wo_ttok]
                if prev_xa_store[oc]:
                    deps.append(prev_xa_store[oc])
                P.op("v", lambda E, p=wo_region(oc), nn=n0[:, oc * Tc:(oc + 1) * Tc]:
                     E.tensor_tensor(xa_o, p, nn, ALU.add), deps=deps)
                vtok = P.inc("v")
                prev_xa_store[oc] = P.dma("y", xa_t[oc][:, c0:c0 + Tc], xa_o,
                                          deps=[vtok])
                P.wait("v", prev_xa_store[oc])
            vtok = P.inc("v")
            for (_, pi, _) in wops:
                pool.release(pi, vtok)
            prev_vtok = vtok
        P.finish_pass()


def _gdn_chunk(P, nc, pool, K, identx8_bf, cc, tg,
               q_t, k_t, v_t, q_f, k_f, bl, gcols, rep, dst, dinc,
               mt, at, atf, at_bf, xbuf, xtbuf, qbuf, x5f, xt5f, q6f,
               x6f, xt6f, xt7, rwg, kpr, wfn, ub, o1b, otm, s0, o_f):
    NH = 8
    la = bl[:, tg * 16 + 8:tg * 16 + 16]
    beta = bl[:, tg * 16:tg * 16 + 8]
    ktT = k_t[:, tg * 1024:(tg + 1) * 1024]
    vtT = v_t[:, tg * 1024:(tg + 1) * 1024]
    qfT = q_f[:, tg * 1024:(tg + 1) * 1024]
    kfT = k_f[:, tg * 1024:(tg + 1) * 1024]
    Sin = s0[cc % 2]
    Sout = s0[(cc + 1) % 2]
    gcol = gcols[:, 0:8]
    ecol = gcols[:, 8:16]
    gC = gcols[:, 16:24]
    edb = gcols[:, 24:32]
    labc = la.unsqueeze(2).broadcast_to((128, 8, 128))

    if P._stok is not None:
        P.wait("t", P._stok)
        P.wait("v", P._stok)

    psc, pic, ptok = pool.acquire()
    if ptok:
        P.wait("t", ptok)
    P.mm(psc[:, 0:8], K["tri_incl"], la, start=True, stop=True)
    P.mm(psc[:, 512:520], K["rev_strict"], la, start=True, stop=True)
    ttok = P.inc("t")
    P.op("s", lambda E: E.activation(gcol, psc[0:128, 0:8], AF.Exp), deps=[ttok])
    P.op("s", lambda E: E.activation(ecol, psc[0:128, 512:520], AF.Exp))
    stok = P.inc("s")
    P.op("v", lambda E: E.tensor_tensor(gC, ecol, gcol, ALU.mult), deps=[stok])
    P.op("v", lambda E: E.tensor_tensor(edb, ecol, beta, ALU.mult))
    P.op("v", lambda E: E.tensor_tensor(
        rep, labc, K["tri01i_x8"].rearrange("p (h t) -> p h t", h=8), ALU.mult))
    gtok = P.inc("v")
    pool.release(pic, gtok)
    psd, pid, ptok = pool.acquire()
    if ptok:
        P.wait("t", ptok)
    P.waits("t", [gtok])
    for half in range(2):
        sl = slice(half * 512, (half + 1) * 512)
        P.mm(psd[:, sl], K["ones128"], rep[:, sl], start=True, stop=False)
        P.mm(psd[:, sl], K["negtriu"], labc[:, half * 4:(half + 1) * 4, :],
             start=False, stop=False)
        P.mm(psd[:, sl], K["ident"], K["maskb_x8"][:, sl], start=False, stop=True)
    ttok = P.inc("t")
    dstdeps = [ttok]
    if getattr(P, "_o1read", None) is not None:
        dstdeps.append(P._o1read)
    P.op("s", lambda E: E.activation(dst, psd, AF.Exp), deps=dstdeps)
    stok = P.inc("s")
    P.op("v", lambda E: E.tensor_tensor(dinc, dst, K["identx8"], ALU.add),
         deps=[stok])
    dtok = P.inc("v")
    pool.release(pid, dtok)
    pskk, pikk, ptok = pool.acquire()
    if ptok:
        P.wait("t", ptok)
    pskq, pikq, ptok2 = pool.acquire()
    if ptok2:
        P.wait("t", ptok2)
    for h in range(NH):
        hs = slice(h * 128, (h + 1) * 128)
        P.mm(pskk[:, hs], kfT[:, hs], kfT[:, hs], start=True, stop=True)
        P.mm(pskq[:, hs], kfT[:, hs], qfT[:, hs], start=True, stop=True)
    ttok = P.inc("t")
    P.wait("v", ttok)
    P.wait("v", dtok)
    for h in range(NH):
        hs = slice(h * 128, (h + 1) * 128)
        P.op("v", lambda E, hs=hs, h=h: E.scalar_tensor_tensor(
            mt[:, hs], pskk[:, hs], beta[:, h:h + 1], dst[:, hs],
            ALU.mult, ALU.mult), dr=(h == 0))
        P.op("v", lambda E, hs=hs, h=h: E.scalar_tensor_tensor(
            at[:, hs], pskq[:, hs], beta[:, h:h + 1], dinc[:, hs],
            ALU.mult, ALU.mult), dr=False)
    mtok = P.inc("v")
    pool.release(pikk, mtok)
    pool.release(pikq, mtok)
    P.op("v", lambda E: E.tensor_tensor(atf, mt, K["identx8"], ALU.add))
    P.op("v", lambda E: E.tensor_copy(at_bf, atf))
    atok = P.inc("v")
    P.waits("t", [atok])

    def batch_mm(psx, lhs, rhs):
        for h in range(NH):
            hs = slice(h * 128, (h + 1) * 128)
            P.mm(psx[:, hs], lhs[:, hs], rhs[:, hs], start=True, stop=True)
        return P.inc("t")

    def evac(dstb, psx, dep, stt=False, eng="v"):
        if stt:
            P.op(eng, lambda E: E.scalar_tensor_tensor(
                dstb, psx, -1.0, K["ident2x8"], ALU.mult, ALU.add), deps=[dep])
        elif eng == "s":
            P.op(eng, lambda E: E.activation(dstb, psx, AF.Copy), deps=[dep])
        else:
            P.op(eng, lambda E: E.tensor_copy(dstb, psx), deps=[dep])
        return P.inc(eng)

    def newton_iter(lhs_at, idq, xin, xtin, xout, xtout, qb, skip_x=False):
        psp, pip, pt = pool.acquire()
        if pt:
            P.wait("t", pt)
        t1 = batch_mm(psp, lhs_at, xin)
        qtok = evac(qb, psp, t1, stt=True)
        pool.release(pip, qtok)
        P.waits("t", [qtok])
        toks = []
        if not skip_x:
            psx, pix, pt = pool.acquire()
            if pt:
                P.wait("t", pt)
            t2 = batch_mm(psx, xtin, qb)
            vt = evac(xout, psx, t2, eng="s")
            pool.release(pix, vt)
            toks.append(vt)
        psxt, pixt, pt = pool.acquire()
        if pt:
            P.wait("t", pt)
        t3 = batch_mm(psxt, qb, xtin)
        vt2 = evac(xtout, psxt, t3)
        pool.release(pixt, vt2)
        toks.append(vt2)
        P.waits("t", toks)
        P._nt = toks

    psp, pip, pt = pool.acquire()
    if pt:
        P.wait("t", pt)
    t1 = batch_mm(psp, at_bf, identx8_bf)
    vtok = evac(xbuf, psp, t1, stt=True)
    pool.release(pip, vtok)
    P.waits("t", [vtok])
    psxt, pixt, pt = pool.acquire()
    if pt:
        P.wait("t", pt)
    t2 = batch_mm(psxt, xbuf, identx8_bf)
    vtok = evac(xtbuf, psxt, t2)
    pool.release(pixt, vtok)
    P.waits("t", [vtok])
    for _ in range(4):
        newton_iter(at_bf, identx8_bf, xbuf, xtbuf, xbuf, xtbuf, qbuf)
    P.op("v", lambda E: E.tensor_copy(x5f, xbuf), deps=P._nt)
    vtok = P.inc("v")
    pstr, pitr, pt = pool.acquire()
    if pt:
        P.wait("t", pt)
    P.waits("t", [vtok])
    for h in range(NH):
        hs = slice(h * 128, (h + 1) * 128)
        P.tr(pstr[:, hs], x5f[:, hs], K["ident"])
    t3 = P.inc("t")
    vtok = evac(xt5f, pstr, t3)
    pool.release(pitr, vtok)
    P.waits("t", [vtok])
    newton_iter(atf, None, x5f, xt5f, x6f, xt6f, q6f)
    newton_iter(atf, None, x6f, xt6f, None, xt7, q6f, skip_x=True)
    P.op("v", lambda E: E.tensor_tensor(
        rwg, ktT, gcol.unsqueeze(2).broadcast_to((128, 8, 128)),
        ALU.mult))
    P.op("v", lambda E: E.tensor_tensor(
        kpr, ktT, edb.unsqueeze(2).broadcast_to((128, 8, 128)),
        ALU.mult))
    sctok = P.inc("v")
    P.waits("t", [sctok])
    psw, piw, pt = pool.acquire()
    if pt:
        P.wait("t", pt)
    t4 = batch_mm(psw, rwg, xt7)
    P.op("v", lambda E, p=psw: E.tensor_scalar_mul(wfn, p, -1.0), deps=[t4])
    wtok = P.inc("v")
    pool.release(piw, wtok)
    P.waits("t", [wtok])
    psu, piu, pt = pool.acquire()
    if pt:
        P.wait("t", pt)
    for h in range(NH):
        hs = slice(h * 128, (h + 1) * 128)
        P.mm(psu[:, hs], xt7[:, hs], vtT[:, hs], start=True, stop=False)
        P.mm(psu[:, hs], wfn[:, hs], Sin[:, hs], start=False, stop=True)
    t5 = P.inc("t")
    utok = evac(ub, psu, t5)
    pool.release(piu, utok)
    P.waits("t", [utok])
    pso1, pio1, pt = pool.acquire()
    if pt:
        P.wait("t", pt)
    t6 = batch_mm(pso1, qfT, Sin)
    P.op("v", lambda E, p=pso1: E.tensor_tensor(
        o1b, p, gcol.unsqueeze(2).broadcast_to((128, 8, 128)),
        ALU.mult), deps=[t6])
    o1tok = P.inc("v")
    pool.release(pio1, o1tok)
    pso2, pio2, pt = pool.acquire()
    if pt:
        P.wait("t", pt)
    pss, pis, pt2 = pool.acquire()
    if pt2:
        P.wait("t", pt2)
    for h in range(NH):
        hs = slice(h * 128, (h + 1) * 128)
        P.mm(pso2[:, hs], at[:, hs], ub[:, hs], start=True, stop=True)
        P.mm(pss[:, hs], kpr[:, hs], ub[:, hs], start=True, stop=True)
    t7 = P.inc("t")
    P.op("v", lambda E, p=pso2: E.tensor_tensor(otm, p, o1b, ALU.add),
         deps=[t7, o1tok])
    otok = P.inc("v")
    P._o1read = otok
    pool.release(pio2, otok)
    for h in range(NH):
        hs = slice(h * 128, (h + 1) * 128)
        P.op("v", lambda E, hs=hs, h=h, p=pss: E.scalar_tensor_tensor(
            Sout[:, hs], Sin[:, hs], gC[:, h:h + 1], p[:, hs],
            ALU.mult, ALU.add), deps=[t7] if h == 0 else (), dr=(h == 0))
    stok_v = P.inc("v")
    pool.release(pis, stok_v)
    P._stok = stok_v
    psof, piof, pt = pool.acquire()
    if pt:
        P.wait("t", pt)
    P.waits("t", [otok])
    for h in range(NH):
        hs = slice(h * 128, (h + 1) * 128)
        P.tr(psof[:, hs], otm[:, hs], K["ident"])
    t8 = P.inc("t")
    deps = [t8]
    if P._wo_ttok is not None:
        deps.append(P._wo_ttok)
    P.op("v", lambda E: E.tensor_copy(
        o_f.rearrange("p (h t) -> p h t", h=8)[:, :, tg * 128:(tg + 1) * 128],
        psof.rearrange("p (h t) -> p h t", h=8)), deps=deps)
    oftok = P.inc("v")
    pool.release(piof, oftok)
    P._ofdone = oftok


def emit_p15(P, nc, x0, s_xa, s_x1, W, c_in):
    Tc = 512
    NCH = (T // Tc) if DBG["nch"] is None else DBG["nch"]
    ND = D // 128
    NH = 4096 // 128
    P.begin()
    with ExitStack() as ar:
        def sb(name, shape, dtype=F32):
            return ar.enter_context(nc.sbuf_tensor("f1" + name, list(shape), dtype)).ap()
        pool = PsumPool(P, ar, 4, name="f1pp")
        K = load_consts(P, nc, ar, c_in, ["ones_col", "ones_row"], tag="f1")
        ln2 = sb("ln2", [128, 8 * 2])
        b1c = sb("b1c", [128, 32])
        b2c = sb("b2c", [128, 8])
        for i in range(8):
            P.dma("y", ln2[:, i * 2:(i + 1) * 2],
                  W["ln2"].rearrange("(c p) n -> c p n", p=128)[i])
            P.dma("y", b2c[:, i:i + 1],
                  W["b2"].rearrange("(c p) n -> c p n", p=128)[i])
        for i in range(32):
            P.dma("y", b1c[:, i:i + 1],
                  W["b1"].rearrange("(c p) n -> c p n", p=128)[i])
        xat = sb("xat", [128, ND * Tc])
        x0r = sb("x0r", [128, 2 * Tc])
        xar = sb("xar", [128, 2 * Tc])
        n2 = sb("n2", [128, ND * Tc])
        h = sb("h", [128, NH * Tc])
        sq = sb("sq", [128, 2 * Tc])
        row = sb("row", [1, 8 * Tc])
        nf = sb("nf", [128, Tc])
        w1p = sb("w1p", [128, 2 * 4096])
        w2p = sb("w2p", [128, 2 * 4096])
        wtok0 = ("dy", P.cnt["dy"])
        for e in ("t", "v", "s"):
            P.wait(e, wtok0)
        xa_t = s_xa.rearrange("(c p) n -> c p n", p=128)
        x0_t = x0.rearrange("(c p) n -> c p n", p=128)
        x1_t = s_x1.rearrange("(c p) n -> c p n", p=128)
        w1_t = W["w1"].rearrange("(c p) n -> c p n", p=128)
        w2_t = W["w2"].rearrange("(c p) n -> c p n", p=128)
        w2s_t = W["w2s"].rearrange("(c p) n -> c p n", p=128)
        P._p15pan = {}
        prev_store = [None] * ND
        prev_vtok = None
        prev_ntoks = None
        prev_w1tok = None
        x0r_tok = [None, None]
        for ci in range(NCH):
            c0 = ci * Tc
            xa_tiles = [xat[:, i * Tc:(i + 1) * Tc] for i in range(ND)]
            dt = []
            for i in range(ND):
                pv = [prev_ntoks] if prev_ntoks else []
                dt.append(P.dma("y", xa_tiles[i], xa_t[i][:, c0:c0 + Tc], deps=pv))
            for tk in dt:
                P.wait("s", tk); P.wait("t", tk); P.wait("v", tk)
                P.wait("g", tk)
            sq_tiles = [sq[:, 0:Tc], sq[:, Tc:2 * Tc]]
            ps_st, ps_sx, ps_sq, pi_st, sttok = emit_ln_stats(
                P, xa_tiles, Tc, pool, sq_tiles, K)
            P.wait("v", sttok)
            s_o = ln_rows(P, ps_sx, ps_sq, row, D, 1e-5, extra_ln=False)
            vtok = P.inc("v")
            pool.release(pi_st, vtok)
            bc, pi_bc, bctok = bcast_rows(P, pool, [s_o[0], s_o[1]], Tc, K,
                                          deps=[vtok])
            P.wait("v", bctok)
            if prev_w1tok is not None:
                P.wait("v", prev_w1tok)
            for i in range(ND):
                P.op("v", lambda E, xi=xa_tiles[i], b=bc[0],
                     o=n2[:, i * Tc:(i + 1) * Tc]: E.tensor_tensor(o, xi, b, ALU.mult),
                     dr=(i == 0))
            for i in range(ND):
                P.op("v", lambda E, o=n2[:, i * Tc:(i + 1) * Tc], b=bc[1]:
                     E.tensor_tensor(o, o, b, ALU.add), dr=(i == 0))
            ntoks = []
            for i in range(ND):
                P.op("v", lambda E, o=n2[:, i * Tc:(i + 1) * Tc],
                     w_=ln2[:, i * 2:i * 2 + 1], b_=ln2[:, i * 2 + 1:i * 2 + 2]:
                     E.tensor_scalar(o, o, w_, b_, ALU.mult, ALU.add), dr=(i == 0))
                ntoks.append(P.inc("v"))
            pool.release(pi_bc, ntoks[-1])
            prev_ntoks = ntoks[-1]
            P.waits("t", ntoks)
            for og in range(NH // 4):
                slot = w1p[:, (og % 2) * 4096:(og % 2) * 4096 + 4096]
                ptoks = []
                for kc in range(ND):
                    dep = [P._p15pan[og % 2]] if P._p15pan.get(og % 2) else []
                    ptoks.append(P.dma("y", slot[:, kc * 512:(kc + 1) * 512],
                                       w1_t[kc][:, og * 512:(og + 1) * 512],
                                       deps=dep))
                P.wait("t", ptoks[-1])
                for j in range(4):
                    oc = og * 4 + j
                    ps, pi, ptk = pool.acquire()
                    if ptk:
                        P.wait("t", ptk)
                    psl = ps[:, 0:Tc]
                    for kc in range(ND):
                        P.mm(psl, slot[:, kc * 512 + j * 128: kc * 512 + j * 128 + 128],
                             n2[:, kc * Tc:(kc + 1) * Tc],
                             start=(kc == 0), stop=(kc == ND - 1))
                    mmt = P.inc("t")
                    if j == 3:
                        P._p15pan[og % 2] = mmt
                    P.op("s", lambda E, o=h[:, oc * Tc:(oc + 1) * Tc], p=psl,
                         b_=b1c[:, oc:oc + 1]: E.activation(o, p, AF.Gelu, bias=b_),
                         deps=[mmt])
                    gtok = P.inc("s")
                    pool.release(pi, gtok)
            prev_w1tok = P._p15pan[(NH // 4 - 1) % 2]
            P.wait("t", gtok)
            for oc in range(ND):
                slot = w2p[:, (oc % 2) * 4096:(oc % 2) * 4096 + 4096]
                dep = [P._p15pan[2 + oc % 2]] if P._p15pan.get(2 + oc % 2) else []
                ptok = P.dma("y", slot, w2s_t[oc], deps=dep)
                ps, pi, ptk = pool.acquire()
                if ptk:
                    P.wait("t", ptk)
                P.wait("t", ptok)
                psl = ps[:, 0:Tc]
                for kc in range(NH):
                    P.mm(psl, slot[:, kc * 128:(kc + 1) * 128],
                         h[:, kc * Tc:(kc + 1) * Tc],
                         start=(kc == 0), stop=(kc == NH - 1))
                mmt = P.inc("t")
                P._p15pan[2 + oc % 2] = mmt
                x0tok = P.dma("y", x0r[:, (oc % 2) * Tc:(oc % 2) * Tc + Tc],
                              x0_t[oc][:, c0:c0 + Tc],
                              deps=[x0r_tok[oc % 2]] if x0r_tok[oc % 2] else [])
                xatok = P.dma("y", xar[:, (oc % 2) * Tc:(oc % 2) * Tc + Tc],
                              xa_t[oc][:, c0:c0 + Tc],
                              deps=[x0r_tok[oc % 2]] if x0r_tok[oc % 2] else [])
                deps = [mmt, x0tok, xatok]
                if prev_store[oc]:
                    deps.append(prev_store[oc])
                P.op("v", lambda E, p=psl,
                     xa_=xar[:, (oc % 2) * Tc:(oc % 2) * Tc + Tc],
                     b_=b2c[:, oc:oc + 1]:
                     E.scalar_tensor_tensor(nf, p, b_, xa_, ALU.add, ALU.add),
                     deps=deps)
                P.op("v", lambda E, x0_=x0r[:, (oc % 2) * Tc:(oc % 2) * Tc + Tc]:
                     E.tensor_tensor(nf, nf, x0_, ALU.add))
                vtok = P.inc("v")
                x0r_tok[oc % 2] = vtok
                prev_store[oc] = P.dma("y", x1_t[oc][:, c0:c0 + Tc], nf,
                                       deps=[vtok])
                P.wait("v", prev_store[oc])
                pool.release(pi, vtok)
            prev_vtok = vtok
        P.finish_pass()


def emit_p3a(P, nc, s_x2, s_hs, s_v, s_delta, W, c_in):
    Tc = 256
    NCH = (T // Tc) if DBG["nch"] is None else DBG["nch"]
    ND = D // 128
    NH = 4
    P.begin()
    with ExitStack() as ar:
        def sb(name, shape, dtype=F32):
            return ar.enter_context(nc.sbuf_tensor("b" + name, list(shape), dtype)).ap()
        pool = PsumPool(P, ar, 4, name="bpp")
        K = load_consts(P, nc, ar, c_in,
                        ["ones_col", "ones_row", "ident", "mask01s_x4",
                         "mask01i_x4", "identx4", "ident2x4"], tag="b")
        identx4_bf = sb("id4bf", [128, 512], BF16)
        P.op("v", lambda E: E.tensor_copy(identx4_bf, K["identx4"]))
        P.inc("v")
        wq = sb("wq", [128, 8 * 1024])
        wk = sb("wk", [128, 8 * 1024])
        wv = sb("wv", [128, 8 * 1024])
        wb4 = sb("wb4", [128, 8 * 4])
        cw = sb("cw", [128, 3 * 8 * 4])
        for i in range(8):
            P.dma("y", wq[:, i * 1024:(i + 1) * 1024],
                  W["pqT"].rearrange("(c p) n -> c p n", p=128)[i])
            P.dma("y", wk[:, i * 1024:(i + 1) * 1024],
                  W["pkT"].rearrange("(c p) n -> c p n", p=128)[i])
            P.dma("y", wv[:, i * 1024:(i + 1) * 1024],
                  W["pvT"].rearrange("(c p) n -> c p n", p=128)[i])
            P.dma("y", wb4[:, i * 4:(i + 1) * 4],
                  W["pbT"].rearrange("(c p) n -> c p n", p=128)[i])
            for ti, wn in enumerate(["qcw", "kcw", "vcw"]):
                P.dma("y", cw[:, ti * 32 + i * 4: ti * 32 + (i + 1) * 4],
                      W[wn].rearrange("(c p) n -> c p n", p=128)[i])
        CB = 3 + Tc
        cbuf = sb("cb", [128, 8 * CB])
        tails = sb("tails", [128, 3 * 8 * 3])
        P.op("g", lambda E: E.memset(tails, 0.0))
        cbtok = P.inc("g")
        xt = sb("xt", [128, 8 * Tc])
        hs = sb("hs", [128, 8 * Tc])
        k_tm = hs[:, 0:1024]
        v_tm = hs[:, 1024:2048]
        sq = sb("sq", [128, 2 * Tc])
        sq2 = sb("sq2", [128, 1024])
        row = sb("row", [1, 8 * Tc])
        acc = sb("acc", [128, 2 * Tc])
        acc2 = sq
        q_fm = sb("q_fm", [128, 8 * Tc])
        k_fm = sb("k_fm", [128, 8 * Tc])
        v_fm = sb("v_fm", [128, 8 * Tc])
        bl = sb("bl", [128, 2 * 4])

        mt = sb("mt", [128, 512])
        at = sb("at", [128, 512])
        atf = sb("atf", [128, 512])
        at_bf = sb("at_bf", [128, 512], BF16)
        xbuf = sb("xbuf", [128, 512], BF16)
        xtbuf = sb("xtbuf", [128, 512], BF16)
        qbuf = sb("qbuf", [128, 512], BF16)
        x5f = sb("x5f", [128, 512])
        xt5f = sb("xt5f", [128, 512])
        q6f = sb("q6f", [128, 512])
        x6f, xt6f, xt7 = x5f, xt5f, xt5f
        wfn = sb("wfn", [128, 1024])
        kpr = sq2
        ub = xt[:, 0:1024]
        otm = xt[:, 1024:2048]
        s0 = [sb("s0a", [128, 2048]), sb("s0b", [128, 2048])]
        delta_fm = cbuf[:, 0:8 * Tc]
        P.op("g", lambda E: E.memset(s0[0], 0.0))
        memtok = P.inc("g")
        P._stok3 = memtok
        P._dstore = [None] * 8
        P._vstore = [None] * 8
        P._hstore = [None] * 8

        x2_t = s_x2.rearrange("(c p) n -> c p n", p=128)
        hs_t = s_hs.rearrange("(c p) n -> c p n", p=128)
        v_t_d = s_v.rearrange("(c p) n -> c p n", p=128)
        d_t = s_delta.rearrange("(c p) n -> c p n", p=128)
        wtok0 = ("dy", P.cnt["dy"])
        for e in ("t", "v", "s", "g"):
            P.wait(e, wtok0)
        P.wait("v", cbtok)
        P.wait("s", cbtok)

        prev_vtok = None
        for ci in range(NCH):
            c0 = ci * Tc
            x_tiles = [xt[:, i * Tc:(i + 1) * Tc] for i in range(ND)]
            hs_tiles = [hs[:, i * Tc:(i + 1) * Tc] for i in range(ND)]
            if prev_vtok is not None:
                spill = ("dy", P.cnt["dy"])
                for e in ("t", "v", "s", "g"):
                    P.wait(e, prev_vtok)
                    P.wait(e, spill)
            dt = []
            for i in range(ND):
                dt.append(P.dma("y", x_tiles[i], x2_t[i][:, c0:c0 + Tc],
                                deps=[prev_vtok] if prev_vtok else []))
            for tk in dt:
                P.wait("s", tk); P.wait("t", tk); P.wait("v", tk)
                P.wait("g", tk)
            sq_tiles = [sq[:, 0:Tc], sq[:, Tc:2 * Tc]]
            ps_st, ps_sx, ps_sq, pi_st, sttok = emit_ln_stats(
                P, x_tiles, Tc, pool, sq_tiles, K)
            P.wait("v", sttok)
            s_o = ln_rows(P, ps_sx, ps_sq, row, D, 1e-5, extra_ln=False)
            vtok = P.inc("v")
            pool.release(pi_st, vtok)
            bc, pi_bc, bctok = bcast_rows(P, pool, [s_o[0], s_o[1]], Tc, K,
                                          deps=[vtok])
            ntoks = []
            P.wait("v", bctok)
            for i in range(ND):
                P.op("v", lambda E, xi=x_tiles[i], b=bc[0], o=hs_tiles[i]:
                     E.tensor_tensor(o, xi, b, ALU.mult), dr=(i == 0))
            for i in range(ND):
                P.op("v", lambda E, o=hs_tiles[i], b=bc[1]:
                     E.tensor_tensor(o, o, b, ALU.add), dr=(i == 0))
                tk = P.inc("v")
                ntoks.append(tk)
                P._hstore[i] = P.dma("y", hs_t[i][:, c0:c0 + Tc], hs_tiles[i],
                                     deps=[tk])
            pool.release(pi_bc, ntoks[-1])
            P.waits("t", ntoks)
            P._hs_spill = ("dy", P.cnt["dy"])
            for ti, (wsb, dstf) in enumerate([(wq, q_fm), (wk, k_fm), (wv, v_fm)]):
                ttoks = []
                for oc in range(ND):
                    cb = cbuf[:, oc * CB:(oc + 1) * CB]
                    tl = tails[:, (ti * 8 + oc) * 3:(ti * 8 + oc) * 3 + 3]
                    dep = [P._convdone] if getattr(P, "_convdone", None) else []
                    P.op("v", lambda E, cb=cb, tl=tl: E.tensor_copy(cb[:, 0:3], tl),
                         deps=dep)
                    ttoks.append(P.inc("v"))
                for oc in range(ND):
                    cb = cbuf[:, oc * CB:(oc + 1) * CB]
                    ps, pi, ptk = pool.acquire()
                    if ptk:
                        P.wait("t", ptk)
                    psl = ps[:, 0:Tc]
                    for kc in range(ND):
                        P.mm(psl, wsb[:, kc * 1024 + oc * 128: kc * 1024 + oc * 128 + 128],
                             hs_tiles[kc], start=(kc == 0), stop=(kc == ND - 1))
                    mtk = P.inc("t")
                    P.op("s", lambda E, o=cb[:, 3:3 + Tc], p=psl:
                         E.activation(o, p, AF.Copy), deps=[mtk, ttoks[oc]])
                    stok = P.inc("s")
                    pool.release(pi, stok)
                    tl = tails[:, (ti * 8 + oc) * 3:(ti * 8 + oc) * 3 + 3]
                    P.op("v", lambda E, cb=cb, tl=tl: E.tensor_copy(
                        tl, cb[:, Tc:Tc + 3]), deps=[stok])
                    eng = "v" if (oc % 4 != 3) else "g"
                    a0 = acc[:, 0:Tc] if eng == "v" else acc2[:, 0:Tc]
                    a1 = acc[:, Tc:2 * Tc] if eng == "v" else acc2[:, Tc:2 * Tc]
                    wc0 = cw[:, ti * 32 + oc * 4: ti * 32 + oc * 4 + 1]
                    if eng == "v":
                        P.op(eng, lambda E, cb=cb, w_=wc0, o=a0: E.tensor_scalar(
                            o, cb[:, 0:Tc], w_, None, ALU.mult), deps=[stok])
                        for j in range(1, 4):
                            src_sl = (a0, a1) if j % 2 == 1 else (a1, a0)
                            P.op(eng, lambda E, cb=cb, j=j,
                                 w_=cw[:, ti * 32 + oc * 4 + j: ti * 32 + oc * 4 + j + 1],
                                 i0=src_sl[0], o=src_sl[1]: E.scalar_tensor_tensor(
                                 o, cb[:, j:j + Tc], w_, i0, ALU.mult, ALU.add))
                    else:
                        P.op(eng, lambda E, cb=cb, w_=wc0, o=a1: E.tensor_tensor(
                            o, cb[:, 0:Tc], w_.broadcast_to((128, Tc)), ALU.mult),
                            deps=[stok])
                        for j in range(1, 4):
                            P.op(eng, lambda E, cb=cb, j=j, o=a0,
                                 w_=cw[:, ti * 32 + oc * 4 + j: ti * 32 + oc * 4 + j + 1]:
                                 E.tensor_tensor(o, cb[:, j:j + Tc],
                                                 w_.broadcast_to((128, Tc)), ALU.mult))
                            P.op(eng, lambda E, o=a1, a=a0: E.tensor_tensor(
                                o, o, a, ALU.add))
                    etok = P.inc(eng)
                    P.op("s", lambda E, o=dstf[:, oc * Tc:(oc + 1) * Tc], i_=a1:
                         E.activation(o, i_, AF.Silu), deps=[etok])
                    P._convdone = P.inc("s")
            for tg in range(2):
                psb, pib, pbt = pool.acquire()
                if pbt:
                    P.wait("t", pbt)
                for kc in range(ND):
                    P.mm(psb[:, 0:4],
                         hs_tiles[kc][:, tg * 128:(tg + 1) * 128],
                         wb4[:, kc * 4:(kc + 1) * 4],
                         start=(kc == 0), stop=(kc == ND - 1))
                ttk = P.inc("t")
                P.op("s", lambda E, o=bl[:, tg * 4:(tg + 1) * 4], p=psb[0:128, 0:4]:
                     E.activation(o, p, AF.Sigmoid), deps=[ttk])
                btok = P.inc("s")
                pool.release(pib, btok)
            for (ff,) in [(q_fm,), (k_fm,)]:
                pss_, pis_, ptk = pool.acquire()
                if ptk:
                    P.wait("t", ptk)
                sqtoks = []
                for pc in range(ND):
                    P.op("s", lambda E, i_=ff[:, pc * Tc:(pc + 1) * Tc],
                         o=sq2[:, (pc % 4) * 256:(pc % 4) * 256 + 256]:
                         E.activation(o, i_, AF.Square),
                         deps=[P._convdone, btok] + (
                             [sqtoks[pc - 4]] if pc >= 4 else []))
                    stk = P.inc("s")
                    sqtoks.append(stk)
                    h = pc // 2
                    P.wait("t", stk)
                    P.mm(pss_[0:1, h * 256:(h + 1) * 256], K["ones_col"],
                         sq2[:, (pc % 4) * 256:(pc % 4) * 256 + 256],
                         start=(pc % 2 == 0), stop=(pc % 2 == 1))
                    sqtoks[pc] = P.inc("t")
                sstok = P.inc("t")
                P.op("v", lambda E, p=pss_[0:1, 0:1024]: E.tensor_scalar_add(
                    row[0:1, 0:1024], p, 1e-12), deps=[sstok])
                P.op("v", lambda E: E.reciprocal(row[0:1, 0:1024], row[0:1, 0:1024]))
                vtk = P.inc("v")
                pool.release(pis_, vtk)
                P.op("s", lambda E: E.activation(row[0:1, 1024:2048],
                                                 row[0:1, 0:1024], AF.Sqrt),
                     deps=[vtk])
                stk = P.inc("s")
                psb2, pib2, ptk2 = pool.acquire()
                if ptk2:
                    P.wait("t", ptk2)
                P.waits("t", [stk])
                for h in range(NH):
                    P.mm(psb2[:, h * 256:(h + 1) * 256], K["ones_row"],
                         row[0:1, 1024 + h * 256:1024 + (h + 1) * 256],
                         start=True, stop=True)
                btk = P.inc("t")
                sc_toks = []
                for pc in range(ND):
                    P.op("v", lambda E, o=ff[:, pc * Tc:(pc + 1) * Tc],
                         b=psb2[:, (pc // 2) * 256:(pc // 2) * 256 + 256]:
                         E.tensor_tensor(o, o, b, ALU.mult), deps=[btk])
                    sc_toks.append(P.inc("v"))
                pool.release(pib2, sc_toks[-1])
            l2tok = sc_toks[-1]
            P.waits("t", [l2tok])
            for tg in range(2):
                _delta_chunk(P, nc, pool, K, identx4_bf, ci * 2 + tg, tg, Tc,
                             q_fm, k_fm, v_fm, bl, k_tm, v_tm, mt, at, atf,
                             at_bf, xbuf, xtbuf, qbuf, x5f, xt5f, q6f,
                             x6f, xt6f, xt7, kpr, wfn, ub, otm, s0, delta_fm)
            for i in range(ND):
                dep = [P._convdone] + ([P._vstore[i]] if P._vstore[i] else [])
                P._vstore[i] = P.dma("y", v_t_d[i][:, c0:c0 + Tc],
                                     v_fm[:, i * Tc:(i + 1) * Tc], deps=dep[:1])
                dep = [P._ddone] + ([P._dstore[i]] if P._dstore[i] else [])
                P._dstore[i] = P.dma("y", d_t[i][:, c0:c0 + Tc],
                                     delta_fm[:, i * Tc:(i + 1) * Tc], deps=dep[:1])
            prev_vtok = P._ddone
        P.finish_pass()


def _delta_chunk(P, nc, pool, K, identx4_bf, cc, tg, Tc,
                 q_fm, k_fm, v_fm, bl, k_tm, v_tm, mt, at, atf,
                 at_bf, xbuf, xtbuf, qbuf, x5f, xt5f, q6f,
                 x6f, xt6f, xt7, kpr, wfn, ub, otm, s0, delta_fm):
    NH = 4
    beta = bl[:, tg * 4:(tg + 1) * 4]
    Sin = s0[cc % 2]
    Sout = s0[(cc + 1) % 2]

    if P._stok3 is not None:
        P.wait("t", P._stok3)
        P.wait("v", P._stok3)

    for (src, dst) in [(k_fm, k_tm), (v_fm, v_tm)]:
        pst, pit, ptk = pool.acquire()
        if ptk:
            P.wait("t", ptk)
        for pc in range(8):
            P.tr(pst[:, pc * 128:(pc + 1) * 128],
                 src[:, pc * Tc + tg * 128: pc * Tc + (tg + 1) * 128], K["ident"])
        ttk = P.inc("t")
        P.op("v", lambda E, o=dst, p=pst: E.tensor_copy(o, p),
             deps=[ttk, P._hs_spill])
        vtk = P.inc("v")
        pool.release(pit, vtk)
    P.waits("t", [vtk])
    pskk, pikk, ptk = pool.acquire()
    if ptk:
        P.wait("t", ptk)
    pskq, pikq, ptk2 = pool.acquire()
    if ptk2:
        P.wait("t", ptk2)
    for h in range(NH):
        hs_ = slice(h * 128, (h + 1) * 128)
        for half in range(2):
            pc = 2 * h + half
            lh = k_fm[:, pc * Tc + tg * 128: pc * Tc + (tg + 1) * 128]
            P.mm(pskk[:, hs_], lh, lh, start=(half == 0), stop=(half == 1))
            P.mm(pskq[:, hs_], lh,
                 q_fm[:, pc * Tc + tg * 128: pc * Tc + (tg + 1) * 128],
                 start=(half == 0), stop=(half == 1))
    ttok = P.inc("t")
    P.wait("v", ttok)
    for h in range(NH):
        hs_ = slice(h * 128, (h + 1) * 128)
        P.op("v", lambda E, hs_=hs_, h=h: E.scalar_tensor_tensor(
            mt[:, hs_], pskk[:, hs_], beta[:, h:h + 1], K["mask01s_x4"][:, hs_],
            ALU.mult, ALU.mult), dr=(h == 0))
        P.op("v", lambda E, hs_=hs_, h=h: E.scalar_tensor_tensor(
            at[:, hs_], pskq[:, hs_], beta[:, h:h + 1], K["mask01i_x4"][:, hs_],
            ALU.mult, ALU.mult), dr=False)
    mtok = P.inc("v")
    pool.release(pikk, mtok)
    pool.release(pikq, mtok)
    P.op("v", lambda E: E.tensor_tensor(atf, mt, K["identx4"], ALU.add))
    P.op("v", lambda E: E.tensor_copy(at_bf, atf))
    atok = P.inc("v")
    P.waits("t", [atok])

    def batch_mm(psx, lhs, rhs):
        for h in range(NH):
            hs_ = slice(h * 128, (h + 1) * 128)
            P.mm(psx[:, hs_], lhs[:, hs_], rhs[:, hs_], start=True, stop=True)
        return P.inc("t")

    def evac(dstb, psx, dep, stt=False, eng="v"):
        if stt:
            P.op(eng, lambda E: E.scalar_tensor_tensor(
                dstb, psx, -1.0, K["ident2x4"], ALU.mult, ALU.add), deps=[dep])
        elif eng == "s":
            P.op(eng, lambda E: E.activation(dstb, psx, AF.Copy), deps=[dep])
        else:
            P.op(eng, lambda E: E.tensor_copy(dstb, psx), deps=[dep])
        return P.inc(eng)

    def newton_iter(lhs_at, xin, xtin, xout, xtout, qb, skip_x=False):
        psp, pip, pt = pool.acquire()
        if pt:
            P.wait("t", pt)
        t1 = batch_mm(psp[:, 0:512], lhs_at, xin)
        qtok = evac(qb, psp[:, 0:512], t1, stt=True)
        pool.release(pip, qtok)
        P.waits("t", [qtok])
        toks = []
        if not skip_x:
            psx, pix, pt = pool.acquire()
            if pt:
                P.wait("t", pt)
            t2 = batch_mm(psx[:, 0:512], xtin, qb)
            vt = evac(xout, psx[:, 0:512], t2, eng="s")
            pool.release(pix, vt)
            toks.append(vt)
        psxt, pixt, pt = pool.acquire()
        if pt:
            P.wait("t", pt)
        t3 = batch_mm(psxt[:, 0:512], qb, xtin)
        vt2 = evac(xtout, psxt[:, 0:512], t3)
        pool.release(pixt, vt2)
        toks.append(vt2)
        P.waits("t", toks)
        P._nt = toks

    psp, pip, pt = pool.acquire()
    if pt:
        P.wait("t", pt)
    t1 = batch_mm(psp[:, 0:512], at_bf, identx4_bf)
    vtok = evac(xbuf, psp[:, 0:512], t1, stt=True)
    pool.release(pip, vtok)
    P.waits("t", [vtok])
    psxt, pixt, pt = pool.acquire()
    if pt:
        P.wait("t", pt)
    t2 = batch_mm(psxt[:, 0:512], xbuf, identx4_bf)
    vtok = evac(xtbuf, psxt[:, 0:512], t2)
    pool.release(pixt, vtok)
    P.waits("t", [vtok])
    for _ in range(4):
        newton_iter(at_bf, xbuf, xtbuf, xbuf, xtbuf, qbuf)
    P.op("v", lambda E: E.tensor_copy(x5f, xbuf), deps=P._nt)
    vtok = P.inc("v")
    pstr, pitr, pt = pool.acquire()
    if pt:
        P.wait("t", pt)
    P.waits("t", [vtok])
    for h in range(NH):
        hs_ = slice(h * 128, (h + 1) * 128)
        P.tr(pstr[:, hs_], x5f[:, hs_], K["ident"])
    t3 = P.inc("t")
    vtok = evac(xt5f, pstr[:, 0:512], t3)
    pool.release(pitr, vtok)
    P.waits("t", [vtok])
    newton_iter(atf, x5f, xt5f, x6f, xt6f, q6f)
    newton_iter(atf, x6f, xt6f, None, xt7, q6f, skip_x=True)
    P.op("v", lambda E: E.tensor_tensor(
        kpr, k_tm, beta.unsqueeze(2).broadcast_to((128, 4, 256)), ALU.mult))
    sctok = P.inc("v")
    P.waits("t", [sctok])
    psw, piw, pt = pool.acquire()
    if pt:
        P.wait("t", pt)
    for h in range(NH):
        for half in range(2):
            P.mm(psw[:, h * 256 + half * 128: h * 256 + (half + 1) * 128],
                 k_tm[:, h * 256 + half * 128: h * 256 + (half + 1) * 128],
                 xt7[:, h * 128:(h + 1) * 128], start=True, stop=True)
    t4 = P.inc("t")
    P.op("v", lambda E, p=psw: E.tensor_scalar_mul(wfn, p, -1.0), deps=[t4])
    wtok = P.inc("v")
    pool.release(piw, wtok)
    P.waits("t", [wtok])
    psu, piu, pt = pool.acquire()
    if pt:
        P.wait("t", pt)
    for h in range(NH):
        hv = slice(h * 256, (h + 1) * 256)
        P.mm(psu[:, hv], xt7[:, h * 128:(h + 1) * 128], v_tm[:, hv],
             start=True, stop=False)
        P.mm(psu[:, hv], wfn[:, h * 256:h * 256 + 128],
             Sin[:, h * 512:h * 512 + 256], start=False, stop=False)
        P.mm(psu[:, hv], wfn[:, h * 256 + 128:h * 256 + 256],
             Sin[:, h * 512 + 256:h * 512 + 512], start=False, stop=True)
    t5 = P.inc("t")
    utok = evac(ub, psu, t5)
    pool.release(piu, utok)
    P.waits("t", [utok])
    pso, pio, pt = pool.acquire()
    if pt:
        P.wait("t", pt)
    for h in range(NH):
        hv = slice(h * 256, (h + 1) * 256)
        for half in range(2):
            pc = 2 * h + half
            P.mm(pso[:, hv],
                 q_fm[:, pc * Tc + tg * 128: pc * Tc + (tg + 1) * 128],
                 Sin[:, h * 512 + half * 256: h * 512 + (half + 1) * 256],
                 start=(half == 0), stop=False)
        P.mm(pso[:, hv], at[:, h * 128:(h + 1) * 128], ub[:, hv],
             start=False, stop=True)
    t6 = P.inc("t")
    otok = evac(otm, pso, t6)
    pool.release(pio, otok)
    pss1, pis1, pt = pool.acquire()
    if pt:
        P.wait("t", pt)
    pss2, pis2, pt2 = pool.acquire()
    if pt2:
        P.wait("t", pt2)
    for h in range(NH):
        hv = slice(h * 256, (h + 1) * 256)
        psx = pss1 if h < 2 else pss2
        base = (h % 2) * 512
        for half in range(2):
            P.mm(psx[:, base + half * 256: base + (half + 1) * 256],
                 kpr[:, h * 256 + half * 128: h * 256 + (half + 1) * 128],
                 ub[:, hv], start=True, stop=True)
    t7 = P.inc("t")
    P.op("v", lambda E, p=pss1: E.tensor_tensor(
        Sout[:, 0:1024], Sin[:, 0:1024], p, ALU.add), deps=[t7])
    P.op("v", lambda E, p=pss2: E.tensor_tensor(
        Sout[:, 1024:2048], Sin[:, 1024:2048], p, ALU.add))
    stok_v = P.inc("v")
    pool.release(pis1, stok_v)
    pool.release(pis2, stok_v)
    P._stok3 = stok_v
    psof, piof, pt = pool.acquire()
    if pt:
        P.wait("t", pt)
    P.waits("t", [otok])
    for pc in range(8):
        P.tr(psof[:, pc * 128:(pc + 1) * 128], otm[:, pc * 128:(pc + 1) * 128],
             K["ident"])
    t8 = P.inc("t")
    deps = [t8]
    if getattr(P, "_ddone", None) is not None and tg == 0:
        pass
    P.op("v", lambda E: E.tensor_copy(
        delta_fm.rearrange("p (c t) -> p c t", c=8)[:, :, tg * 128:(tg + 1) * 128],
        psof.rearrange("p (c t) -> p c t", c=8)), deps=deps)
    dftok = P.inc("v")
    pool.release(piof, dftok)
    P._ddone = dftok


def emit_p3b(P, nc, s_x2, s_hs, s_v, s_delta, s_x3, W, c_in, k_itc):
    Tc = 512
    NCH = (T // Tc) if DBG["nch"] is None else DBG["nch"]
    ND = D // 128
    PAD = 30
    CB = PAD + Tc
    P.begin()
    with ExitStack() as ar:
        def sb(name, shape, dtype=F32):
            return ar.enter_context(nc.sbuf_tensor("c" + name, list(shape), dtype)).ap()
        pool = PsumPool(P, ar, 4, name="cpp")
        K = load_consts(P, nc, ar, c_in, ["ones_col", "ones_row", "ident",
                                          "fgb2row"], tag="c")
        fsw = sb("fsw", [128, 8 * 3])
        flw_ = sb("flw", [128, 8 * 31])
        fgb1c = sb("fgb1c", [128, 8])
        fw2 = sb("fw2", [128, 8 * 4])
        onwc = sb("onwc", [128, 8])
        for i in range(8):
            P.dma("y", fsw[:, i * 3:(i + 1) * 3],
                  W["firs"].rearrange("(c p) n -> c p n", p=128)[i])
            P.dma("y", flw_[:, i * 31:(i + 1) * 31],
                  W["firl"].rearrange("(c p) n -> c p n", p=128)[i])
            P.dma("y", fgb1c[:, i:i + 1],
                  W["fgb1"].rearrange("(c p) n -> c p n", p=128)[i])
            P.dma("y", fw2[:, i * 4:(i + 1) * 4],
                  W["fgw2T"].rearrange("(c p) n -> c p n", p=128)[i])
            P.dma("y", onwc[:, i:i + 1],
                  W["onw"].rearrange("(c p) n -> c p n", p=128)[i])
        cbuf = sb("cb", [128, 8 * CB])
        tails = sb("tails", [128, 8 * PAD])
        P.op("g", lambda E: E.memset(tails, 0.0))
        tailtok = P.inc("g")
        delta = sb("delta", [128, 8 * Tc])
        hsb = sb("hs", [128, 8 * Tc])
        short = sb("short", [128, 8 * Tc])
        longb = sb("long", [128, 8 * Tc])
        ghid = sb("ghid", [128, 8 * Tc])
        accv = sb("accv", [128, 2 * Tc])
        accg = sb("accg", [128, 2 * Tc])
        gscr = sb("gscr", [128, 2 * Tc])
        fdg = sb("fdg", [128, 3 * 31 * 128])
        P.dma("y", fdg, W["fdiag"])
        xres = sb("xres", [128, 2 * Tc])
        nf = sb("nf", [128, Tc])
        lgt = sb("lgt", [128, 16])
        wgt = sb("wgt", [128, 16])
        scol = sb("scol", [128, 2])
        wrow = sb("wrow", [1, 4 * Tc])
        rrow = sb("rrow", [1, 4 * Tc])
        fgp = sb("fgp", [128, 2 * 14 * 128])
        opp = sb("opp", [128, 2 * 1024])
        sq2 = ghid

        x2_t = s_x2.rearrange("(c p) n -> c p n", p=128)
        hs_t = s_hs.rearrange("(c p) n -> c p n", p=128)
        v_t = s_v.rearrange("(c p) n -> c p n", p=128)
        d_t = s_delta.rearrange("(c p) n -> c p n", p=128)
        x3_t = s_x3.rearrange("(c p) n -> c p n", p=128)
        fg_t = W["fgw1T"].rearrange("(c p) n -> c p n", p=128)
        fgs_t = W["fgw1s"].rearrange("(c p) n -> c p n", p=128)
        ops_t = W["oprs"].rearrange("(c p) n -> c p n", p=128)
        op_t = W["oprT"].rearrange("(c p) n -> c p n", p=128)
        wtok0 = ("dy", P.cnt["dy"])
        for e in ("t", "v", "s", "g"):
            P.wait(e, wtok0)
            P.wait(e, tailtok)
        P._fgpan = {}
        P._oppan = {}
        prev_vtok = None
        prev_store = [None] * ND
        for ci in range(NCH):
            c0 = ci * Tc
            if prev_vtok is not None:
                spill = ("dy", P.cnt["dy"])
                for e in ("t", "v", "s", "g"):
                    P.wait(e, prev_vtok)
                    P.wait(e, spill)
            pdep = [prev_vtok] if prev_vtok else []
            dts = []
            for i in range(ND):
                cb = cbuf[:, i * CB:(i + 1) * CB]
                tl = tails[:, i * PAD:(i + 1) * PAD]
                P.op("v", lambda E, cb=cb, tl=tl: E.tensor_copy(cb[:, 0:PAD], tl),
                     deps=pdep)
                vtk = P.inc("v")
                dts.append(P.dma("y", cb[:, PAD:PAD + Tc], v_t[i][:, c0:c0 + Tc],
                                 deps=[vtk]))
                dts.append(P.dma("y", delta[:, i * Tc:(i + 1) * Tc],
                                 d_t[i][:, c0:c0 + Tc], deps=pdep))
                dts.append(P.dma("y", hsb[:, i * Tc:(i + 1) * Tc],
                                 hs_t[i][:, c0:c0 + Tc], deps=pdep))
            ldtok = ("dy", P.cnt["dy"])
            for e in ("t", "v", "s", "g"):
                P.wait(e, ldtok)
            cbs = [cbuf[:, i * CB:(i + 1) * CB] for i in range(ND)]
            for i in range(ND):
                P.op("v", lambda E, cb=cbs[i], tl=tails[:, i * PAD:(i + 1) * PAD]:
                     E.tensor_copy(tl, cb[:, Tc:Tc + PAD]), dr=(i == 0))
            NPE = 3
            ps_a, pi_a, ptk = pool.acquire()
            if ptk:
                P.wait("t", ptk)
            ps_b, pi_b, ptk2 = pool.acquire()
            if ptk2:
                P.wait("t", ptk2)
            pe_toks = []
            for b in range(NPE):
                psl = (ps_a if b < 2 else ps_b)[:, (b % 2) * 512:(b % 2) * 512 + 512]
                for j in range(31):
                    P.mm(psl, fdg[:, (b * 31 + j) * 128:(b * 31 + j + 1) * 128],
                         cbs[b][:, j:j + Tc], start=(j == 0), stop=(j == 30))
                pe_toks.append(P.inc("t"))
                P.op("s", lambda E, o=longb[:, b * Tc:(b + 1) * Tc], p=psl:
                     E.activation(o, p, AF.Copy), deps=[pe_toks[-1]])
            evac_tok = P.inc("s")
            pool.release(pi_a, evac_tok)
            pool.release(pi_b, evac_tok)
            for i in range(NPE, ND):
                P.op("v", lambda E, cb=cbs[i], w_=flw_[:, i * 31:i * 31 + 1],
                     o=longb[:, i * Tc:(i + 1) * Tc]:
                     E.tensor_scalar(o, cb[:, 0:Tc], w_, None, ALU.mult),
                     dr=(i == NPE))
            for j in range(1, 31):
                for i in range(NPE, ND):
                    P.op("v", lambda E, cb=cbs[i], j=j,
                         o=longb[:, i * Tc:(i + 1) * Tc],
                         w_=flw_[:, i * 31 + j:i * 31 + j + 1]:
                         E.scalar_tensor_tensor(o, cb[:, j:j + Tc], w_, o,
                                                ALU.mult, ALU.add), dr=(i == NPE))
            dve_tok = P.inc("v")
            for i in range(ND):
                P.op("g", lambda E, cb=cbs[i], o=short[:, i * Tc:(i + 1) * Tc],
                     w_=fsw[:, i * 3:i * 3 + 1]:
                     E.tensor_tensor(o, cb[:, PAD - 2:PAD - 2 + Tc],
                                     w_.broadcast_to((128, Tc)), ALU.mult),
                     dr=True)
                P.op("g", lambda E, cb=cbs[i], o=gscr[:, 0:Tc],
                     w_=fsw[:, i * 3 + 1:i * 3 + 2]:
                     E.tensor_tensor(o, cb[:, PAD - 1:PAD - 1 + Tc],
                                     w_.broadcast_to((128, Tc)), ALU.mult),
                     dr=False)
                P.op("g", lambda E, cb=cbs[i], o=gscr[:, Tc:2 * Tc],
                     w_=fsw[:, i * 3 + 2:i * 3 + 3]:
                     E.tensor_tensor(o, cb[:, PAD:PAD + Tc],
                                     w_.broadcast_to((128, Tc)), ALU.mult),
                     dr=False)
                P.op("g", lambda E, o=short[:, i * Tc:(i + 1) * Tc], a=gscr[:, 0:Tc]:
                     E.tensor_tensor(o, o, a, ALU.add))
                P.op("g", lambda E, o=short[:, i * Tc:(i + 1) * Tc],
                     a=gscr[:, Tc:2 * Tc]:
                     E.tensor_tensor(o, o, a, ALU.add))
            convtok = dve_tok
            mean_dst = {"sm": [accv[:, 0:Tc], accv[:, Tc:2 * Tc]],
                        "lm": [accg[:, 0:Tc], accg[:, Tc:2 * Tc]],
                        "dm": [xres[:, 0:Tc], xres[:, Tc:2 * Tc]]}
            srcs = {"sm": short, "lm": longb, "dm": delta}
            mtriples = []
            for nm in ["sm", "lm", "dm"]:
                ssrc = srcs[nm]
                for half in range(2):
                    tiles = [ssrc[:, (2 * h + half) * Tc:(2 * h + half + 1) * Tc]
                             for h in range(4)]
                    mtriples.append((mean_dst[nm][half], tiles))
            for k, (o, tl) in enumerate(mtriples):
                P.op("g", lambda E, o=o, a=tl[0], b=tl[1]:
                     E.tensor_tensor(o, a, b, ALU.add),
                     deps=[convtok, evac_tok] if k == 0 else (), dr=(k == 0))
            for ph in (2, 3):
                for k, (o, tl) in enumerate(mtriples):
                    P.op("g", lambda E, o=o, a=tl[ph]:
                         E.tensor_tensor(o, o, a, ALU.add), dr=(k == 0))
            for k, (o, tl) in enumerate(mtriples):
                P.op("g", lambda E, o=o: E.tensor_scalar_mul(o, o, 0.25),
                     dr=(k == 0))
            meantok = P.inc("g")
            P.wait("v", meantok)
            P.waits("t", [meantok])
            gate_in = ([hsb[:, i * Tc:(i + 1) * Tc] for i in range(8)]
                       + mean_dst["sm"] + mean_dst["lm"] + mean_dst["dm"])
            for oc in range(ND):
                slot = fgp[:, (oc % 2) * 14 * 128:(oc % 2) * 14 * 128 + 14 * 128]
                dep = [P._fgpan[oc % 2]] if P._fgpan.get(oc % 2) else []
                ptok = P.dma("y", slot, fgs_t[oc], deps=dep)
                ps, pi, ptk = pool.acquire()
                if ptk:
                    P.wait("t", ptk)
                P.wait("t", ptok)
                psl = ps[:, 0:Tc]
                for kc in range(14):
                    P.mm(psl, slot[:, kc * 128:(kc + 1) * 128], gate_in[kc],
                         start=(kc == 0), stop=(kc == 13))
                P._fgpan[oc % 2] = P.inc("t")
                P.op("s", lambda E, o=ghid[:, oc * Tc:(oc + 1) * Tc], p=psl,
                     b_=fgb1c[:, oc:oc + 1]: E.activation(o, p, AF.Gelu, bias=b_),
                     deps=[P._fgpan[oc % 2]])
                gtok = P.inc("s")
                pool.release(pi, gtok)
            P.wait("t", gtok)
            psl4, pil4, ptk = pool.acquire()
            if ptk:
                P.wait("t", ptk)
            pswr, piwr, ptk2 = pool.acquire()
            if ptk2:
                P.wait("t", ptk2)
            pswr2, piwr2, ptk3 = pool.acquire()
            if ptk3:
                P.wait("t", ptk3)
            for tg in range(4):
                reg = psl4[:, tg * 256:tg * 256 + 4]
                for kc in range(ND):
                    P.mm(reg, ghid[:, kc * Tc + tg * 128: kc * Tc + (tg + 1) * 128],
                         fw2[:, kc * 4:(kc + 1) * 4],
                         start=(kc == 0), stop=(kc == ND - 1))
                ltk = P.inc("t")
                lg = lgt[:, tg * 4:(tg + 1) * 4]
                wg = wgt[:, tg * 4:(tg + 1) * 4]
                P.op("v", lambda E, lg=lg, p=reg: E.tensor_tensor(
                    lg, p, K["fgb2row"], ALU.add), deps=[ltk])
                P.op("v", lambda E, lg=lg: E.tensor_scalar(
                    lg, lg, k_itc, None, ALU.mult))
                P.op("v", lambda E, lg=lg: E.reduce_max(
                    scol[:, 0:1], lg, axis=mybir.AxisListType.X))
                P.op("v", lambda E, lg=lg: E.tensor_scalar(
                    lg, lg, scol[:, 0:1], None, ALU.subtract))
                vtk = P.inc("v")
                P.op("s", lambda E, lg=lg: E.activation(lg, lg, AF.Exp),
                     deps=[vtk])
                stk = P.inc("s")
                P.op("v", lambda E, lg=lg: E.reduce_sum(
                    scol[:, 0:1], lg, axis=mybir.AxisListType.X), deps=[stk])
                P.op("v", lambda E: E.reciprocal(scol[:, 0:1], scol[:, 0:1]))
                P.op("v", lambda E, lg=lg, wg=wg: E.tensor_scalar(
                    wg, lg, scol[:, 0:1], None, ALU.mult))
                P.op("v", lambda E, wg=wg: E.tensor_scalar(
                    scol[:, 1:2], wg[:, 3:4], 0.05, None, ALU.mult))
                P.op("v", lambda E, wg=wg: E.tensor_tensor(
                    wg[:, 0:2], wg[:, 0:2],
                    scol[:, 1:2].broadcast_to((128, 2)), ALU.max))
                P.op("v", lambda E, wg=wg: E.reduce_sum(
                    scol[:, 0:1], wg, axis=mybir.AxisListType.X))
                P.op("v", lambda E: E.reciprocal(scol[:, 0:1], scol[:, 0:1]))
                P.op("v", lambda E, wg=wg: E.tensor_scalar(
                    wg, wg, scol[:, 0:1], None, ALU.mult))
                wtk = P.inc("v")
                P.waits("t", [wtk])
                for kk in range(4):
                    psdst = pswr if kk < 2 else pswr2
                    P.tr(psdst[0:1, (kk % 2) * 512 + tg * 128:
                               (kk % 2) * 512 + (tg + 1) * 128],
                         wg[:, kk:kk + 1], K["ident"])
            trtok = P.inc("t")
            pool.release(pil4, trtok)
            P.op("v", lambda E, p=pswr: E.tensor_copy(wrow[0:1, 0:1024],
                                                      p[0:1, 0:1024]),
                 deps=[trtok])
            P.op("v", lambda E, p=pswr2: E.tensor_copy(wrow[0:1, 1024:2048],
                                                       p[0:1, 0:1024]))
            wrtok = P.inc("v")
            pool.release(piwr, wrtok)
            pool.release(piwr2, wrtok)
            psb1, pib1, ptk = pool.acquire()
            if ptk:
                P.wait("t", ptk)
            psb2, pib2, ptk2 = pool.acquire()
            if ptk2:
                P.wait("t", ptk2)
            P.waits("t", [wrtok])
            wb = []
            for kk in range(4):
                psdst = psb1 if kk < 2 else psb2
                reg = psdst[:, (kk % 2) * 512:(kk % 2) * 512 + 512]
                P.mm(reg, K["ones_row"], wrow[0:1, kk * 512:(kk + 1) * 512],
                     start=True, stop=True)
                wb.append(reg)
            bctok = P.inc("t")
            for i in range(ND):
                P.op("v", lambda E, o=short[:, i * Tc:(i + 1) * Tc], b=wb[0]:
                     E.tensor_tensor(o, o, b, ALU.mult),
                     deps=[bctok] if i == 0 else (), dr=(i == 0))
            for i in range(ND):
                P.op("v", lambda E, o=longb[:, i * Tc:(i + 1) * Tc], b=wb[1]:
                     E.tensor_tensor(o, o, b, ALU.mult), dr=(i == 0))
            for i in range(ND):
                P.op("v", lambda E, o=short[:, i * Tc:(i + 1) * Tc],
                     a=longb[:, i * Tc:(i + 1) * Tc]:
                     E.tensor_tensor(o, o, a, ALU.add), dr=(i == 0))
            for i in range(ND):
                P.op("v", lambda E, o=delta[:, i * Tc:(i + 1) * Tc], b=wb[2]:
                     E.tensor_tensor(o, o, b, ALU.mult), dr=(i == 0))
            for i in range(ND):
                P.op("v", lambda E, o=short[:, i * Tc:(i + 1) * Tc],
                     a=delta[:, i * Tc:(i + 1) * Tc]:
                     E.tensor_tensor(o, o, a, ALU.add), dr=(i == 0))
            for i in range(ND):
                P.op("v", lambda E, o=delta[:, i * Tc:(i + 1) * Tc],
                     cb=cbs[i], b=wb[3]:
                     E.tensor_tensor(o, cb[:, PAD:PAD + Tc], b, ALU.mult),
                     dr=(i == 0))
            ws_toks = []
            for i in range(ND):
                P.op("v", lambda E, o=short[:, i * Tc:(i + 1) * Tc],
                     a=delta[:, i * Tc:(i + 1) * Tc]:
                     E.tensor_tensor(o, o, a, ALU.add), dr=(i == 0))
                ws_toks.append(P.inc("v"))
            pool.release(pib1, ws_toks[-1])
            pool.release(pib2, ws_toks[-1])
            pss1, pis1, ptk = pool.acquire()
            if ptk:
                P.wait("t", ptk)
            pss2, pis2, ptk2 = pool.acquire()
            if ptk2:
                P.wait("t", ptk2)
            sq_toks = []
            for i in range(ND):
                P.op("s", lambda E, i_=short[:, i * Tc:(i + 1) * Tc],
                     o=sq2[:, (i % 4) * Tc:(i % 4) * Tc + Tc]:
                     E.activation(o, i_, AF.Square),
                     deps=[ws_toks[i]] + ([sq_toks[i - 4]] if i >= 4 else []))
                stk = P.inc("s")
                P.wait("t", stk)
                h = i // 2
                psdst = pss1 if h < 2 else pss2
                P.mm(psdst[0:1, (h % 2) * 512:(h % 2) * 512 + 512], K["ones_col"],
                     sq2[:, (i % 4) * Tc:(i % 4) * Tc + Tc],
                     start=(i % 2 == 0), stop=(i % 2 == 1))
                sq_toks.append(P.inc("t"))
            sstok = P.inc("t")
            P.op("v", lambda E, p=pss1: E.tensor_scalar(
                rrow[0:1, 0:1024], p[0:1, 0:1024], 1.0 / 256, 1e-5,
                ALU.mult, ALU.add), deps=[sstok])
            P.op("v", lambda E, p=pss2: E.tensor_scalar(
                rrow[0:1, 1024:2048], p[0:1, 0:1024], 1.0 / 256, 1e-5,
                ALU.mult, ALU.add))
            P.op("v", lambda E: E.reciprocal(rrow[0:1, 0:2048], rrow[0:1, 0:2048]))
            vtk = P.inc("v")
            pool.release(pis1, vtk)
            pool.release(pis2, vtk)
            P.op("s", lambda E: E.activation(wrow[0:1, 0:2048],
                                             rrow[0:1, 0:2048], AF.Sqrt),
                 deps=[vtk])
            rstok = P.inc("s")
            psb1, pib1, ptk = pool.acquire()
            if ptk:
                P.wait("t", ptk)
            psb2, pib2, ptk2 = pool.acquire()
            if ptk2:
                P.wait("t", ptk2)
            P.waits("t", [rstok])
            rb = []
            for h in range(4):
                psdst = psb1 if h < 2 else psb2
                reg = psdst[:, (h % 2) * 512:(h % 2) * 512 + 512]
                P.mm(reg, K["ones_row"], wrow[0:1, h * 512:(h + 1) * 512],
                     start=True, stop=True)
                rb.append(reg)
            rbtok = P.inc("t")
            rm_toks = []
            for i in range(ND):
                P.op("v", lambda E, o=short[:, i * Tc:(i + 1) * Tc],
                     b=rb[i // 2]: E.tensor_tensor(o, o, b, ALU.mult),
                     deps=[rbtok] if i == 0 else (), dr=(i == 0))
            for i in range(ND):
                P.op("v", lambda E, o=short[:, i * Tc:(i + 1) * Tc],
                     w_=onwc[:, i:i + 1]: E.tensor_scalar(
                     o, o, w_, None, ALU.mult), dr=(i == 0))
                rm_toks.append(P.inc("v"))
            pool.release(pib1, rm_toks[-1])
            pool.release(pib2, rm_toks[-1])
            P.waits("t", rm_toks)
            xres_tok = [trtok, trtok]
            for oc in range(ND):
                slot = opp[:, (oc % 2) * 1024:(oc % 2) * 1024 + 1024]
                dep = [P._oppan[oc % 2]] if P._oppan.get(oc % 2) else []
                ptok = P.dma("y", slot, ops_t[oc], deps=dep)
                xrtok = P.dma("y", xres[:, (oc % 2) * Tc:(oc % 2) * Tc + Tc],
                              x2_t[oc][:, c0:c0 + Tc],
                              deps=[xres_tok[oc % 2]])
                ps, pi, ptk = pool.acquire()
                if ptk:
                    P.wait("t", ptk)
                P.wait("t", ptok)
                psl = ps[:, 0:Tc]
                for kc in range(ND):
                    P.mm(psl, slot[:, kc * 128:(kc + 1) * 128],
                         short[:, kc * Tc:(kc + 1) * Tc],
                         start=(kc == 0), stop=(kc == ND - 1))
                P._oppan[oc % 2] = P.inc("t")
                deps = [P._oppan[oc % 2], xrtok]
                if prev_store[oc]:
                    deps.append(prev_store[oc])
                P.op("v", lambda E, p=psl, xr=xres[:, (oc % 2) * Tc:(oc % 2) * Tc + Tc]:
                     E.tensor_tensor(nf, p, xr, ALU.add), deps=deps)
                vtk = P.inc("v")
                xres_tok[oc % 2] = vtk
                prev_store[oc] = P.dma("y", x3_t[oc][:, c0:c0 + Tc], nf,
                                       deps=[vtk])
                P.wait("v", prev_store[oc])
                pool.release(pi, vtk)
            prev_vtok = vtk
        P.finish_pass()



def make_inputs(args):
    f32 = np.float32
    g = {k: np.asarray(v, dtype=f32) for k, v in args.items()}
    m = {}
    m["wq"] = np.ascontiguousarray(g["L0_attn_a_Wq"])
    m["wk"] = np.ascontiguousarray(g["L0_attn_a_Wk"])
    m["wv"] = np.ascontiguousarray(g["L0_attn_a_Wv"])
    m["wba"] = np.ascontiguousarray(np.concatenate(
        [g["L0_attn_a_Wb"], g["L0_attn_a_Wa"]], axis=1))
    m["ln1wb"] = np.ascontiguousarray(np.stack(
        [g["L0_attn_a_ln1_w"], g["L0_attn_a_ln1_b"]], axis=1))
    m["wo"] = np.ascontiguousarray(g["L0_attn_a_Wo"])
    m["w1"] = np.ascontiguousarray(g["L0_attn_a_W1"])
    m["b1"] = np.ascontiguousarray(g["L0_attn_a_b1"][:, None])
    m["w2"] = np.ascontiguousarray(g["L0_attn_a_W2"])
    m["w2s"] = np.ascontiguousarray(
        g["L0_attn_a_W2"].reshape(32, 128, 8, 128).transpose(2, 1, 0, 3)
        .reshape(1024, 4096))
    m["b2"] = np.ascontiguousarray(g["L0_attn_a_b2"][:, None])
    m["ln2wb"] = np.ascontiguousarray(np.stack(
        [g["L0_attn_a_ln2_w"], g["L0_attn_a_ln2_b"]], axis=1))
    m["wfcT"] = np.ascontiguousarray(g["L0_ffn_W_fc"].T)
    m["flnwb"] = np.ascontiguousarray(np.stack(
        [g["L0_ffn_ln_w"], g["L0_ffn_ln_b"]], axis=1))
    m["wprT"] = np.ascontiguousarray(g["L0_ffn_W_proj"].T)
    m["pqT"] = np.ascontiguousarray(g["L0_attn_b_q_proj_w"].T)
    m["pkT"] = np.ascontiguousarray(g["L0_attn_b_k_proj_w"].T)
    m["pvT"] = np.ascontiguousarray(g["L0_attn_b_v_proj_w"].T)
    m["pbT"] = np.ascontiguousarray(g["L0_attn_b_b_proj_w"].T)
    m["qcw"] = np.ascontiguousarray(g["L0_attn_b_q_conv_w"])
    m["kcw"] = np.ascontiguousarray(g["L0_attn_b_k_conv_w"])
    m["vcw"] = np.ascontiguousarray(g["L0_attn_b_v_conv_w"])
    m["firs"] = np.ascontiguousarray(g["L0_attn_b_fir_short_w"].reshape(D, 3))
    m["firl"] = np.ascontiguousarray(g["L0_attn_b_fir_long_w"].reshape(D, 31))
    fl = m["firl"]
    fd = np.zeros((128, 3 * 31 * 128), np.float32)
    arp = np.arange(128)
    for b_ in range(3):
        for j_ in range(31):
            fd[arp, (b_ * 31 + j_) * 128 + arp] = fl[b_ * 128 + arp, j_]
    m["fdiag"] = fd
    m["fgw1T"] = np.ascontiguousarray(g["L0_attn_b_fg_w1"].T)
    m["fgw1s"] = np.ascontiguousarray(
        g["L0_attn_b_fg_w1"].T.reshape(14, 128, 8, 128).transpose(2, 1, 0, 3)
        .reshape(1024, 1792))
    m["fgb1"] = np.ascontiguousarray(g["L0_attn_b_fg_b1"][:, None])
    m["fgw2T"] = np.ascontiguousarray(g["L0_attn_b_fg_w2"].T)
    m["ltemp"] = np.ascontiguousarray(g["L0_attn_b_logit_temp"][:, None])
    m["onw"] = np.ascontiguousarray(np.tile(g["L0_attn_b_o_norm_w"], 4)[:, None])
    m["oprT"] = np.ascontiguousarray(g["L0_attn_b_o_proj_w"].T)
    m["oprs"] = np.ascontiguousarray(
        g["L0_attn_b_o_proj_w"].T.reshape(8, 128, 8, 128).transpose(2, 1, 0, 3)
        .reshape(1024, 1024))
    one = np.ones
    tri_i = np.tril(one((128, 128), f32), 0)
    m["c_ones_col"] = one((128, 1), f32)
    m["c_ones_row"] = one((1, 128), f32)
    m["c_ones128"] = one((128, 128), f32)
    m["c_negtriu"] = -np.triu(one((128, 128), f32), 0)
    m["c_tri01i_x8"] = np.tile(np.triu(one((128, 128), f32), 0), (1, 8))
    maskb = np.where(np.triu(one((128, 128), bool), 1), 0.0, NEG).astype(f32)
    m["c_maskb_x8"] = np.tile(maskb, (1, 8))
    m["c_ident"] = np.eye(128, dtype=f32)
    m["c_identx8"] = np.tile(np.eye(128, dtype=f32), (1, 8))
    m["c_ident2x8"] = 2.0 * m["c_identx8"]
    m["c_mask01s_x4"] = np.tile(np.triu(one((128, 128), f32), 1), (1, 4))
    m["c_mask01i_x4"] = np.tile(np.triu(one((128, 128), f32), 0), (1, 4))
    m["c_identx4"] = np.tile(np.eye(128, dtype=f32), (1, 4))
    m["c_ident2x4"] = 2.0 * m["c_identx4"]
    m["c_tri_incl"] = np.tril(one((128, 128), f32), 0).T
    m["c_rev_strict"] = np.tril(one((128, 128), f32), -1)
    m["c_fgb2row"] = np.tile(g["L0_attn_b_fg_b2"][None, :], (128, 1))
    return m


def kernel(**inputs):
    nc = build_program()
    shared = make_inputs(inputs)
    x = np.asarray(inputs["L0_attn_a_x"], dtype=np.float32)
    B = x.shape[0]
    in_maps = []
    for b in range(B):
        im = dict(shared)
        im["x"] = np.ascontiguousarray(x[b].T)
        in_maps.append(im)
    res = run_bass_kernel_spmd(nc, in_maps, core_ids=list(range(B)))
    outs = [np.ascontiguousarray(r["out"].T) for r in res.results]
    return np.stack(outs, axis=0)


# version 29
# speedup vs baseline: 1.0063x; 1.0051x over previous
import numpy as np
from contextlib import ExitStack
import concourse.bass as bass
import concourse.mybir as mybir
from concourse.bass_utils import run_bass_kernel_spmd

F32 = mybir.dt.float32
BF16 = mybir.dt.bfloat16
AF = mybir.ActivationFunctionType
ALU = mybir.AluOpType

T = 4096
D = 1024
DBG = {"nch": None}
C = 128
NEG = -1.0e30


ENGS = ("t", "v", "s", "g", "y")
DMA_SEM = {"t": None, "v": "dv", "s": "ds", "g": "dg", "y": "dy"}
ENG_SEM = {"t": "et", "v": "ev", "s": "es", "g": "eg", "y": None}


class Prog:
    def __init__(self, nc):
        self.nc = nc
        self.cnt = {n: 0 for n in ("et", "ev", "es", "eg", "dy", "dg", "ds", "dv")}
        self.sems = {}
        self.waited = {e: {} for e in ENGS}
        self.bounds = {n: [0] for n in ("dy", "dg", "ds", "dv")}
        self.th = None
        self.last = None
        self.dirty = None
        self.last_tok = {}
        self._need_drain = {e: False for e in ENGS}

    def open_sems(self, stack):
        for n in self.cnt:
            self.sems[n] = stack.enter_context(self.nc.semaphore(n))

    def begin(self):
        self.th = {e: [] for e in ENGS}
        self.last = {e: None for e in ENGS}
        self.last_comp = {e: None for e in ENGS}
        self.dirty = {e: False for e in ENGS}
        self._sq_tok = {}
        self._qser = None

    def finish_pass(self):
        self.barrier()
        with self.nc.Block() as block:
            self.run(block)

    def emit(self, e, fn, dr=True):
        if dr and e in ("v", "s", "g") and self._need_drain.get(e):
            self.th[e].append({"fn": lambda E: E.drain(), "incs": None})
            self._need_drain[e] = False
        item = {"fn": fn, "incs": []}
        self.th[e].append(item)
        self.last[e] = item
        self.last_comp[e] = item
        self.dirty[e] = True
        self._need_drain[e] = True
        return item

    def wait(self, e, tok):
        if tok is None:
            return
        sem, val = tok
        if sem in self.QSEMS and DMA_SEM.get(e) != sem:
            bs = self.bounds[sem]
            up = [b for b in bs if b >= val]
            if up:
                val = up[0]
            else:
                self.fence(self.QOWNER[sem])
                val = self.bounds[sem][-1]
        if self.waited[e].get(sem, 0) >= val:
            return
        self.waited[e][sem] = val
        h = self.sems[sem]
        self.th[e].append({"fn": lambda E, h=h, val=val: E.wait_ge(h, val), "incs": None})

    QSEMS = ("dy", "dg", "ds", "dv")
    QOWNER = {"dy": "y", "dg": "g", "ds": "s", "dv": "v"}

    def fence(self, q):
        sem = DMA_SEM[q]
        val = self.cnt[sem]
        if self.bounds[sem] and self.bounds[sem][-1] >= val:
            return (sem, val)
        h = self.sems[sem]
        self.th[q].append({"fn": lambda E, h=h, val=val: E.wait_ge(h, val), "incs": None})
        self.waited[q][sem] = max(self.waited[q].get(sem, 0), val)
        self.bounds[sem].append(val)
        return (sem, val)

    def waits(self, e, toks):
        for tk in toks:
            self.wait(e, tk)

    def inc(self, e):
        item = self.last_comp[e]
        sem = ENG_SEM[e]
        assert item is not None and sem is not None
        for s, v in item["incs"]:
            if s == sem:
                return (sem, v)
        self.cnt[sem] += 1
        item["incs"].append((sem, self.cnt[sem]))
        return (sem, self.cnt[sem])

    def dma(self, q, out, in_, deps=()):
        self.waits(q, deps)
        sem = DMA_SEM[q]
        self.cnt[sem] += 16
        item = {"fn": lambda E, out=out, in_=in_: E.dma_start(out=out, in_=in_),
                "incs": [(sem, 16)], "dma": True}
        self.th[q].append(item)
        self.last[q] = item
        self.dirty[q] = True
        tok = (sem, self.cnt[sem])
        self.last_tok[sem] = tok
        return tok

    def op(self, e, fn, deps=(), dr=True):
        self.waits(e, deps)
        return self.emit(e, fn, dr=dr)

    def mm(self, out, lhsT, rhs, start, stop, deps=()):
        self.op("t", lambda E, o=out, l=lhsT, r=rhs, a=start, b=stop:
                E.matmul(o, l, r, start=a, stop=b), deps)

    def tr(self, out, in_, ident, deps=()):
        self.op("t", lambda E, o=out, i=in_, d=ident: E.transpose(o, i, d), deps)

    def run(self, block):
        for e in ENGS:
            items = self.th[e]
            if not items:
                continue
            sec = getattr(block, {"t": "tensor", "v": "vector", "s": "scalar",
                                  "g": "gpsimd", "y": "sync"}[e])

            def body(E, items=items):
                for it in items:
                    ins = it["fn"](E)
                    if it["incs"]:
                        for s, _v in it["incs"]:
                            ins.then_inc(self.sems[s], 16 if it.get("dma") else 1)
            sec(body)

    def barrier(self):
        toks = []
        for e in ENGS:
            if self.dirty[e] and ENG_SEM[e] and self.last_comp[e] is not None:
                toks.append(self.inc(e))
        for sem in ("dy", "dg", "ds", "dv"):
            if self.cnt[sem] > 0:
                toks.append((sem, self.cnt[sem]))
        for e in ENGS:
            self.waits(e, toks)


def batch(item_fn, n):
    return [item_fn(i) for i in range(n)]


def load_consts(P, nc, ar, c_in, names, tag=""):
    out = {}
    for nm in names:
        shape = list(c_in[nm].shape)
        ap = ar.enter_context(nc.sbuf_tensor(tag + "k_" + nm, shape, F32)).ap()
        P.dma("y", ap, c_in[nm])
        out[nm] = ap
    tok = ("dy", P.cnt["dy"])
    for e in ("t", "v", "s", "g"):
        P.wait(e, tok)
    return out


class PsumPool:

    def __init__(self, P, arena, n=4, name="pp"):
        self.P = P
        self.free_tok = [None] * n
        self.aps = [arena.enter_context(P.nc.psum_tensor(f"{name}{i}", [128, 1024], F32)).ap()
                    for i in range(n)]
        self.i = 0
        self.n = n

    def acquire(self):
        i = self.i
        self.i = (self.i + 1) % self.n
        tok = self.free_tok[i]
        return self.aps[i], i, tok

    def release(self, idx, tok):
        self.free_tok[idx] = tok



def ln_rows(P, ps_sx, ps_sq, row, ndfeat, eps, extra_ln=False):
    Tc = ps_sx.shape[-1]
    rD = 1.0 / ndfeat

    def sl(i):
        return row[0:1, i * Tc:(i + 1) * Tc]
    mu, m2, t0, t1, s_, o_ = (sl(i) for i in range(6))
    if extra_ln:
        sf, of = sl(6), sl(7)
    v, s = "v", "s"
    P.op(v, lambda E: E.tensor_scalar_mul(mu, ps_sx, rD))
    P.op(v, lambda E: E.tensor_scalar_mul(m2, ps_sq, rD))
    P.op(v, lambda E: E.tensor_tensor(t0, mu, mu, ALU.mult))
    P.op(v, lambda E: E.tensor_tensor(t0, m2, t0, ALU.subtract))
    P.op(v, lambda E: E.tensor_scalar_add(t0, t0, eps))
    P.op(v, lambda E: E.reciprocal(t0, t0))
    tok = P.inc(v)
    P.op(s, lambda E: E.activation(s_, t0, AF.Sqrt), deps=[tok])
    tok_s = P.inc(s)
    P.op(v, lambda E: E.tensor_tensor(o_, mu, s_, ALU.mult), deps=[tok_s])
    P.op(v, lambda E: E.tensor_scalar_mul(o_, o_, -1.0))
    if not extra_ln:
        P.inc(v)
        return s_, o_
    P.op(v, lambda E: E.tensor_tensor(t1, s_, mu, ALU.mult))
    P.op(v, lambda E: E.tensor_tensor(mu, t1, o_, ALU.add))
    P.op(v, lambda E: E.tensor_scalar_mul(t1, t1, 2.0))
    P.op(v, lambda E: E.tensor_tensor(t1, t1, o_, ALU.add))
    P.op(v, lambda E: E.tensor_tensor(t1, t1, o_, ALU.mult))
    P.op(v, lambda E: E.tensor_tensor(t0, s_, s_, ALU.mult))
    P.op(v, lambda E: E.tensor_tensor(t0, t0, m2, ALU.mult))
    P.op(v, lambda E: E.tensor_tensor(t0, t0, t1, ALU.add))
    P.op(v, lambda E: E.tensor_tensor(t1, mu, mu, ALU.mult))
    P.op(v, lambda E: E.tensor_tensor(t0, t0, t1, ALU.subtract))
    P.op(v, lambda E: E.tensor_scalar_add(t0, t0, eps))
    P.op(v, lambda E: E.reciprocal(t0, t0))
    tok = P.inc(v)
    P.op(s, lambda E: E.activation(t1, t0, AF.Sqrt), deps=[tok])
    tok_s = P.inc(s)
    P.op(v, lambda E: E.tensor_tensor(sf, s_, t1, ALU.mult), deps=[tok_s])
    P.op(v, lambda E: E.tensor_tensor(of, o_, mu, ALU.subtract))
    P.op(v, lambda E: E.tensor_tensor(of, of, t1, ALU.mult))
    P.inc(v)
    return (s_, o_), (sf, of)


def emit_ln_stats(P, x_tiles, Tc, pool, sq_tiles, consts):
    nd = len(x_tiles)
    g = len(sq_tiles)
    ps, pi, ptok = pool.acquire()
    ps_sx = ps[0:1, 0:Tc]
    ps_sq = ps[0:1, 512:512 + Tc]
    if ptok:
        P.wait("t", ptok)
    for i in range(nd):
        P.mm(ps_sx, consts["ones_col"], x_tiles[i], start=(i == 0), stop=(i == nd - 1))
    for base in range(0, nd, g):
        grp = range(base, min(base + g, nd))
        tk = P._sq_tok.get("last")
        if tk is not None:
            P.wait("g", tk)
        for i in grp:
            P.op("g", lambda E, o=sq_tiles[i - base], x=x_tiles[i]:
                 E.tensor_tensor(o, x, x, ALU.mult), dr=(i == base))
        stok = P.inc("g")
        P.wait("t", stok)
        for i in grp:
            P.mm(ps_sq, consts["ones_col"], sq_tiles[i - base],
                 start=(i == 0), stop=(i == nd - 1))
        P._sq_tok["last"] = P.inc("t")
    ttok = P.inc("t")
    return ps, ps_sx, ps_sq, pi, ttok


def bcast_rows(P, pool, rows, Tc, consts, deps=()):
    ps, pi, ptok = pool.acquire()
    if ptok:
        P.wait("t", ptok)
    P.waits("t", deps)
    outs = []
    for j, r in enumerate(rows):
        dst = ps[:, j * Tc:(j + 1) * Tc]
        P.mm(dst, consts["ones_row"], r, start=True, stop=True)
        outs.append(dst)
    tok = P.inc("t")
    return outs, pi, tok



def build_program(dbg_passes=None):
    nc = bass.Bass("TRN2", target_bir_lowering=False)
    want = dbg_passes

    def on(name):
        return want is None or name in want

    ext_scratch = want is not None

    def din(name, shape):
        return nc.dram_tensor(name, list(shape), F32, kind="ExternalInput").ap()

    x0 = din("x", [D, T])
    wq, wk, wv = din("wq", [D, D]), din("wk", [D, D]), din("wv", [D, D])
    wba = din("wba", [D, 16])
    ln1w = din("ln1wb", [D, 2])
    wo = din("wo", [D, D])
    w1 = din("w1", [D, 4096])
    b1 = din("b1", [4096, 1])
    w2 = din("w2", [4096, D])
    w2s = din("w2s", [1024, 4096])
    b2 = din("b2", [D, 1])
    ln2w = din("ln2wb", [D, 2])
    wfcT = din("wfcT", [D, 4096])
    flnw = din("flnwb", [4096, 2])
    wprT = din("wprT", [4096, D])
    pqT, pkT, pvT = din("pqT", [D, D]), din("pkT", [D, D]), din("pvT", [D, D])
    pbT = din("pbT", [D, 4])
    qcw, kcw, vcw = din("qcw", [D, 4]), din("kcw", [D, 4]), din("vcw", [D, 4])
    firs = din("firs", [D, 3])
    firl = din("firl", [D, 31])
    fdiag = din("fdiag", [128, 3 * 31 * 128])
    fgw1T = din("fgw1T", [1792, D])
    fgw1s = din("fgw1s", [1024, 1792])
    fgb1 = din("fgb1", [D, 1])
    fgw2T = din("fgw2T", [D, 4])
    ltemp = din("ltemp", [1, 1])
    onw = din("onw", [D, 1])
    oprT = din("oprT", [D, D])
    oprs = din("oprs", [1024, 1024])
    c_in = {}
    for nm, shape in [("ones_col", (128, 1)), ("ones_row", (1, 128)),
                      ("ones128", (128, 128)), ("negtriu", (128, 128)),
                      ("tri01i_x8", (128, 1024)), ("maskb_x8", (128, 1024)),
                      ("ident", (128, 128)), ("identx8", (128, 1024)),
                      ("ident2x8", (128, 1024)),
                      ("mask01s_x4", (128, 512)), ("mask01i_x4", (128, 512)),
                      ("identx4", (128, 512)), ("ident2x4", (128, 512)),
                      ("tri_incl", (128, 128)), ("rev_strict", (128, 128)),
                      ("fgb2row", (128, 4))]:
        c_in[nm] = din("c_" + nm, shape)

    producer = {"s_xa": "P1", "s_x1": "P15", "s_x2": "P2", "s_hs": "P3a",
                "s_v": "P3a", "s_delta": "P3a", "s_x3": "P3b",
                "s_wfq": "P0", "s_wpq": "P0"}
    consumers = {"s_xa": ["P15"], "s_x1": ["P2"], "s_x2": ["P3a", "P3b"],
                 "s_hs": ["P3b"], "s_v": ["P3b"], "s_delta": ["P3b"],
                 "s_x3": ["P4"], "s_wfq": ["P2", "P4"], "s_wpq": ["P2", "P4"]}

    def scr(name, shape, dtype=F32):
        if not ext_scratch:
            kind = "Internal"
        elif producer[name] == "P0" or on(producer[name]):
            kind = "ExternalOutput"
        elif any(on(c) for c in consumers[name]):
            kind = "ExternalInput"
        else:
            kind = "Internal"
        return nc.dram_tensor(name, list(shape), dtype, kind=kind).ap()

    s_xa = scr("s_xa", [D, T])
    s_x1 = scr("s_x1", [D, T])
    s_x2 = scr("s_x2", [D, T])
    s_hs = scr("s_hs", [D, T])
    s_v = scr("s_v", [D, T])
    s_delta = scr("s_delta", [D, T])
    s_x3 = scr("s_x3", [D, T])
    s_wfq = scr("s_wfq", [D, 4096], BF16)
    s_wpq = scr("s_wpq", [4096, D], BF16)
    out = nc.dram_tensor("out", [D, T], F32, kind="ExternalOutput").ap()

    P = Prog(nc)
    with ExitStack() as stack:
        P.open_sems(stack)
        consts_sb = {}

        cst = ExitStack()
        stack.enter_context(cst)
        k_itc = cst.enter_context(nc.sbuf_tensor("k_itc", [128, 1], F32)).ap()

        P.begin()
        with ExitStack() as ar:
            def sb(name, shape, dtype=F32):
                return ar.enter_context(nc.sbuf_tensor(name, list(shape), dtype)).ap()
            pool = PsumPool(P, ar, 4)
            K0 = load_consts(P, nc, ar, c_in, ["ones_col", "ones_row"], tag="p0")
            trow = sb("trow", [1, 4])
            P.dma("y", trow[0:1, 0:1], ltemp)
            tok = ("dy", P.cnt["dy"])
            P.op("s", lambda E: E.activation(trow[0:1, 1:2], trow[0:1, 0:1], AF.Exp),
                 deps=[tok])
            P.op("s", lambda E: E.drain())
            P.op("s", lambda E: E.activation(trow[0:1, 2:3], trow[0:1, 1:2], AF.Ln, bias=1.0))
            stok = P.inc("s")
            P.op("v", lambda E: E.tensor_scalar_add(trow[0:1, 3:4], trow[0:1, 2:3], 1e-4),
                 deps=[stok])
            P.op("v", lambda E: E.reciprocal(trow[0:1, 3:4], trow[0:1, 3:4]))
            vtok = P.inc("v")
            ps, pi, _ = pool.acquire()
            P.waits("t", [vtok])
            P.mm(ps[:, 0:1], K0["ones_row"], trow[0:1, 3:4], start=True, stop=True)
            ttok = P.inc("t")
            P.op("v", lambda E, ps=ps: E.tensor_copy(k_itc, ps[:, 0:1]), deps=[ttok])
            pool.release(pi, P.inc("v"))

            wbuf = sb("q_w", [128, 4096])
            absb = sb("q_abs", [128, 4096])
            sgnb = sb("q_sgn", [128, 4096])
            m01 = sb("q_m01", [128, 4096])
            qout = sb("q_out", [128, 4096], BF16)
            acc = sb("q_acc", [128, 40])
            grow = sb("q_grow", [1, 8])
            gcol = sb("q_gcol", [128, 1])

            for (win, wout, nchunk, ncols, nelem) in [
                    (wfcT, s_wfq, 8, 4096, D * 4096), (wprT, s_wpq, 32, D, D * 4096)]:
                wt = win.rearrange("(c p) n -> c p n", p=128)
                qt = wout.rearrange("(c p) n -> c p n", p=128)
                vtoks = []
                for i in range(nchunk):
                    dtok = P.dma("y", wbuf[:, 0:ncols], wt[i],
                                 deps=[P._qser] if P._qser else [])
                    P.op("v", lambda E, i=i, nco=ncols: E.reduce_sum(
                        acc[:, i:i + 1], wbuf[:, 0:nco], axis=mybir.AxisListType.X,
                        apply_absolute_value=True), deps=[dtok])
                    vtoks.append(P.inc("v"))
                    if i + 1 < nchunk:
                        pass
                    P._qser = vtoks[-1]
                    if i + 1 < nchunk:
                        P.waits("y", [vtoks[-1]])
                P.op("v", lambda E, nck=nchunk: E.reduce_sum(
                    acc[:, 36:37], acc[:, 0:nck], axis=mybir.AxisListType.X))
                vtok = P.inc("v")
                ps, pi, ptok = pool.acquire()
                if ptok:
                    P.wait("t", ptok)
                P.waits("t", [vtok])
                P.mm(ps[0:1, 0:1], acc[:, 36:37], K0["ones_col"], start=True, stop=True)
                ttok = P.inc("t")
                P.op("v", lambda E, ps=ps, ne=nelem: E.tensor_scalar(
                    grow[0:1, 0:1], ps[0:1, 0:1], 1.0 / ne, 1e-6, ALU.mult, ALU.add),
                    deps=[ttok])
                P.op("v", lambda E: E.tensor_scalar_mul(grow[0:1, 1:2], grow[0:1, 0:1], 0.5))
                vtok = P.inc("v")
                pool.release(pi, vtok)
                ps2, pi2, ptok2 = pool.acquire()
                if ptok2:
                    P.wait("t", ptok2)
                P.waits("t", [vtok])
                P.mm(ps2[:, 0:1], K0["ones_row"], grow[0:1, 1:2], start=True, stop=True)
                ttok = P.inc("t")
                P.op("v", lambda E, ps2=ps2: E.tensor_copy(gcol, ps2[:, 0:1]), deps=[ttok])
                gtok = P.inc("v")
                pool.release(pi2, gtok)
                last_store = getattr(P, "_qstore", None)
                for i in range(nchunk):
                    dtok = P.dma("y", wbuf[:, 0:ncols], wt[i], deps=[P._qser])
                    P.op("s", lambda E, nco=ncols: E.activation(
                        absb[:, 0:nco], wbuf[:, 0:nco], AF.Abs), deps=[dtok, gtok])
                    P.op("s", lambda E, nco=ncols: E.activation(
                        sgnb[:, 0:nco], wbuf[:, 0:nco], AF.Sign))
                    stok = P.inc("s")
                    P.op("v", lambda E, nco=ncols: E.tensor_scalar(
                        m01[:, 0:nco], absb[:, 0:nco], gcol, None, ALU.is_gt),
                        deps=[stok])
                    P.op("v", lambda E, nco=ncols: E.tensor_tensor(
                        qout[:, 0:nco], sgnb[:, 0:nco], m01[:, 0:nco], ALU.mult),
                        deps=[last_store] if last_store else [])
                    vtok = P.inc("v")
                    P._qser = vtok
                    last_store = P.dma("y", qt[i], qout[:, 0:ncols], deps=[vtok])
                    P._qstore = last_store
            P.finish_pass()

        if on("P1"):
            emit_p1(P, nc, x0, s_xa, dict(wq=wq, wk=wk, wv=wv, wba=wba, ln1=ln1w, wo=wo),
                    c_in)

        if on("P15"):
            emit_p15(P, nc, x0, s_xa, s_x1,
                     dict(w1=w1, b1=b1, w2=w2, w2s=w2s, b2=b2, ln2=ln2w), c_in)

        if on("P2"):
            emit_ffn(P, nc, s_x1, s_x2, s_wfq, s_wpq, flnw, c_in, tag="f2")

        if on("P3a"):
            emit_p3a(P, nc, s_x2, s_hs, s_v, s_delta,
                     dict(pqT=pqT, pkT=pkT, pvT=pvT, pbT=pbT, qcw=qcw, kcw=kcw, vcw=vcw),
                     c_in)

        if on("P3b"):
            emit_p3b(P, nc, s_x2, s_hs, s_v, s_delta, s_x3,
                     dict(firs=firs, firl=firl, fdiag=fdiag, fgw1T=fgw1T,
                          fgw1s=fgw1s, fgb1=fgb1, fgw2T=fgw2T, onw=onw,
                          oprT=oprT, oprs=oprs),
                     c_in, k_itc)

        if on("P4"):
            emit_ffn(P, nc, s_x3, out, s_wfq, s_wpq, flnw, c_in, tag="f4")

        P.begin()
        P.finish_pass()

    return nc



def emit_ffn(P, nc, xin, xout, s_wfq, s_wpq, flnw, c_in, tag):
    Tc = 256
    NCH = T // Tc if DBG["nch"] is None else DBG["nch"]
    ND = D // 128
    NH = 4096 // 128
    P.begin()
    with ExitStack() as ar:
        def sb(name, shape, dtype=F32):
            return ar.enter_context(nc.sbuf_tensor(tag + name, list(shape), dtype)).ap()
        pool = PsumPool(P, ar, 3, name=tag + "pp")
        ps_ded = ar.enter_context(nc.psum_tensor(tag + "st2", [128, 1024], F32)).ap()
        K = load_consts(P, nc, ar, c_in, ["ones_col", "ones_row"], tag=tag)
        wfq = sb("wfq", [128, 8 * 4096], BF16)
        wpq = sb("wpq", [128, 32 * D], BF16)
        flw = sb("flw", [128, 32 * 2])
        sq = sb("sq", [128, 4 * Tc])
        xres_b = sb("xres", [128, 2 * Tc])
        nf = sb("nf", [128, Tc])
        h = sb("h", [128, 32 * Tc])
        hh = sb("hh", [128, 32 * Tc], BF16)
        hl = sb("hl", [128, 32 * Tc], BF16)
        row = sb("row", [1, 6 * Tc])
        nh = hh[:, 0:8 * Tc]
        nl = hl[:, 0:8 * Tc]
        nbig = h[:, 8 * Tc:16 * Tc]
        xtb = h[:, 24 * Tc:32 * Tc]
        wfq_t = s_wfq.rearrange("(c p) n -> c p n", p=128)
        wpq_t = s_wpq.rearrange("(c p) n -> c p n", p=128)
        for i in range(8):
            P.dma("y", wfq[:, i * 4096:(i + 1) * 4096], wfq_t[i])
        for i in range(32):
            P.dma("y", wpq[:, i * D:(i + 1) * D], wpq_t[i])
        fl_t = flnw.rearrange("(c p) n -> c p n", p=128)
        for i in range(32):
            P.dma("y", flw[:, i * 2:(i + 1) * 2], fl_t[i])
        wtok = ("dy", P.cnt["dy"])
        xin_t = xin.rearrange("(c p) n -> c p n", p=128)
        xout_t = xout.rearrange("(c p) n -> c p n", p=128)
        x_tiles = [xtb[:, i * Tc:(i + 1) * Tc] for i in range(ND)]
        st_tiles = [sq[:, 2 * Tc:3 * Tc], sq[:, 3 * Tc:4 * Tc]]

        def stage1a(ci, xfree_deps):
            c0 = ci * Tc
            dtoks = []
            for i in range(ND):
                dtoks.append(P.dma("g", x_tiles[i], xin_t[i][:, c0:c0 + Tc],
                                   deps=xfree_deps))
            for i in range(ND):
                P.wait("s", dtoks[i])
                P.wait("t", dtoks[i])
                P.wait("g", dtoks[i])
            ps_st, ps_sx, ps_sq, pi_st, sttok = emit_ln_stats(
                P, x_tiles, Tc, pool, st_tiles, K)
            P.wait("v", sttok)
            s_o = ln_rows(P, ps_sx, ps_sq, row, D, 1e-5, extra_ln=False)
            vtok = P.inc("v")
            pool.release(pi_st, vtok)
            bc, pi_bc, bctok = bcast_rows(P, pool, [s_o[0], s_o[1]], Tc, K,
                                          deps=[vtok])
            P.wait("v", bctok)
            for i in range(ND):
                P.op("v", lambda E, xi=x_tiles[i], b0=bc[0],
                     o=nbig[:, i * Tc:(i + 1) * Tc]:
                     E.tensor_tensor(o, xi, b0, ALU.mult), dr=(i == 0))
            for i in range(ND):
                P.op("v", lambda E, b1=bc[1], o=nbig[:, i * Tc:(i + 1) * Tc]:
                     E.tensor_tensor(o, o, b1, ALU.add), dr=(i == 0))
            napp = P.inc("v")
            pool.release(pi_bc, napp)
            return {"napp": napp, "sttok": sttok}

        def stage1b(st, wdep):
            for i in range(ND):
                P.op("s", lambda E, o=nh[:, i * Tc:(i + 1) * Tc],
                     i_=nbig[:, i * Tc:(i + 1) * Tc]:
                     E.activation(o, i_, AF.Copy),
                     deps=([st["napp"]] + ([wdep] if wdep else []))
                     if i == 0 else (), dr=(i == 0))
            sctok = P.inc("s")
            hilo_toks = []
            for i in range(ND):
                P.op("v", lambda E, o=nl[:, i * Tc:(i + 1) * Tc],
                     i_=nbig[:, i * Tc:(i + 1) * Tc],
                     hi=nh[:, i * Tc:(i + 1) * Tc]: E.tensor_tensor(
                    o, i_, hi, ALU.subtract),
                    deps=[sctok] if i == 0 else (), dr=(i == 0))
                hilo_toks.append(P.inc("v"))
            st["hilo"] = hilo_toks

        prev_store = [None] * 8
        prev_inner_rows = None
        xres_tok = [None, None]
        cur = stage1a(0, [])
        stage1b(cur, None)
        for ci in range(NCH):
            c0 = ci * Tc
            P.waits("t", [wtok] + cur["hilo"])
            h_tiles = [h[:, i * Tc:(i + 1) * Tc] for i in range(NH)]
            ps_sx = ps_ded[0:1, 0:Tc]
            ps_sq = ps_ded[0:1, 512:512 + Tc]
            sq_tiles = [sq[:, j * Tc:(j + 1) * Tc] for j in range(2)]
            pend = []

            def flush_stat():
                o0, ht, sqt, g0, gs0 = pend.pop(0)
                if o0 == 0 and prev_inner_rows is not None:
                    P.wait("t", prev_inner_rows)
                P.wait("t", g0)
                P.mm(ps_sx, K["ones_col"], ht, start=(o0 == 0), stop=(o0 == NH - 1))
                P.wait("t", gs0)
                P.mm(ps_sq, K["ones_col"], sqt, start=(o0 == 0), stop=(o0 == NH - 1))
                P._sq_tok["last"] = P.inc("t")

            for oc in range(NH):
                ps, pi, ptok = pool.acquire()
                if ptok:
                    P.wait("t", ptok)
                psl = ps[:, 0:Tc]
                for kc in range(ND):
                    lh = wfq[:, kc * 4096 + oc * 128: kc * 4096 + oc * 128 + 128]
                    P.mm(psl, lh, nh[:, kc * Tc:(kc + 1) * Tc], start=(kc == 0), stop=False)
                    P.mm(psl, lh, nl[:, kc * Tc:(kc + 1) * Tc], start=False,
                         stop=(kc == ND - 1))
                ttok = P.inc("t")
                P.op("s", lambda E, oc=oc, psl=psl: E.activation(
                    h_tiles[oc], psl, AF.Gelu), deps=[ttok])
                gtok = P.inc("s")
                pool.release(pi, gtok)
                tk = P._sq_tok.get("last")
                P.op("g", lambda E, o=sq_tiles[oc % 2], x=h_tiles[oc]:
                     E.tensor_tensor(o, x, x, ALU.mult),
                     deps=[gtok] + ([tk] if tk is not None else []), dr=True)
                gstok = P.inc("g")
                pend.append((oc, h_tiles[oc], sq_tiles[oc % 2], gtok, gstok))
                if len(pend) == 2:
                    flush_stat()
            while pend:
                flush_stat()
            sttok = P._sq_tok["last"]
            P.wait("v", sttok)
            s_o = ln_rows(P, ps_sx, ps_sq, row, 4096, 1e-5, extra_ln=False)
            vtok = P.inc("v")
            prev_inner_rows = vtok
            bc, pi_bc, bctok = bcast_rows(P, pool, [s_o[0], s_o[1]], Tc, K, deps=[vtok])
            P.wait("v", bctok)
            for i in range(NH):
                P.op("v", lambda E, o=h_tiles[i], b0=bc[0]: E.tensor_tensor(
                    o, o, b0, ALU.mult), dr=(i == 0))
            for i in range(NH):
                P.op("v", lambda E, o=h_tiles[i], b1=bc[1]: E.tensor_tensor(
                    o, o, b1, ALU.add), dr=(i == 0))
            for i in range(NH):
                P.op("v", lambda E, o=h_tiles[i], w_=flw[:, i * 2:i * 2 + 1],
                     b_=flw[:, i * 2 + 1:i * 2 + 2]: E.tensor_scalar(
                    o, o, w_, b_, ALU.mult, ALU.add), dr=(i == 0))
            happ = P.inc("v")
            for i in range(NH):
                P.op("s", lambda E, o=hh[:, i * Tc:(i + 1) * Tc], i_=h_tiles[i]:
                     E.activation(o, i_, AF.Copy),
                     deps=[happ] if i == 0 else (), dr=(i == 0))
            sctok2 = P.inc("s")
            h2_toks = []
            for i in range(NH):
                P.op("v", lambda E, o=hl[:, i * Tc:(i + 1) * Tc], i_=h_tiles[i],
                     hi=hh[:, i * Tc:(i + 1) * Tc]: E.tensor_tensor(
                    o, i_, hi, ALU.subtract),
                    deps=[sctok2] if i == 0 else (), dr=(i == 0))
                h2_toks.append(P.inc("v"))
            pool.release(pi_bc, h2_toks[-1])
            P.waits("t", h2_toks)
            nxt = None
            if ci + 1 < NCH:
                nxt = stage1a(ci + 1, [cur["napp"], cur["sttok"],
                                       h2_toks[-1], sctok2])
            xres = [xres_b[:, 0:Tc], xres_b[:, Tc:2 * Tc]]
            for oc in range(ND):
                ps, pi, ptok = pool.acquire()
                if ptok:
                    P.wait("t", ptok)
                psl = ps[:, 0:Tc]
                rtok = P.dma("y", xres[oc % 2], xin_t[oc][:, c0:c0 + Tc],
                             deps=[xres_tok[oc % 2]] if xres_tok[oc % 2] else [])
                for kc in range(NH):
                    lh = wpq[:, kc * D + oc * 128: kc * D + oc * 128 + 128]
                    P.mm(psl, lh, hh[:, kc * Tc:(kc + 1) * Tc], start=(kc == 0), stop=False)
                    P.mm(psl, lh, hl[:, kc * Tc:(kc + 1) * Tc], start=False,
                         stop=(kc == NH - 1))
                ttok = P.inc("t")
                P.op("v", lambda E, oc=oc, psl=psl, xr=xres[oc % 2]: E.tensor_tensor(
                    nf, psl, xr, ALU.add), deps=[ttok, rtok])
                vtok = P.inc("v")
                xres_tok[oc % 2] = vtok
                prev_store[oc] = P.dma("y", xout_t[oc][:, c0:c0 + Tc], nf, deps=[vtok])
                P.wait("v", prev_store[oc])
                pool.release(pi, vtok)
            wp_tok = ttok
            if nxt is not None:
                stage1b(nxt, wp_tok)
                cur = nxt
        P.finish_pass()




def emit_p1(P, nc, x0, s_xa, W, c_in):
    Tc = 256
    NCH = (T // Tc) if DBG["nch"] is None else DBG["nch"]
    ND = D // 128
    NH = 8
    P.begin()
    with ExitStack() as ar:
        def sb(name, shape, dtype=F32):
            return ar.enter_context(nc.sbuf_tensor("a" + name, list(shape), dtype)).ap()
        pool = PsumPool(P, ar, 4, name="app")
        K = load_consts(P, nc, ar, c_in,
                        ["ones_col", "ones_row", "ones128", "negtriu", "tri01i_x8",
                         "maskb_x8", "ident", "identx8", "ident2x8",
                         "tri_incl", "rev_strict"], tag="a")
        identx8_bf = sb("idbf", [128, 1024], BF16)
        P.op("v", lambda E: E.tensor_copy(identx8_bf, K["identx8"]))
        P.inc("v")
        wba = sb("wba", [128, 8 * 16])
        ln1 = sb("ln1", [128, 8 * 2])
        for i in range(8):
            P.dma("y", wba[:, i * 16:(i + 1) * 16],
                  W["wba"].rearrange("(c p) n -> c p n", p=128)[i])
            P.dma("y", ln1[:, i * 2:(i + 1) * 2],
                  W["ln1"].rearrange("(c p) n -> c p n", p=128)[i])
        wpan = sb("wpan", [128, 2 * 1024])
        wopan = sb("wopan", [128, 2 * 1024])
        xt = sb("xt", [128, ND * Tc])
        sq = sb("sq", [128, 2 * Tc])
        row = sb("row", [1, 8 * Tc])
        tmp8 = sb("tmp8", [128, 8])
        n0 = sb("n0", [128, ND * Tc])
        n1 = sb("n1", [128, ND * Tc])
        q_t = sb("q_t", [128, 2 * 1024])
        k_t = sb("k_t", [128, 2 * 1024])
        v_t = sb("v_t", [128, 2 * 1024])
        q_f = sb("q_f", [128, 2 * 1024])
        k_f = sb("k_f", [128, 2 * 1024])
        sq2 = sb("sq2", [128, 1024])
        rcol = sb("rcol", [128, 8])
        rcol2 = sb("rcol2", [128, 8])
        bl = sb("bl", [128, 2 * 16])
        gcols = sb("gcols", [128, 4 * 8])
        rep = sb("rep", [128, 1024])
        dst = sb("dst", [128, 1024])
        dinc = sb("dinc", [128, 1024])
        mt = sb("mt", [128, 1024])
        at = sb("at", [128, 1024])
        atf = sb("atf", [128, 1024])
        at_bf = sb("at_bf", [128, 1024], BF16)
        xbuf = sb("xbuf", [128, 1024], BF16)
        xtbuf = sb("xtbuf", [128, 1024], BF16)
        qbuf = sb("qbuf", [128, 1024], BF16)
        x5f = sb("x5f", [128, 1024])
        xt5f = sb("xt5f", [128, 1024])
        q6f = sb("q6f", [128, 1024])
        x6f = x5f
        xt6f = xt5f
        xt7 = xt5f
        rwg = sb("rwg", [128, 1024])
        kpr = sb("kpr", [128, 1024])
        wfn = rep
        ub = dinc
        o1b = dst
        otm = mt
        s0 = [sb("s0a", [128, 1024]), sb("s0b", [128, 1024])]
        o_f = sb("o_f", [128, 8 * 256])
        xa_o = sb("xa_o", [128, Tc])
        P.op("g", lambda E: E.memset(s0[0], 0.0))
        memtok = P.inc("g")
        P._ppan = {}
        P._wopan = {}
        P._ofdone = None
        P._stok = memtok
        P._wo_ttok = None
        P._o1read = None

        x0_t = x0.rearrange("(c p) n -> c p n", p=128)
        xa_t = s_xa.rearrange("(c p) n -> c p n", p=128)
        wsrcs = {"q": W["wq"].rearrange("(c p) n -> c p n", p=128),
                 "k": W["wk"].rearrange("(c p) n -> c p n", p=128),
                 "v": W["wv"].rearrange("(c p) n -> c p n", p=128)}
        wo_t = W["wo"].rearrange("(c p) n -> c p n", p=128)
        wtok0 = ("dy", P.cnt["dy"])
        for e in ("t", "s", "v"):
            P.wait(e, wtok0)

        prev_xa_store = [None] * ND
        prev_vtok = None
        for ci in range(NCH):
            c0 = ci * Tc
            x_tiles = [xt[:, i * Tc:(i + 1) * Tc] for i in range(ND)]
            if prev_vtok is not None:
                for e in ("t", "v", "s"):
                    P.wait(e, prev_vtok)
                    P.wait(e, P._wo_ttok)
            dt = []
            for i in range(ND):
                dt.append(P.dma("y", x_tiles[i], x0_t[i][:, c0:c0 + Tc],
                                deps=[prev_vtok] if prev_vtok else []))
            for i in range(ND):
                P.wait("s", dt[i]); P.wait("t", dt[i]); P.wait("v", dt[i])
                P.wait("g", dt[i])
            sq_tiles = [sq[:, 0:Tc], sq[:, Tc:2 * Tc]]
            ps_st, ps_sx, ps_sq, pi_st, sttok = emit_ln_stats(
                P, x_tiles, Tc, pool, sq_tiles, K)
            P.wait("v", sttok)
            (s0r, o0r), (s1r, o1r) = ln_rows(P, ps_sx, ps_sq, row, D, 1e-5,
                                             extra_ln=True)
            vtok = P.inc("v")
            pool.release(pi_st, vtok)
            bc, pi_bc, bctok = bcast_rows(P, pool, [s0r, o0r, s1r, o1r], Tc, K,
                                          deps=[vtok])
            P.wait("v", bctok)
            for i in range(ND):
                P.op("v", lambda E, xi=x_tiles[i], b=bc[0],
                     o=n0[:, i * Tc:(i + 1) * Tc]: E.tensor_tensor(o, xi, b, ALU.mult),
                     dr=(i == 0))
            for i in range(ND):
                P.op("v", lambda E, o=n0[:, i * Tc:(i + 1) * Tc], b=bc[1]:
                     E.tensor_tensor(o, o, b, ALU.add), dr=(i == 0))
            for i in range(ND):
                P.op("v", lambda E, xi=x_tiles[i], b=bc[2],
                     o=n1[:, i * Tc:(i + 1) * Tc]: E.tensor_tensor(o, xi, b, ALU.mult),
                     dr=(i == 0))
            for i in range(ND):
                P.op("v", lambda E, o=n1[:, i * Tc:(i + 1) * Tc], b=bc[3]:
                     E.tensor_tensor(o, o, b, ALU.add), dr=(i == 0))
            ntoks = []
            for i in range(ND):
                P.op("v", lambda E, o=n1[:, i * Tc:(i + 1) * Tc],
                     w_=ln1[:, i * 2:i * 2 + 1], b_=ln1[:, i * 2 + 1:i * 2 + 2]:
                     E.tensor_scalar(o, o, w_, b_, ALU.mult, ALU.add), dr=(i == 0))
                ntoks.append(P.inc("v"))
            pool.release(pi_bc, ntoks[-1])
            P.waits("t", ntoks)
            for ti, nm in enumerate(["q", "k", "v"]):
                dstt = {"q": q_t, "k": k_t, "v": v_t}[nm]
                pr = [pool.acquire(), pool.acquire()]
                for (_, _, ptk) in pr:
                    if ptk:
                        P.wait("t", ptk)
                for kc in range(ND):
                    slot = wpan[:, (kc % 2) * 1024:(kc % 2) * 1024 + 1024]
                    ptok = P.dma("y", slot, wsrcs[nm][kc],
                                 deps=[P._ppan[kc % 2]] if P._ppan.get(kc % 2) else [])
                    P.wait("t", ptok)
                    for tg in range(2):
                        for hf in range(2):
                            P.mm(pr[tg][0][:, hf * 512:(hf + 1) * 512],
                                 n1[:, kc * Tc + tg * 128: kc * Tc + tg * 128 + 128],
                                 slot[:, hf * 512:(hf + 1) * 512],
                                 start=(kc == 0), stop=(kc == ND - 1))
                    P._ppan[kc % 2] = P.inc("t")
                ttok = P.inc("t")
                for tg in range(2):
                    P.op("s", lambda E, o=dstt[:, tg * 1024:(tg + 1) * 1024],
                         p=pr[tg][0]: E.activation(o, p, AF.Silu), deps=[ttok])
                    stok = P.inc("s")
                    pool.release(pr[tg][1], stok)
                P._silu = stok
            for tg in range(2):
                psb, pib, pbt = pool.acquire()
                if pbt:
                    P.wait("t", pbt)
                for kc in range(ND):
                    P.mm(psb[:, 0:16],
                         n1[:, kc * Tc + tg * 128: kc * Tc + tg * 128 + 128],
                         wba[:, kc * 16:(kc + 1) * 16],
                         start=(kc == 0), stop=(kc == ND - 1))
                ttok = P.inc("t")
                P.op("s", lambda E, o=bl[:, tg * 16:tg * 16 + 8], p=psb[0:128, 0:8]:
                     E.activation(o, p, AF.Sigmoid), deps=[ttok])
                P.op("s", lambda E, p=psb[0:128, 8:16]:
                     E.activation(tmp8, p, AF.Sigmoid))
                P.op("s", lambda E, o=bl[:, tg * 16 + 8:tg * 16 + 16]:
                     E.activation(o, tmp8, AF.Ln))
                stok = P.inc("s")
                pool.release(pib, stok)
            bltok = stok
            l2prev = None
            for (tt, ff) in [(q_t, q_f), (k_t, k_f)]:
                for tg in range(2):
                    tile = tt[:, tg * 1024:(tg + 1) * 1024]
                    P.op("g", lambda E, i_=tile: E.tensor_tensor(
                        sq2, i_, i_, ALU.mult),
                        deps=[P._silu] + ([l2prev] if l2prev else []), dr=True)
                    stok = P.inc("g")
                    P.op("v", lambda E: E.reduce_sum(
                        rcol, sq2.rearrange("p (h k) -> p h k", h=8),
                        axis=mybir.AxisListType.X), deps=[stok])
                    P.op("v", lambda E: E.tensor_scalar_add(rcol, rcol, 1e-6))
                    P.op("v", lambda E: E.reciprocal(rcol, rcol))
                    vtok = P.inc("v")
                    l2prev = vtok
                    P.op("s", lambda E: E.activation(rcol2, rcol, AF.Sqrt),
                         deps=[vtok])
                    stok = P.inc("s")
                    P.op("v", lambda E, o=tile: E.tensor_tensor(
                        o, o, rcol2.unsqueeze(2).broadcast_to(
                            (128, 8, 128)), ALU.mult), deps=[stok])
                    ltok = P.inc("v")
                    pst, pit, ptt = pool.acquire()
                    if ptt:
                        P.wait("t", ptt)
                    P.wait("t", ltok)
                    for h in range(NH):
                        P.tr(pst[:, h * 128:(h + 1) * 128],
                             tile[:, h * 128:(h + 1) * 128], K["ident"])
                    ttok = P.inc("t")
                    P.op("v", lambda E, o=ff[:, tg * 1024:(tg + 1) * 1024], p=pst:
                         E.tensor_copy(o, p), deps=[ttok])
                    vtok = P.inc("v")
                    pool.release(pit, vtok)
            P.waits("t", [vtok, bltok])
            for tg in range(2):
                cc = ci * 2 + tg
                _gdn_chunk(P, nc, pool, K, identx8_bf, cc, tg,
                           q_t, k_t, v_t, q_f, k_f, bl, gcols, rep, dst, dinc,
                           mt, at, atf, at_bf, xbuf, xtbuf, qbuf, x5f, xt5f,
                           q6f, x6f, xt6f, xt7, rwg, kpr, wfn, ub, o1b, otm,
                           s0, o_f)
            wops = [pool.acquire() for _ in range(4)]
            for (_, _, ptk) in wops:
                if ptk:
                    P.wait("t", ptk)
            P.waits("t", [P._ofdone])

            def wo_region(oc):
                ps = wops[oc // 2][0]
                return ps[:, (oc % 2) * 512:(oc % 2) * 512 + 256]
            for kc in range(ND):
                slot = wopan[:, (kc % 2) * 1024:(kc % 2) * 1024 + 1024]
                wtok = P.dma("y", slot, wo_t[kc],
                             deps=[P._wopan[kc % 2]] if P._wopan.get(kc % 2) else [])
                P.wait("t", wtok)
                for oc in range(ND):
                    P.mm(wo_region(oc), slot[:, oc * 128:(oc + 1) * 128],
                         o_f[:, kc * 256:(kc + 1) * 256],
                         start=(kc == 0), stop=(kc == ND - 1))
                P._wopan[kc % 2] = P.inc("t")
            P._wo_ttok = P.inc("t")
            for oc in range(ND):
                deps = [P._wo_ttok]
                if prev_xa_store[oc]:
                    deps.append(prev_xa_store[oc])
                P.op("v", lambda E, p=wo_region(oc), nn=n0[:, oc * Tc:(oc + 1) * Tc]:
                     E.tensor_tensor(xa_o, p, nn, ALU.add), deps=deps)
                vtok = P.inc("v")
                prev_xa_store[oc] = P.dma("y", xa_t[oc][:, c0:c0 + Tc], xa_o,
                                          deps=[vtok])
                P.wait("v", prev_xa_store[oc])
            vtok = P.inc("v")
            for (_, pi, _) in wops:
                pool.release(pi, vtok)
            prev_vtok = vtok
        P.finish_pass()


def _gdn_chunk(P, nc, pool, K, identx8_bf, cc, tg,
               q_t, k_t, v_t, q_f, k_f, bl, gcols, rep, dst, dinc,
               mt, at, atf, at_bf, xbuf, xtbuf, qbuf, x5f, xt5f, q6f,
               x6f, xt6f, xt7, rwg, kpr, wfn, ub, o1b, otm, s0, o_f):
    NH = 8
    la = bl[:, tg * 16 + 8:tg * 16 + 16]
    beta = bl[:, tg * 16:tg * 16 + 8]
    ktT = k_t[:, tg * 1024:(tg + 1) * 1024]
    vtT = v_t[:, tg * 1024:(tg + 1) * 1024]
    qfT = q_f[:, tg * 1024:(tg + 1) * 1024]
    kfT = k_f[:, tg * 1024:(tg + 1) * 1024]
    Sin = s0[cc % 2]
    Sout = s0[(cc + 1) % 2]
    gcol = gcols[:, 0:8]
    ecol = gcols[:, 8:16]
    gC = gcols[:, 16:24]
    edb = gcols[:, 24:32]
    labc = la.unsqueeze(2).broadcast_to((128, 8, 128))

    if P._stok is not None:
        P.wait("t", P._stok)
        P.wait("v", P._stok)

    psc, pic, ptok = pool.acquire()
    if ptok:
        P.wait("t", ptok)
    P.mm(psc[:, 0:8], K["tri_incl"], la, start=True, stop=True)
    P.mm(psc[:, 512:520], K["rev_strict"], la, start=True, stop=True)
    ttok = P.inc("t")
    P.op("s", lambda E: E.activation(gcol, psc[0:128, 0:8], AF.Exp), deps=[ttok])
    P.op("s", lambda E: E.activation(ecol, psc[0:128, 512:520], AF.Exp))
    stok = P.inc("s")
    P.op("v", lambda E: E.tensor_tensor(gC, ecol, gcol, ALU.mult), deps=[stok])
    P.op("v", lambda E: E.tensor_tensor(edb, ecol, beta, ALU.mult))
    P.op("v", lambda E: E.tensor_tensor(
        rep, labc, K["tri01i_x8"].rearrange("p (h t) -> p h t", h=8), ALU.mult))
    gtok = P.inc("v")
    pool.release(pic, gtok)
    psd, pid, ptok = pool.acquire()
    if ptok:
        P.wait("t", ptok)
    P.waits("t", [gtok])
    for half in range(2):
        sl = slice(half * 512, (half + 1) * 512)
        P.mm(psd[:, sl], K["ones128"], rep[:, sl], start=True, stop=False)
        P.mm(psd[:, sl], K["negtriu"], labc[:, half * 4:(half + 1) * 4, :],
             start=False, stop=False)
        P.mm(psd[:, sl], K["ident"], K["maskb_x8"][:, sl], start=False, stop=True)
    ttok = P.inc("t")
    dstdeps = [ttok]
    if getattr(P, "_o1read", None) is not None:
        dstdeps.append(P._o1read)
    P.op("s", lambda E: E.activation(dst, psd, AF.Exp), deps=dstdeps)
    stok = P.inc("s")
    P.op("v", lambda E: E.tensor_tensor(dinc, dst, K["identx8"], ALU.add),
         deps=[stok])
    dtok = P.inc("v")
    pool.release(pid, dtok)
    pskk, pikk, ptok = pool.acquire()
    if ptok:
        P.wait("t", ptok)
    pskq, pikq, ptok2 = pool.acquire()
    if ptok2:
        P.wait("t", ptok2)
    for h in range(NH):
        hs = slice(h * 128, (h + 1) * 128)
        P.mm(pskk[:, hs], kfT[:, hs], kfT[:, hs], start=True, stop=True)
        P.mm(pskq[:, hs], kfT[:, hs], qfT[:, hs], start=True, stop=True)
    ttok = P.inc("t")
    P.wait("v", ttok)
    P.wait("v", dtok)
    for h in range(NH):
        hs = slice(h * 128, (h + 1) * 128)
        P.op("v", lambda E, hs=hs, h=h: E.scalar_tensor_tensor(
            mt[:, hs], pskk[:, hs], beta[:, h:h + 1], dst[:, hs],
            ALU.mult, ALU.mult), dr=(h == 0))
        P.op("v", lambda E, hs=hs, h=h: E.scalar_tensor_tensor(
            at[:, hs], pskq[:, hs], beta[:, h:h + 1], dinc[:, hs],
            ALU.mult, ALU.mult), dr=False)
    mtok = P.inc("v")
    pool.release(pikk, mtok)
    pool.release(pikq, mtok)
    P.op("v", lambda E: E.tensor_tensor(atf, mt, K["identx8"], ALU.add))
    P.op("v", lambda E: E.tensor_copy(at_bf, atf))
    atok = P.inc("v")
    P.waits("t", [atok])

    def batch_mm(psx, lhs, rhs):
        for h in range(NH):
            hs = slice(h * 128, (h + 1) * 128)
            P.mm(psx[:, hs], lhs[:, hs], rhs[:, hs], start=True, stop=True)
        return P.inc("t")

    def evac(dstb, psx, dep, stt=False, eng="v"):
        if stt:
            P.op(eng, lambda E: E.scalar_tensor_tensor(
                dstb, psx, -1.0, K["ident2x8"], ALU.mult, ALU.add), deps=[dep])
        elif eng == "s":
            P.op(eng, lambda E: E.activation(dstb, psx, AF.Copy), deps=[dep])
        else:
            P.op(eng, lambda E: E.tensor_copy(dstb, psx), deps=[dep])
        return P.inc(eng)

    def newton_iter(lhs_at, idq, xin, xtin, xout, xtout, qb, skip_x=False):
        psp, pip, pt = pool.acquire()
        if pt:
            P.wait("t", pt)
        t1 = batch_mm(psp, lhs_at, xin)
        qtok = evac(qb, psp, t1, stt=True)
        pool.release(pip, qtok)
        P.waits("t", [qtok])
        toks = []
        if not skip_x:
            psx, pix, pt = pool.acquire()
            if pt:
                P.wait("t", pt)
            t2 = batch_mm(psx, xtin, qb)
            vt = evac(xout, psx, t2, eng="s")
            pool.release(pix, vt)
            toks.append(vt)
        psxt, pixt, pt = pool.acquire()
        if pt:
            P.wait("t", pt)
        t3 = batch_mm(psxt, qb, xtin)
        vt2 = evac(xtout, psxt, t3)
        pool.release(pixt, vt2)
        toks.append(vt2)
        P.waits("t", toks)
        P._nt = toks

    psp, pip, pt = pool.acquire()
    if pt:
        P.wait("t", pt)
    t1 = batch_mm(psp, at_bf, identx8_bf)
    vtok = evac(xbuf, psp, t1, stt=True)
    pool.release(pip, vtok)
    P.waits("t", [vtok])
    psxt, pixt, pt = pool.acquire()
    if pt:
        P.wait("t", pt)
    t2 = batch_mm(psxt, xbuf, identx8_bf)
    vtok = evac(xtbuf, psxt, t2)
    pool.release(pixt, vtok)
    P.waits("t", [vtok])
    for _ in range(4):
        newton_iter(at_bf, identx8_bf, xbuf, xtbuf, xbuf, xtbuf, qbuf)
    P.op("v", lambda E: E.tensor_copy(x5f, xbuf), deps=P._nt)
    vtok = P.inc("v")
    pstr, pitr, pt = pool.acquire()
    if pt:
        P.wait("t", pt)
    P.waits("t", [vtok])
    for h in range(NH):
        hs = slice(h * 128, (h + 1) * 128)
        P.tr(pstr[:, hs], x5f[:, hs], K["ident"])
    t3 = P.inc("t")
    vtok = evac(xt5f, pstr, t3)
    pool.release(pitr, vtok)
    P.waits("t", [vtok])
    newton_iter(atf, None, x5f, xt5f, x6f, xt6f, q6f)
    newton_iter(atf, None, x6f, xt6f, None, xt7, q6f, skip_x=True)
    P.op("v", lambda E: E.tensor_tensor(
        rwg, ktT, gcol.unsqueeze(2).broadcast_to((128, 8, 128)),
        ALU.mult))
    P.op("v", lambda E: E.tensor_tensor(
        kpr, ktT, edb.unsqueeze(2).broadcast_to((128, 8, 128)),
        ALU.mult))
    sctok = P.inc("v")
    P.waits("t", [sctok])
    psw, piw, pt = pool.acquire()
    if pt:
        P.wait("t", pt)
    t4 = batch_mm(psw, rwg, xt7)
    P.op("v", lambda E, p=psw: E.tensor_scalar_mul(wfn, p, -1.0), deps=[t4])
    wtok = P.inc("v")
    pool.release(piw, wtok)
    P.waits("t", [wtok])
    psu, piu, pt = pool.acquire()
    if pt:
        P.wait("t", pt)
    for h in range(NH):
        hs = slice(h * 128, (h + 1) * 128)
        P.mm(psu[:, hs], xt7[:, hs], vtT[:, hs], start=True, stop=False)
        P.mm(psu[:, hs], wfn[:, hs], Sin[:, hs], start=False, stop=True)
    t5 = P.inc("t")
    utok = evac(ub, psu, t5)
    pool.release(piu, utok)
    P.waits("t", [utok])
    pso1, pio1, pt = pool.acquire()
    if pt:
        P.wait("t", pt)
    t6 = batch_mm(pso1, qfT, Sin)
    P.op("v", lambda E, p=pso1: E.tensor_tensor(
        o1b, p, gcol.unsqueeze(2).broadcast_to((128, 8, 128)),
        ALU.mult), deps=[t6])
    o1tok = P.inc("v")
    pool.release(pio1, o1tok)
    pso2, pio2, pt = pool.acquire()
    if pt:
        P.wait("t", pt)
    pss, pis, pt2 = pool.acquire()
    if pt2:
        P.wait("t", pt2)
    for h in range(NH):
        hs = slice(h * 128, (h + 1) * 128)
        P.mm(pso2[:, hs], at[:, hs], ub[:, hs], start=True, stop=True)
        P.mm(pss[:, hs], kpr[:, hs], ub[:, hs], start=True, stop=True)
    t7 = P.inc("t")
    P.op("v", lambda E, p=pso2: E.tensor_tensor(otm, p, o1b, ALU.add),
         deps=[t7, o1tok])
    otok = P.inc("v")
    P._o1read = otok
    pool.release(pio2, otok)
    for h in range(NH):
        hs = slice(h * 128, (h + 1) * 128)
        P.op("v", lambda E, hs=hs, h=h, p=pss: E.scalar_tensor_tensor(
            Sout[:, hs], Sin[:, hs], gC[:, h:h + 1], p[:, hs],
            ALU.mult, ALU.add), deps=[t7] if h == 0 else (), dr=(h == 0))
    stok_v = P.inc("v")
    pool.release(pis, stok_v)
    P._stok = stok_v
    psof, piof, pt = pool.acquire()
    if pt:
        P.wait("t", pt)
    P.waits("t", [otok])
    for h in range(NH):
        hs = slice(h * 128, (h + 1) * 128)
        P.tr(psof[:, hs], otm[:, hs], K["ident"])
    t8 = P.inc("t")
    deps = [t8]
    if P._wo_ttok is not None:
        deps.append(P._wo_ttok)
    P.op("v", lambda E: E.tensor_copy(
        o_f.rearrange("p (h t) -> p h t", h=8)[:, :, tg * 128:(tg + 1) * 128],
        psof.rearrange("p (h t) -> p h t", h=8)), deps=deps)
    oftok = P.inc("v")
    pool.release(piof, oftok)
    P._ofdone = oftok


def emit_p15(P, nc, x0, s_xa, s_x1, W, c_in):
    Tc = 512
    NCH = (T // Tc) if DBG["nch"] is None else DBG["nch"]
    ND = D // 128
    NH = 4096 // 128
    P.begin()
    with ExitStack() as ar:
        def sb(name, shape, dtype=F32):
            return ar.enter_context(nc.sbuf_tensor("f1" + name, list(shape), dtype)).ap()
        pool = PsumPool(P, ar, 4, name="f1pp")
        K = load_consts(P, nc, ar, c_in, ["ones_col", "ones_row"], tag="f1")
        ln2 = sb("ln2", [128, 8 * 2])
        b1c = sb("b1c", [128, 32])
        b2c = sb("b2c", [128, 8])
        for i in range(8):
            P.dma("y", ln2[:, i * 2:(i + 1) * 2],
                  W["ln2"].rearrange("(c p) n -> c p n", p=128)[i])
            P.dma("y", b2c[:, i:i + 1],
                  W["b2"].rearrange("(c p) n -> c p n", p=128)[i])
        for i in range(32):
            P.dma("y", b1c[:, i:i + 1],
                  W["b1"].rearrange("(c p) n -> c p n", p=128)[i])
        xat = sb("xat", [128, ND * Tc])
        x0r = sb("x0r", [128, 2 * Tc])
        xar = sb("xar", [128, 2 * Tc])
        n2 = sb("n2", [128, ND * Tc])
        h = sb("h", [128, NH * Tc])
        sq = sb("sq", [128, 2 * Tc])
        row = sb("row", [1, 8 * Tc])
        nf = sb("nf", [128, Tc])
        w1p = sb("w1p", [128, 2 * 4096])
        w2p = sb("w2p", [128, 2 * 4096])
        wtok0 = ("dy", P.cnt["dy"])
        for e in ("t", "v", "s"):
            P.wait(e, wtok0)
        xa_t = s_xa.rearrange("(c p) n -> c p n", p=128)
        x0_t = x0.rearrange("(c p) n -> c p n", p=128)
        x1_t = s_x1.rearrange("(c p) n -> c p n", p=128)
        w1_t = W["w1"].rearrange("(c p) n -> c p n", p=128)
        w2_t = W["w2"].rearrange("(c p) n -> c p n", p=128)
        w2s_t = W["w2s"].rearrange("(c p) n -> c p n", p=128)
        P._p15pan = {}
        prev_store = [None] * ND
        prev_vtok = None
        x0r_tok = [None, None]

        def p15_stage1(ci, free_dep, w1_dep):
            c0 = ci * Tc
            xa_tiles = [xat[:, i * Tc:(i + 1) * Tc] for i in range(ND)]
            dt = []
            for i in range(ND):
                dt.append(P.dma("g", xa_tiles[i], xa_t[i][:, c0:c0 + Tc],
                                deps=[free_dep] if free_dep else []))
            for tk in dt:
                P.wait("s", tk); P.wait("t", tk); P.wait("v", tk)
                P.wait("g", tk)
            sq_tiles = [sq[:, 0:Tc], sq[:, Tc:2 * Tc]]
            ps_st, ps_sx, ps_sq, pi_st, sttok = emit_ln_stats(
                P, xa_tiles, Tc, pool, sq_tiles, K)
            P.wait("v", sttok)
            s_o = ln_rows(P, ps_sx, ps_sq, row, D, 1e-5, extra_ln=False)
            vtok = P.inc("v")
            pool.release(pi_st, vtok)
            bc, pi_bc, bctok = bcast_rows(P, pool, [s_o[0], s_o[1]], Tc, K,
                                          deps=[vtok])
            P.wait("v", bctok)
            for i in range(ND):
                P.op("v", lambda E, xi=xa_tiles[i], b=bc[0],
                     o=n2[:, i * Tc:(i + 1) * Tc]: E.tensor_tensor(o, xi, b, ALU.mult),
                     deps=[w1_dep] if (w1_dep and i == 0) else (), dr=(i == 0))
            for i in range(ND):
                P.op("v", lambda E, o=n2[:, i * Tc:(i + 1) * Tc], b=bc[1]:
                     E.tensor_tensor(o, o, b, ALU.add), dr=(i == 0))
            ntoks = []
            for i in range(ND):
                P.op("v", lambda E, o=n2[:, i * Tc:(i + 1) * Tc],
                     w_=ln2[:, i * 2:i * 2 + 1], b_=ln2[:, i * 2 + 1:i * 2 + 2]:
                     E.tensor_scalar(o, o, w_, b_, ALU.mult, ALU.add), dr=(i == 0))
                ntoks.append(P.inc("v"))
            pool.release(pi_bc, ntoks[-1])
            return ntoks

        cur = p15_stage1(0, None, None)
        for ci in range(NCH):
            c0 = ci * Tc
            P.waits("t", cur)
            for og in range(NH // 4):
                slot = w1p[:, (og % 2) * 4096:(og % 2) * 4096 + 4096]
                ptoks = []
                for kc in range(ND):
                    dep = [P._p15pan[og % 2]] if P._p15pan.get(og % 2) else []
                    ptoks.append(P.dma("y", slot[:, kc * 512:(kc + 1) * 512],
                                       w1_t[kc][:, og * 512:(og + 1) * 512],
                                       deps=dep))
                P.wait("t", ptoks[-1])
                for j in range(4):
                    oc = og * 4 + j
                    ps, pi, ptk = pool.acquire()
                    if ptk:
                        P.wait("t", ptk)
                    psl = ps[:, 0:Tc]
                    for kc in range(ND):
                        P.mm(psl, slot[:, kc * 512 + j * 128: kc * 512 + j * 128 + 128],
                             n2[:, kc * Tc:(kc + 1) * Tc],
                             start=(kc == 0), stop=(kc == ND - 1))
                    mmt = P.inc("t")
                    if j == 3:
                        P._p15pan[og % 2] = mmt
                    P.op("s", lambda E, o=h[:, oc * Tc:(oc + 1) * Tc], p=psl,
                         b_=b1c[:, oc:oc + 1]: E.activation(o, p, AF.Gelu, bias=b_),
                         deps=[mmt])
                    gtok = P.inc("s")
                    pool.release(pi, gtok)
            w1tok = P._p15pan[(NH // 4 - 1) % 2]
            cur = p15_stage1(ci + 1, cur[-1], w1tok) if ci + 1 < NCH else None
            P.wait("t", gtok)
            for oc in range(ND):
                slot = w2p[:, (oc % 2) * 4096:(oc % 2) * 4096 + 4096]
                dep = [P._p15pan[2 + oc % 2]] if P._p15pan.get(2 + oc % 2) else []
                ptok = P.dma("y", slot, w2s_t[oc], deps=dep)
                ps, pi, ptk = pool.acquire()
                if ptk:
                    P.wait("t", ptk)
                P.wait("t", ptok)
                psl = ps[:, 0:Tc]
                for kc in range(NH):
                    P.mm(psl, slot[:, kc * 128:(kc + 1) * 128],
                         h[:, kc * Tc:(kc + 1) * Tc],
                         start=(kc == 0), stop=(kc == NH - 1))
                mmt = P.inc("t")
                P._p15pan[2 + oc % 2] = mmt
                x0tok = P.dma("y", x0r[:, (oc % 2) * Tc:(oc % 2) * Tc + Tc],
                              x0_t[oc][:, c0:c0 + Tc],
                              deps=[x0r_tok[oc % 2]] if x0r_tok[oc % 2] else [])
                xatok = P.dma("y", xar[:, (oc % 2) * Tc:(oc % 2) * Tc + Tc],
                              xa_t[oc][:, c0:c0 + Tc],
                              deps=[x0r_tok[oc % 2]] if x0r_tok[oc % 2] else [])
                deps = [mmt, x0tok, xatok]
                if prev_store[oc]:
                    deps.append(prev_store[oc])
                P.op("v", lambda E, p=psl,
                     xa_=xar[:, (oc % 2) * Tc:(oc % 2) * Tc + Tc],
                     b_=b2c[:, oc:oc + 1]:
                     E.scalar_tensor_tensor(nf, p, b_, xa_, ALU.add, ALU.add),
                     deps=deps)
                P.op("v", lambda E, x0_=x0r[:, (oc % 2) * Tc:(oc % 2) * Tc + Tc]:
                     E.tensor_tensor(nf, nf, x0_, ALU.add))
                vtok = P.inc("v")
                x0r_tok[oc % 2] = vtok
                prev_store[oc] = P.dma("y", x1_t[oc][:, c0:c0 + Tc], nf,
                                       deps=[vtok])
                P.wait("v", prev_store[oc])
                pool.release(pi, vtok)
            prev_vtok = vtok
        P.finish_pass()


def emit_p3a(P, nc, s_x2, s_hs, s_v, s_delta, W, c_in):
    Tc = 256
    NCH = (T // Tc) if DBG["nch"] is None else DBG["nch"]
    ND = D // 128
    NH = 4
    P.begin()
    with ExitStack() as ar:
        def sb(name, shape, dtype=F32):
            return ar.enter_context(nc.sbuf_tensor("b" + name, list(shape), dtype)).ap()
        pool = PsumPool(P, ar, 4, name="bpp")
        K = load_consts(P, nc, ar, c_in,
                        ["ones_col", "ones_row", "ident", "mask01s_x4",
                         "mask01i_x4", "identx4", "ident2x4"], tag="b")
        identx4_bf = sb("id4bf", [128, 512], BF16)
        P.op("v", lambda E: E.tensor_copy(identx4_bf, K["identx4"]))
        P.inc("v")
        wq = sb("wq", [128, 8 * 1024])
        wk = sb("wk", [128, 8 * 1024])
        wv = sb("wv", [128, 8 * 1024])
        wb4 = sb("wb4", [128, 8 * 4])
        cw = sb("cw", [128, 3 * 8 * 4])
        for i in range(8):
            P.dma("y", wq[:, i * 1024:(i + 1) * 1024],
                  W["pqT"].rearrange("(c p) n -> c p n", p=128)[i])
            P.dma("y", wk[:, i * 1024:(i + 1) * 1024],
                  W["pkT"].rearrange("(c p) n -> c p n", p=128)[i])
            P.dma("y", wv[:, i * 1024:(i + 1) * 1024],
                  W["pvT"].rearrange("(c p) n -> c p n", p=128)[i])
            P.dma("y", wb4[:, i * 4:(i + 1) * 4],
                  W["pbT"].rearrange("(c p) n -> c p n", p=128)[i])
            for ti, wn in enumerate(["qcw", "kcw", "vcw"]):
                P.dma("y", cw[:, ti * 32 + i * 4: ti * 32 + (i + 1) * 4],
                      W[wn].rearrange("(c p) n -> c p n", p=128)[i])
        CB = 3 + Tc
        cbuf = sb("cb", [128, 8 * CB])
        tails = sb("tails", [128, 3 * 8 * 3])
        P.op("g", lambda E: E.memset(tails, 0.0))
        cbtok = P.inc("g")
        xt = sb("xt", [128, 8 * Tc])
        hs = sb("hs", [128, 8 * Tc])
        k_tm = hs[:, 0:1024]
        v_tm = hs[:, 1024:2048]
        sq = sb("sq", [128, 2 * Tc])
        sq2 = sb("sq2", [128, 1024])
        row = sb("row", [1, 8 * Tc])
        acc = sb("acc", [128, 2 * Tc])
        acc2 = sq
        q_fm = sb("q_fm", [128, 8 * Tc])
        k_fm = sb("k_fm", [128, 8 * Tc])
        v_fm = sb("v_fm", [128, 8 * Tc])
        bl = sb("bl", [128, 2 * 4])

        mt = sb("mt", [128, 512])
        at = sb("at", [128, 512])
        atf = sb("atf", [128, 512])
        at_bf = sb("at_bf", [128, 512], BF16)
        xbuf = sb("xbuf", [128, 512], BF16)
        xtbuf = sb("xtbuf", [128, 512], BF16)
        qbuf = sb("qbuf", [128, 512], BF16)
        x5f = sb("x5f", [128, 512])
        xt5f = sb("xt5f", [128, 512])
        q6f = sb("q6f", [128, 512])
        x6f, xt6f, xt7 = x5f, xt5f, xt5f
        wfn = sb("wfn", [128, 1024])
        kpr = sq2
        ub = xt[:, 0:1024]
        otm = xt[:, 1024:2048]
        s0 = [sb("s0a", [128, 2048]), sb("s0b", [128, 2048])]
        delta_fm = cbuf[:, 0:8 * Tc]
        P.op("g", lambda E: E.memset(s0[0], 0.0))
        memtok = P.inc("g")
        P._stok3 = memtok
        P._dstore = [None] * 8
        P._vstore = [None] * 8
        P._hstore = [None] * 8

        x2_t = s_x2.rearrange("(c p) n -> c p n", p=128)
        hs_t = s_hs.rearrange("(c p) n -> c p n", p=128)
        v_t_d = s_v.rearrange("(c p) n -> c p n", p=128)
        d_t = s_delta.rearrange("(c p) n -> c p n", p=128)
        wtok0 = ("dy", P.cnt["dy"])
        for e in ("t", "v", "s", "g"):
            P.wait(e, wtok0)
        P.wait("v", cbtok)
        P.wait("s", cbtok)

        prev_vtok = None
        for ci in range(NCH):
            c0 = ci * Tc
            x_tiles = [xt[:, i * Tc:(i + 1) * Tc] for i in range(ND)]
            hs_tiles = [hs[:, i * Tc:(i + 1) * Tc] for i in range(ND)]
            if prev_vtok is not None:
                spill = ("dy", P.cnt["dy"])
                for e in ("t", "v", "s", "g"):
                    P.wait(e, prev_vtok)
                    P.wait(e, spill)
            dt = []
            for i in range(ND):
                dt.append(P.dma("y", x_tiles[i], x2_t[i][:, c0:c0 + Tc],
                                deps=[prev_vtok] if prev_vtok else []))
            for tk in dt:
                P.wait("s", tk); P.wait("t", tk); P.wait("v", tk)
                P.wait("g", tk)
            sq_tiles = [sq[:, 0:Tc], sq[:, Tc:2 * Tc]]
            ps_st, ps_sx, ps_sq, pi_st, sttok = emit_ln_stats(
                P, x_tiles, Tc, pool, sq_tiles, K)
            P.wait("v", sttok)
            s_o = ln_rows(P, ps_sx, ps_sq, row, D, 1e-5, extra_ln=False)
            vtok = P.inc("v")
            pool.release(pi_st, vtok)
            bc, pi_bc, bctok = bcast_rows(P, pool, [s_o[0], s_o[1]], Tc, K,
                                          deps=[vtok])
            ntoks = []
            P.wait("v", bctok)
            for i in range(ND):
                P.op("v", lambda E, xi=x_tiles[i], b=bc[0], o=hs_tiles[i]:
                     E.tensor_tensor(o, xi, b, ALU.mult), dr=(i == 0))
            for i in range(ND):
                P.op("v", lambda E, o=hs_tiles[i], b=bc[1]:
                     E.tensor_tensor(o, o, b, ALU.add), dr=(i == 0))
                tk = P.inc("v")
                ntoks.append(tk)
                P._hstore[i] = P.dma("y", hs_t[i][:, c0:c0 + Tc], hs_tiles[i],
                                     deps=[tk])
            pool.release(pi_bc, ntoks[-1])
            P.waits("t", ntoks)
            P._hs_spill = ("dy", P.cnt["dy"])
            for ti, (wsb, dstf) in enumerate([(wq, q_fm), (wk, k_fm), (wv, v_fm)]):
                ttoks = []
                for oc in range(ND):
                    cb = cbuf[:, oc * CB:(oc + 1) * CB]
                    tl = tails[:, (ti * 8 + oc) * 3:(ti * 8 + oc) * 3 + 3]
                    dep = [P._convdone] if getattr(P, "_convdone", None) else []
                    P.op("v", lambda E, cb=cb, tl=tl: E.tensor_copy(cb[:, 0:3], tl),
                         deps=dep)
                    ttoks.append(P.inc("v"))
                for oc in range(ND):
                    cb = cbuf[:, oc * CB:(oc + 1) * CB]
                    ps, pi, ptk = pool.acquire()
                    if ptk:
                        P.wait("t", ptk)
                    psl = ps[:, 0:Tc]
                    for kc in range(ND):
                        P.mm(psl, wsb[:, kc * 1024 + oc * 128: kc * 1024 + oc * 128 + 128],
                             hs_tiles[kc], start=(kc == 0), stop=(kc == ND - 1))
                    mtk = P.inc("t")
                    P.op("s", lambda E, o=cb[:, 3:3 + Tc], p=psl:
                         E.activation(o, p, AF.Copy), deps=[mtk, ttoks[oc]])
                    stok = P.inc("s")
                    pool.release(pi, stok)
                    tl = tails[:, (ti * 8 + oc) * 3:(ti * 8 + oc) * 3 + 3]
                    P.op("v", lambda E, cb=cb, tl=tl: E.tensor_copy(
                        tl, cb[:, Tc:Tc + 3]), deps=[stok])
                    eng = "v" if (oc % 4 != 3) else "g"
                    a0 = acc[:, 0:Tc] if eng == "v" else acc2[:, 0:Tc]
                    a1 = acc[:, Tc:2 * Tc] if eng == "v" else acc2[:, Tc:2 * Tc]
                    wc0 = cw[:, ti * 32 + oc * 4: ti * 32 + oc * 4 + 1]
                    if eng == "v":
                        P.op(eng, lambda E, cb=cb, w_=wc0, o=a0: E.tensor_scalar(
                            o, cb[:, 0:Tc], w_, None, ALU.mult), deps=[stok])
                        for j in range(1, 4):
                            src_sl = (a0, a1) if j % 2 == 1 else (a1, a0)
                            P.op(eng, lambda E, cb=cb, j=j,
                                 w_=cw[:, ti * 32 + oc * 4 + j: ti * 32 + oc * 4 + j + 1],
                                 i0=src_sl[0], o=src_sl[1]: E.scalar_tensor_tensor(
                                 o, cb[:, j:j + Tc], w_, i0, ALU.mult, ALU.add))
                    else:
                        P.op(eng, lambda E, cb=cb, w_=wc0, o=a1: E.tensor_tensor(
                            o, cb[:, 0:Tc], w_.broadcast_to((128, Tc)), ALU.mult),
                            deps=[stok])
                        for j in range(1, 4):
                            P.op(eng, lambda E, cb=cb, j=j, o=a0,
                                 w_=cw[:, ti * 32 + oc * 4 + j: ti * 32 + oc * 4 + j + 1]:
                                 E.tensor_tensor(o, cb[:, j:j + Tc],
                                                 w_.broadcast_to((128, Tc)), ALU.mult))
                            P.op(eng, lambda E, o=a1, a=a0: E.tensor_tensor(
                                o, o, a, ALU.add))
                    etok = P.inc(eng)
                    P.op("s", lambda E, o=dstf[:, oc * Tc:(oc + 1) * Tc], i_=a1:
                         E.activation(o, i_, AF.Silu), deps=[etok])
                    P._convdone = P.inc("s")
            for tg in range(2):
                psb, pib, pbt = pool.acquire()
                if pbt:
                    P.wait("t", pbt)
                for kc in range(ND):
                    P.mm(psb[:, 0:4],
                         hs_tiles[kc][:, tg * 128:(tg + 1) * 128],
                         wb4[:, kc * 4:(kc + 1) * 4],
                         start=(kc == 0), stop=(kc == ND - 1))
                ttk = P.inc("t")
                P.op("s", lambda E, o=bl[:, tg * 4:(tg + 1) * 4], p=psb[0:128, 0:4]:
                     E.activation(o, p, AF.Sigmoid), deps=[ttk])
                btok = P.inc("s")
                pool.release(pib, btok)
            for (ff,) in [(q_fm,), (k_fm,)]:
                pss_, pis_, ptk = pool.acquire()
                if ptk:
                    P.wait("t", ptk)
                sqtoks = []
                for pc in range(ND):
                    P.op("s", lambda E, i_=ff[:, pc * Tc:(pc + 1) * Tc],
                         o=sq2[:, (pc % 4) * 256:(pc % 4) * 256 + 256]:
                         E.activation(o, i_, AF.Square),
                         deps=[P._convdone, btok] + (
                             [sqtoks[pc - 4]] if pc >= 4 else []))
                    stk = P.inc("s")
                    sqtoks.append(stk)
                    h = pc // 2
                    P.wait("t", stk)
                    P.mm(pss_[0:1, h * 256:(h + 1) * 256], K["ones_col"],
                         sq2[:, (pc % 4) * 256:(pc % 4) * 256 + 256],
                         start=(pc % 2 == 0), stop=(pc % 2 == 1))
                    sqtoks[pc] = P.inc("t")
                sstok = P.inc("t")
                P.op("v", lambda E, p=pss_[0:1, 0:1024]: E.tensor_scalar_add(
                    row[0:1, 0:1024], p, 1e-12), deps=[sstok])
                P.op("v", lambda E: E.reciprocal(row[0:1, 0:1024], row[0:1, 0:1024]))
                vtk = P.inc("v")
                pool.release(pis_, vtk)
                P.op("s", lambda E: E.activation(row[0:1, 1024:2048],
                                                 row[0:1, 0:1024], AF.Sqrt),
                     deps=[vtk])
                stk = P.inc("s")
                psb2, pib2, ptk2 = pool.acquire()
                if ptk2:
                    P.wait("t", ptk2)
                P.waits("t", [stk])
                for h in range(NH):
                    P.mm(psb2[:, h * 256:(h + 1) * 256], K["ones_row"],
                         row[0:1, 1024 + h * 256:1024 + (h + 1) * 256],
                         start=True, stop=True)
                btk = P.inc("t")
                sc_toks = []
                for pc in range(ND):
                    P.op("v", lambda E, o=ff[:, pc * Tc:(pc + 1) * Tc],
                         b=psb2[:, (pc // 2) * 256:(pc // 2) * 256 + 256]:
                         E.tensor_tensor(o, o, b, ALU.mult), deps=[btk])
                    sc_toks.append(P.inc("v"))
                pool.release(pib2, sc_toks[-1])
            l2tok = sc_toks[-1]
            P.waits("t", [l2tok])
            for tg in range(2):
                _delta_chunk(P, nc, pool, K, identx4_bf, ci * 2 + tg, tg, Tc,
                             q_fm, k_fm, v_fm, bl, k_tm, v_tm, mt, at, atf,
                             at_bf, xbuf, xtbuf, qbuf, x5f, xt5f, q6f,
                             x6f, xt6f, xt7, kpr, wfn, ub, otm, s0, delta_fm)
            for i in range(ND):
                dep = [P._convdone] + ([P._vstore[i]] if P._vstore[i] else [])
                P._vstore[i] = P.dma("y", v_t_d[i][:, c0:c0 + Tc],
                                     v_fm[:, i * Tc:(i + 1) * Tc], deps=dep[:1])
                dep = [P._ddone] + ([P._dstore[i]] if P._dstore[i] else [])
                P._dstore[i] = P.dma("y", d_t[i][:, c0:c0 + Tc],
                                     delta_fm[:, i * Tc:(i + 1) * Tc], deps=dep[:1])
            prev_vtok = P._ddone
        P.finish_pass()


def _delta_chunk(P, nc, pool, K, identx4_bf, cc, tg, Tc,
                 q_fm, k_fm, v_fm, bl, k_tm, v_tm, mt, at, atf,
                 at_bf, xbuf, xtbuf, qbuf, x5f, xt5f, q6f,
                 x6f, xt6f, xt7, kpr, wfn, ub, otm, s0, delta_fm):
    NH = 4
    beta = bl[:, tg * 4:(tg + 1) * 4]
    Sin = s0[cc % 2]
    Sout = s0[(cc + 1) % 2]

    if P._stok3 is not None:
        P.wait("t", P._stok3)
        P.wait("v", P._stok3)

    for (src, dst) in [(k_fm, k_tm), (v_fm, v_tm)]:
        pst, pit, ptk = pool.acquire()
        if ptk:
            P.wait("t", ptk)
        for pc in range(8):
            P.tr(pst[:, pc * 128:(pc + 1) * 128],
                 src[:, pc * Tc + tg * 128: pc * Tc + (tg + 1) * 128], K["ident"])
        ttk = P.inc("t")
        P.op("v", lambda E, o=dst, p=pst: E.tensor_copy(o, p),
             deps=[ttk, P._hs_spill])
        vtk = P.inc("v")
        pool.release(pit, vtk)
    P.waits("t", [vtk])
    pskk, pikk, ptk = pool.acquire()
    if ptk:
        P.wait("t", ptk)
    pskq, pikq, ptk2 = pool.acquire()
    if ptk2:
        P.wait("t", ptk2)
    for h in range(NH):
        hs_ = slice(h * 128, (h + 1) * 128)
        for half in range(2):
            pc = 2 * h + half
            lh = k_fm[:, pc * Tc + tg * 128: pc * Tc + (tg + 1) * 128]
            P.mm(pskk[:, hs_], lh, lh, start=(half == 0), stop=(half == 1))
            P.mm(pskq[:, hs_], lh,
                 q_fm[:, pc * Tc + tg * 128: pc * Tc + (tg + 1) * 128],
                 start=(half == 0), stop=(half == 1))
    ttok = P.inc("t")
    P.wait("v", ttok)
    for h in range(NH):
        hs_ = slice(h * 128, (h + 1) * 128)
        P.op("v", lambda E, hs_=hs_, h=h: E.scalar_tensor_tensor(
            mt[:, hs_], pskk[:, hs_], beta[:, h:h + 1], K["mask01s_x4"][:, hs_],
            ALU.mult, ALU.mult), dr=(h == 0))
        P.op("v", lambda E, hs_=hs_, h=h: E.scalar_tensor_tensor(
            at[:, hs_], pskq[:, hs_], beta[:, h:h + 1], K["mask01i_x4"][:, hs_],
            ALU.mult, ALU.mult), dr=False)
    mtok = P.inc("v")
    pool.release(pikk, mtok)
    pool.release(pikq, mtok)
    P.op("v", lambda E: E.tensor_tensor(atf, mt, K["identx4"], ALU.add))
    P.op("v", lambda E: E.tensor_copy(at_bf, atf))
    atok = P.inc("v")
    P.waits("t", [atok])

    def batch_mm(psx, lhs, rhs):
        for h in range(NH):
            hs_ = slice(h * 128, (h + 1) * 128)
            P.mm(psx[:, hs_], lhs[:, hs_], rhs[:, hs_], start=True, stop=True)
        return P.inc("t")

    def evac(dstb, psx, dep, stt=False, eng="v"):
        if stt:
            P.op(eng, lambda E: E.scalar_tensor_tensor(
                dstb, psx, -1.0, K["ident2x4"], ALU.mult, ALU.add), deps=[dep])
        elif eng == "s":
            P.op(eng, lambda E: E.activation(dstb, psx, AF.Copy), deps=[dep])
        else:
            P.op(eng, lambda E: E.tensor_copy(dstb, psx), deps=[dep])
        return P.inc(eng)

    def newton_iter(lhs_at, xin, xtin, xout, xtout, qb, skip_x=False):
        psp, pip, pt = pool.acquire()
        if pt:
            P.wait("t", pt)
        t1 = batch_mm(psp[:, 0:512], lhs_at, xin)
        qtok = evac(qb, psp[:, 0:512], t1, stt=True)
        pool.release(pip, qtok)
        P.waits("t", [qtok])
        toks = []
        if not skip_x:
            psx, pix, pt = pool.acquire()
            if pt:
                P.wait("t", pt)
            t2 = batch_mm(psx[:, 0:512], xtin, qb)
            vt = evac(xout, psx[:, 0:512], t2, eng="s")
            pool.release(pix, vt)
            toks.append(vt)
        psxt, pixt, pt = pool.acquire()
        if pt:
            P.wait("t", pt)
        t3 = batch_mm(psxt[:, 0:512], qb, xtin)
        vt2 = evac(xtout, psxt[:, 0:512], t3)
        pool.release(pixt, vt2)
        toks.append(vt2)
        P.waits("t", toks)
        P._nt = toks

    psp, pip, pt = pool.acquire()
    if pt:
        P.wait("t", pt)
    t1 = batch_mm(psp[:, 0:512], at_bf, identx4_bf)
    vtok = evac(xbuf, psp[:, 0:512], t1, stt=True)
    pool.release(pip, vtok)
    P.waits("t", [vtok])
    psxt, pixt, pt = pool.acquire()
    if pt:
        P.wait("t", pt)
    t2 = batch_mm(psxt[:, 0:512], xbuf, identx4_bf)
    vtok = evac(xtbuf, psxt[:, 0:512], t2)
    pool.release(pixt, vtok)
    P.waits("t", [vtok])
    for _ in range(4):
        newton_iter(at_bf, xbuf, xtbuf, xbuf, xtbuf, qbuf)
    P.op("v", lambda E: E.tensor_copy(x5f, xbuf), deps=P._nt)
    vtok = P.inc("v")
    pstr, pitr, pt = pool.acquire()
    if pt:
        P.wait("t", pt)
    P.waits("t", [vtok])
    for h in range(NH):
        hs_ = slice(h * 128, (h + 1) * 128)
        P.tr(pstr[:, hs_], x5f[:, hs_], K["ident"])
    t3 = P.inc("t")
    vtok = evac(xt5f, pstr[:, 0:512], t3)
    pool.release(pitr, vtok)
    P.waits("t", [vtok])
    newton_iter(atf, x5f, xt5f, x6f, xt6f, q6f)
    newton_iter(atf, x6f, xt6f, None, xt7, q6f, skip_x=True)
    P.op("v", lambda E: E.tensor_tensor(
        kpr, k_tm, beta.unsqueeze(2).broadcast_to((128, 4, 256)), ALU.mult))
    sctok = P.inc("v")
    P.waits("t", [sctok])
    psw, piw, pt = pool.acquire()
    if pt:
        P.wait("t", pt)
    for h in range(NH):
        for half in range(2):
            P.mm(psw[:, h * 256 + half * 128: h * 256 + (half + 1) * 128],
                 k_tm[:, h * 256 + half * 128: h * 256 + (half + 1) * 128],
                 xt7[:, h * 128:(h + 1) * 128], start=True, stop=True)
    t4 = P.inc("t")
    P.op("v", lambda E, p=psw: E.tensor_scalar_mul(wfn, p, -1.0), deps=[t4])
    wtok = P.inc("v")
    pool.release(piw, wtok)
    P.waits("t", [wtok])
    psu, piu, pt = pool.acquire()
    if pt:
        P.wait("t", pt)
    for h in range(NH):
        hv = slice(h * 256, (h + 1) * 256)
        P.mm(psu[:, hv], xt7[:, h * 128:(h + 1) * 128], v_tm[:, hv],
             start=True, stop=False)
        P.mm(psu[:, hv], wfn[:, h * 256:h * 256 + 128],
             Sin[:, h * 512:h * 512 + 256], start=False, stop=False)
        P.mm(psu[:, hv], wfn[:, h * 256 + 128:h * 256 + 256],
             Sin[:, h * 512 + 256:h * 512 + 512], start=False, stop=True)
    t5 = P.inc("t")
    utok = evac(ub, psu, t5)
    pool.release(piu, utok)
    P.waits("t", [utok])
    pso, pio, pt = pool.acquire()
    if pt:
        P.wait("t", pt)
    for h in range(NH):
        hv = slice(h * 256, (h + 1) * 256)
        for half in range(2):
            pc = 2 * h + half
            P.mm(pso[:, hv],
                 q_fm[:, pc * Tc + tg * 128: pc * Tc + (tg + 1) * 128],
                 Sin[:, h * 512 + half * 256: h * 512 + (half + 1) * 256],
                 start=(half == 0), stop=False)
        P.mm(pso[:, hv], at[:, h * 128:(h + 1) * 128], ub[:, hv],
             start=False, stop=True)
    t6 = P.inc("t")
    otok = evac(otm, pso, t6)
    pool.release(pio, otok)
    pss1, pis1, pt = pool.acquire()
    if pt:
        P.wait("t", pt)
    pss2, pis2, pt2 = pool.acquire()
    if pt2:
        P.wait("t", pt2)
    for h in range(NH):
        hv = slice(h * 256, (h + 1) * 256)
        psx = pss1 if h < 2 else pss2
        base = (h % 2) * 512
        for half in range(2):
            P.mm(psx[:, base + half * 256: base + (half + 1) * 256],
                 kpr[:, h * 256 + half * 128: h * 256 + (half + 1) * 128],
                 ub[:, hv], start=True, stop=True)
    t7 = P.inc("t")
    P.op("v", lambda E, p=pss1: E.tensor_tensor(
        Sout[:, 0:1024], Sin[:, 0:1024], p, ALU.add), deps=[t7])
    P.op("v", lambda E, p=pss2: E.tensor_tensor(
        Sout[:, 1024:2048], Sin[:, 1024:2048], p, ALU.add))
    stok_v = P.inc("v")
    pool.release(pis1, stok_v)
    pool.release(pis2, stok_v)
    P._stok3 = stok_v
    psof, piof, pt = pool.acquire()
    if pt:
        P.wait("t", pt)
    P.waits("t", [otok])
    for pc in range(8):
        P.tr(psof[:, pc * 128:(pc + 1) * 128], otm[:, pc * 128:(pc + 1) * 128],
             K["ident"])
    t8 = P.inc("t")
    deps = [t8]
    if getattr(P, "_ddone", None) is not None and tg == 0:
        pass
    P.op("v", lambda E: E.tensor_copy(
        delta_fm.rearrange("p (c t) -> p c t", c=8)[:, :, tg * 128:(tg + 1) * 128],
        psof.rearrange("p (c t) -> p c t", c=8)), deps=deps)
    dftok = P.inc("v")
    pool.release(piof, dftok)
    P._ddone = dftok


def emit_p3b(P, nc, s_x2, s_hs, s_v, s_delta, s_x3, W, c_in, k_itc):
    Tc = 512
    NCH = (T // Tc) if DBG["nch"] is None else DBG["nch"]
    ND = D // 128
    PAD = 30
    CB = PAD + Tc
    P.begin()
    with ExitStack() as ar:
        def sb(name, shape, dtype=F32):
            return ar.enter_context(nc.sbuf_tensor("c" + name, list(shape), dtype)).ap()
        pool = PsumPool(P, ar, 4, name="cpp")
        K = load_consts(P, nc, ar, c_in, ["ones_col", "ones_row", "ident",
                                          "fgb2row"], tag="c")
        fsw = sb("fsw", [128, 8 * 3])
        flw_ = sb("flw", [128, 8 * 31])
        fgb1c = sb("fgb1c", [128, 8])
        fw2 = sb("fw2", [128, 8 * 4])
        onwc = sb("onwc", [128, 8])
        for i in range(8):
            P.dma("y", fsw[:, i * 3:(i + 1) * 3],
                  W["firs"].rearrange("(c p) n -> c p n", p=128)[i])
            P.dma("y", flw_[:, i * 31:(i + 1) * 31],
                  W["firl"].rearrange("(c p) n -> c p n", p=128)[i])
            P.dma("y", fgb1c[:, i:i + 1],
                  W["fgb1"].rearrange("(c p) n -> c p n", p=128)[i])
            P.dma("y", fw2[:, i * 4:(i + 1) * 4],
                  W["fgw2T"].rearrange("(c p) n -> c p n", p=128)[i])
            P.dma("y", onwc[:, i:i + 1],
                  W["onw"].rearrange("(c p) n -> c p n", p=128)[i])
        cbuf = sb("cb", [128, 8 * CB])
        tails = sb("tails", [128, 8 * PAD])
        P.op("g", lambda E: E.memset(tails, 0.0))
        tailtok = P.inc("g")
        delta = sb("delta", [128, 8 * Tc])
        hsb = sb("hs", [128, 8 * Tc])
        short = sb("short", [128, 8 * Tc])
        longb = sb("long", [128, 8 * Tc])
        ghid = sb("ghid", [128, 8 * Tc])
        accv = sb("accv", [128, 2 * Tc])
        accg = sb("accg", [128, 2 * Tc])
        gscr = sb("gscr", [128, 2 * Tc])
        fdg = sb("fdg", [128, 3 * 31 * 128])
        P.dma("y", fdg, W["fdiag"])
        xres = sb("xres", [128, 2 * Tc])
        nf = sb("nf", [128, Tc])
        lgt = sb("lgt", [128, 16])
        wgt = sb("wgt", [128, 16])
        scol = sb("scol", [128, 2])
        wrow = sb("wrow", [1, 4 * Tc])
        rrow = sb("rrow", [1, 4 * Tc])
        fgp = sb("fgp", [128, 2 * 14 * 128])
        opp = sb("opp", [128, 2 * 1024])
        sq2 = ghid

        x2_t = s_x2.rearrange("(c p) n -> c p n", p=128)
        hs_t = s_hs.rearrange("(c p) n -> c p n", p=128)
        v_t = s_v.rearrange("(c p) n -> c p n", p=128)
        d_t = s_delta.rearrange("(c p) n -> c p n", p=128)
        x3_t = s_x3.rearrange("(c p) n -> c p n", p=128)
        fg_t = W["fgw1T"].rearrange("(c p) n -> c p n", p=128)
        fgs_t = W["fgw1s"].rearrange("(c p) n -> c p n", p=128)
        ops_t = W["oprs"].rearrange("(c p) n -> c p n", p=128)
        op_t = W["oprT"].rearrange("(c p) n -> c p n", p=128)
        wtok0 = ("dy", P.cnt["dy"])
        for e in ("t", "v", "s", "g"):
            P.wait(e, wtok0)
            P.wait(e, tailtok)
        P._fgpan = {}
        P._oppan = {}
        prev_vtok = None
        prev_store = [None] * ND
        for ci in range(NCH):
            c0 = ci * Tc
            if prev_vtok is not None:
                spill = ("dy", P.cnt["dy"])
                for e in ("t", "v", "s", "g"):
                    P.wait(e, prev_vtok)
                    P.wait(e, spill)
            pdep = [prev_vtok] if prev_vtok else []
            dts = []
            for i in range(ND):
                cb = cbuf[:, i * CB:(i + 1) * CB]
                tl = tails[:, i * PAD:(i + 1) * PAD]
                P.op("v", lambda E, cb=cb, tl=tl: E.tensor_copy(cb[:, 0:PAD], tl),
                     deps=pdep)
                vtk = P.inc("v")
                dts.append(P.dma("y", cb[:, PAD:PAD + Tc], v_t[i][:, c0:c0 + Tc],
                                 deps=[vtk]))
                dts.append(P.dma("y", delta[:, i * Tc:(i + 1) * Tc],
                                 d_t[i][:, c0:c0 + Tc], deps=pdep))
                dts.append(P.dma("y", hsb[:, i * Tc:(i + 1) * Tc],
                                 hs_t[i][:, c0:c0 + Tc], deps=pdep))
            ldtok = ("dy", P.cnt["dy"])
            for e in ("t", "v", "s", "g"):
                P.wait(e, ldtok)
            cbs = [cbuf[:, i * CB:(i + 1) * CB] for i in range(ND)]
            for i in range(ND):
                P.op("v", lambda E, cb=cbs[i], tl=tails[:, i * PAD:(i + 1) * PAD]:
                     E.tensor_copy(tl, cb[:, Tc:Tc + PAD]), dr=(i == 0))
            NPE = 3
            ps_a, pi_a, ptk = pool.acquire()
            if ptk:
                P.wait("t", ptk)
            ps_b, pi_b, ptk2 = pool.acquire()
            if ptk2:
                P.wait("t", ptk2)
            pe_toks = []
            for b in range(NPE):
                psl = (ps_a if b < 2 else ps_b)[:, (b % 2) * 512:(b % 2) * 512 + 512]
                for j in range(31):
                    P.mm(psl, fdg[:, (b * 31 + j) * 128:(b * 31 + j + 1) * 128],
                         cbs[b][:, j:j + Tc], start=(j == 0), stop=(j == 30))
                pe_toks.append(P.inc("t"))
                P.op("s", lambda E, o=longb[:, b * Tc:(b + 1) * Tc], p=psl:
                     E.activation(o, p, AF.Copy), deps=[pe_toks[-1]])
            evac_tok = P.inc("s")
            pool.release(pi_a, evac_tok)
            pool.release(pi_b, evac_tok)
            for i in range(NPE, ND):
                P.op("v", lambda E, cb=cbs[i], w_=flw_[:, i * 31:i * 31 + 1],
                     o=longb[:, i * Tc:(i + 1) * Tc]:
                     E.tensor_scalar(o, cb[:, 0:Tc], w_, None, ALU.mult),
                     dr=(i == NPE))
            for j in range(1, 31):
                for i in range(NPE, ND):
                    P.op("v", lambda E, cb=cbs[i], j=j,
                         o=longb[:, i * Tc:(i + 1) * Tc],
                         w_=flw_[:, i * 31 + j:i * 31 + j + 1]:
                         E.scalar_tensor_tensor(o, cb[:, j:j + Tc], w_, o,
                                                ALU.mult, ALU.add), dr=(i == NPE))
            dve_tok = P.inc("v")
            for i in range(ND):
                P.op("g", lambda E, cb=cbs[i], o=short[:, i * Tc:(i + 1) * Tc],
                     w_=fsw[:, i * 3:i * 3 + 1]:
                     E.tensor_tensor(o, cb[:, PAD - 2:PAD - 2 + Tc],
                                     w_.broadcast_to((128, Tc)), ALU.mult),
                     dr=True)
                P.op("g", lambda E, cb=cbs[i], o=gscr[:, 0:Tc],
                     w_=fsw[:, i * 3 + 1:i * 3 + 2]:
                     E.tensor_tensor(o, cb[:, PAD - 1:PAD - 1 + Tc],
                                     w_.broadcast_to((128, Tc)), ALU.mult),
                     dr=False)
                P.op("g", lambda E, cb=cbs[i], o=gscr[:, Tc:2 * Tc],
                     w_=fsw[:, i * 3 + 2:i * 3 + 3]:
                     E.tensor_tensor(o, cb[:, PAD:PAD + Tc],
                                     w_.broadcast_to((128, Tc)), ALU.mult),
                     dr=False)
                P.op("g", lambda E, o=short[:, i * Tc:(i + 1) * Tc], a=gscr[:, 0:Tc]:
                     E.tensor_tensor(o, o, a, ALU.add))
                P.op("g", lambda E, o=short[:, i * Tc:(i + 1) * Tc],
                     a=gscr[:, Tc:2 * Tc]:
                     E.tensor_tensor(o, o, a, ALU.add))
            convtok = dve_tok
            mean_dst = {"sm": [accv[:, 0:Tc], accv[:, Tc:2 * Tc]],
                        "lm": [accg[:, 0:Tc], accg[:, Tc:2 * Tc]],
                        "dm": [xres[:, 0:Tc], xres[:, Tc:2 * Tc]]}
            srcs = {"sm": short, "lm": longb, "dm": delta}
            mtriples = []
            for nm in ["sm", "lm", "dm"]:
                ssrc = srcs[nm]
                for half in range(2):
                    tiles = [ssrc[:, (2 * h + half) * Tc:(2 * h + half + 1) * Tc]
                             for h in range(4)]
                    mtriples.append((mean_dst[nm][half], tiles))
            for k, (o, tl) in enumerate(mtriples):
                P.op("g", lambda E, o=o, a=tl[0], b=tl[1]:
                     E.tensor_tensor(o, a, b, ALU.add),
                     deps=[convtok, evac_tok] if k == 0 else (), dr=(k == 0))
            for ph in (2, 3):
                for k, (o, tl) in enumerate(mtriples):
                    P.op("g", lambda E, o=o, a=tl[ph]:
                         E.tensor_tensor(o, o, a, ALU.add), dr=(k == 0))
            for k, (o, tl) in enumerate(mtriples):
                P.op("g", lambda E, o=o: E.tensor_scalar_mul(o, o, 0.25),
                     dr=(k == 0))
            meantok = P.inc("g")
            P.wait("v", meantok)
            P.waits("t", [meantok])
            gate_in = ([hsb[:, i * Tc:(i + 1) * Tc] for i in range(8)]
                       + mean_dst["sm"] + mean_dst["lm"] + mean_dst["dm"])
            for oc in range(ND):
                slot = fgp[:, (oc % 2) * 14 * 128:(oc % 2) * 14 * 128 + 14 * 128]
                dep = [P._fgpan[oc % 2]] if P._fgpan.get(oc % 2) else []
                ptok = P.dma("y", slot, fgs_t[oc], deps=dep)
                ps, pi, ptk = pool.acquire()
                if ptk:
                    P.wait("t", ptk)
                P.wait("t", ptok)
                psl = ps[:, 0:Tc]
                for kc in range(14):
                    P.mm(psl, slot[:, kc * 128:(kc + 1) * 128], gate_in[kc],
                         start=(kc == 0), stop=(kc == 13))
                P._fgpan[oc % 2] = P.inc("t")
                P.op("s", lambda E, o=ghid[:, oc * Tc:(oc + 1) * Tc], p=psl,
                     b_=fgb1c[:, oc:oc + 1]: E.activation(o, p, AF.Gelu, bias=b_),
                     deps=[P._fgpan[oc % 2]])
                gtok = P.inc("s")
                pool.release(pi, gtok)
            P.wait("t", gtok)
            psl4, pil4, ptk = pool.acquire()
            if ptk:
                P.wait("t", ptk)
            pswr, piwr, ptk2 = pool.acquire()
            if ptk2:
                P.wait("t", ptk2)
            pswr2, piwr2, ptk3 = pool.acquire()
            if ptk3:
                P.wait("t", ptk3)
            for tg in range(4):
                reg = psl4[:, tg * 256:tg * 256 + 4]
                for kc in range(ND):
                    P.mm(reg, ghid[:, kc * Tc + tg * 128: kc * Tc + (tg + 1) * 128],
                         fw2[:, kc * 4:(kc + 1) * 4],
                         start=(kc == 0), stop=(kc == ND - 1))
                ltk = P.inc("t")
                lg = lgt[:, tg * 4:(tg + 1) * 4]
                wg = wgt[:, tg * 4:(tg + 1) * 4]
                P.op("v", lambda E, lg=lg, p=reg: E.tensor_tensor(
                    lg, p, K["fgb2row"], ALU.add), deps=[ltk])
                P.op("v", lambda E, lg=lg: E.tensor_scalar(
                    lg, lg, k_itc, None, ALU.mult))
                P.op("v", lambda E, lg=lg: E.reduce_max(
                    scol[:, 0:1], lg, axis=mybir.AxisListType.X))
                P.op("v", lambda E, lg=lg: E.tensor_scalar(
                    lg, lg, scol[:, 0:1], None, ALU.subtract))
                vtk = P.inc("v")
                P.op("s", lambda E, lg=lg: E.activation(lg, lg, AF.Exp),
                     deps=[vtk])
                stk = P.inc("s")
                P.op("v", lambda E, lg=lg: E.reduce_sum(
                    scol[:, 0:1], lg, axis=mybir.AxisListType.X), deps=[stk])
                P.op("v", lambda E: E.reciprocal(scol[:, 0:1], scol[:, 0:1]))
                P.op("v", lambda E, lg=lg, wg=wg: E.tensor_scalar(
                    wg, lg, scol[:, 0:1], None, ALU.mult))
                P.op("v", lambda E, wg=wg: E.tensor_scalar(
                    scol[:, 1:2], wg[:, 3:4], 0.05, None, ALU.mult))
                P.op("v", lambda E, wg=wg: E.tensor_tensor(
                    wg[:, 0:2], wg[:, 0:2],
                    scol[:, 1:2].broadcast_to((128, 2)), ALU.max))
                P.op("v", lambda E, wg=wg: E.reduce_sum(
                    scol[:, 0:1], wg, axis=mybir.AxisListType.X))
                P.op("v", lambda E: E.reciprocal(scol[:, 0:1], scol[:, 0:1]))
                P.op("v", lambda E, wg=wg: E.tensor_scalar(
                    wg, wg, scol[:, 0:1], None, ALU.mult))
                wtk = P.inc("v")
                P.waits("t", [wtk])
                for kk in range(4):
                    psdst = pswr if kk < 2 else pswr2
                    P.tr(psdst[0:1, (kk % 2) * 512 + tg * 128:
                               (kk % 2) * 512 + (tg + 1) * 128],
                         wg[:, kk:kk + 1], K["ident"])
            trtok = P.inc("t")
            pool.release(pil4, trtok)
            P.op("v", lambda E, p=pswr: E.tensor_copy(wrow[0:1, 0:1024],
                                                      p[0:1, 0:1024]),
                 deps=[trtok])
            P.op("v", lambda E, p=pswr2: E.tensor_copy(wrow[0:1, 1024:2048],
                                                       p[0:1, 0:1024]))
            wrtok = P.inc("v")
            pool.release(piwr, wrtok)
            pool.release(piwr2, wrtok)
            psb1, pib1, ptk = pool.acquire()
            if ptk:
                P.wait("t", ptk)
            psb2, pib2, ptk2 = pool.acquire()
            if ptk2:
                P.wait("t", ptk2)
            P.waits("t", [wrtok])
            wb = []
            for kk in range(4):
                psdst = psb1 if kk < 2 else psb2
                reg = psdst[:, (kk % 2) * 512:(kk % 2) * 512 + 512]
                P.mm(reg, K["ones_row"], wrow[0:1, kk * 512:(kk + 1) * 512],
                     start=True, stop=True)
                wb.append(reg)
            bctok = P.inc("t")
            for i in range(ND):
                P.op("v", lambda E, o=short[:, i * Tc:(i + 1) * Tc], b=wb[0]:
                     E.tensor_tensor(o, o, b, ALU.mult),
                     deps=[bctok] if i == 0 else (), dr=(i == 0))
            for i in range(ND):
                P.op("v", lambda E, o=longb[:, i * Tc:(i + 1) * Tc], b=wb[1]:
                     E.tensor_tensor(o, o, b, ALU.mult), dr=(i == 0))
            for i in range(ND):
                P.op("v", lambda E, o=short[:, i * Tc:(i + 1) * Tc],
                     a=longb[:, i * Tc:(i + 1) * Tc]:
                     E.tensor_tensor(o, o, a, ALU.add), dr=(i == 0))
            for i in range(ND):
                P.op("v", lambda E, o=delta[:, i * Tc:(i + 1) * Tc], b=wb[2]:
                     E.tensor_tensor(o, o, b, ALU.mult), dr=(i == 0))
            for i in range(ND):
                P.op("v", lambda E, o=short[:, i * Tc:(i + 1) * Tc],
                     a=delta[:, i * Tc:(i + 1) * Tc]:
                     E.tensor_tensor(o, o, a, ALU.add), dr=(i == 0))
            for i in range(ND):
                P.op("v", lambda E, o=delta[:, i * Tc:(i + 1) * Tc],
                     cb=cbs[i], b=wb[3]:
                     E.tensor_tensor(o, cb[:, PAD:PAD + Tc], b, ALU.mult),
                     dr=(i == 0))
            ws_toks = []
            for i in range(ND):
                P.op("v", lambda E, o=short[:, i * Tc:(i + 1) * Tc],
                     a=delta[:, i * Tc:(i + 1) * Tc]:
                     E.tensor_tensor(o, o, a, ALU.add), dr=(i == 0))
                ws_toks.append(P.inc("v"))
            pool.release(pib1, ws_toks[-1])
            pool.release(pib2, ws_toks[-1])
            pss1, pis1, ptk = pool.acquire()
            if ptk:
                P.wait("t", ptk)
            pss2, pis2, ptk2 = pool.acquire()
            if ptk2:
                P.wait("t", ptk2)
            sq_toks = []
            for i in range(ND):
                P.op("s", lambda E, i_=short[:, i * Tc:(i + 1) * Tc],
                     o=sq2[:, (i % 4) * Tc:(i % 4) * Tc + Tc]:
                     E.activation(o, i_, AF.Square),
                     deps=[ws_toks[i]] + ([sq_toks[i - 4]] if i >= 4 else []))
                stk = P.inc("s")
                P.wait("t", stk)
                h = i // 2
                psdst = pss1 if h < 2 else pss2
                P.mm(psdst[0:1, (h % 2) * 512:(h % 2) * 512 + 512], K["ones_col"],
                     sq2[:, (i % 4) * Tc:(i % 4) * Tc + Tc],
                     start=(i % 2 == 0), stop=(i % 2 == 1))
                sq_toks.append(P.inc("t"))
            sstok = P.inc("t")
            P.op("v", lambda E, p=pss1: E.tensor_scalar(
                rrow[0:1, 0:1024], p[0:1, 0:1024], 1.0 / 256, 1e-5,
                ALU.mult, ALU.add), deps=[sstok])
            P.op("v", lambda E, p=pss2: E.tensor_scalar(
                rrow[0:1, 1024:2048], p[0:1, 0:1024], 1.0 / 256, 1e-5,
                ALU.mult, ALU.add))
            P.op("v", lambda E: E.reciprocal(rrow[0:1, 0:2048], rrow[0:1, 0:2048]))
            vtk = P.inc("v")
            pool.release(pis1, vtk)
            pool.release(pis2, vtk)
            P.op("s", lambda E: E.activation(wrow[0:1, 0:2048],
                                             rrow[0:1, 0:2048], AF.Sqrt),
                 deps=[vtk])
            rstok = P.inc("s")
            psb1, pib1, ptk = pool.acquire()
            if ptk:
                P.wait("t", ptk)
            psb2, pib2, ptk2 = pool.acquire()
            if ptk2:
                P.wait("t", ptk2)
            P.waits("t", [rstok])
            rb = []
            for h in range(4):
                psdst = psb1 if h < 2 else psb2
                reg = psdst[:, (h % 2) * 512:(h % 2) * 512 + 512]
                P.mm(reg, K["ones_row"], wrow[0:1, h * 512:(h + 1) * 512],
                     start=True, stop=True)
                rb.append(reg)
            rbtok = P.inc("t")
            rm_toks = []
            for i in range(ND):
                P.op("v", lambda E, o=short[:, i * Tc:(i + 1) * Tc],
                     b=rb[i // 2]: E.tensor_tensor(o, o, b, ALU.mult),
                     deps=[rbtok] if i == 0 else (), dr=(i == 0))
            for i in range(ND):
                P.op("v", lambda E, o=short[:, i * Tc:(i + 1) * Tc],
                     w_=onwc[:, i:i + 1]: E.tensor_scalar(
                     o, o, w_, None, ALU.mult), dr=(i == 0))
                rm_toks.append(P.inc("v"))
            pool.release(pib1, rm_toks[-1])
            pool.release(pib2, rm_toks[-1])
            P.waits("t", rm_toks)
            xres_tok = [trtok, trtok]
            for oc in range(ND):
                slot = opp[:, (oc % 2) * 1024:(oc % 2) * 1024 + 1024]
                dep = [P._oppan[oc % 2]] if P._oppan.get(oc % 2) else []
                ptok = P.dma("y", slot, ops_t[oc], deps=dep)
                xrtok = P.dma("y", xres[:, (oc % 2) * Tc:(oc % 2) * Tc + Tc],
                              x2_t[oc][:, c0:c0 + Tc],
                              deps=[xres_tok[oc % 2]])
                ps, pi, ptk = pool.acquire()
                if ptk:
                    P.wait("t", ptk)
                P.wait("t", ptok)
                psl = ps[:, 0:Tc]
                for kc in range(ND):
                    P.mm(psl, slot[:, kc * 128:(kc + 1) * 128],
                         short[:, kc * Tc:(kc + 1) * Tc],
                         start=(kc == 0), stop=(kc == ND - 1))
                P._oppan[oc % 2] = P.inc("t")
                deps = [P._oppan[oc % 2], xrtok]
                if prev_store[oc]:
                    deps.append(prev_store[oc])
                P.op("v", lambda E, p=psl, xr=xres[:, (oc % 2) * Tc:(oc % 2) * Tc + Tc]:
                     E.tensor_tensor(nf, p, xr, ALU.add), deps=deps)
                vtk = P.inc("v")
                xres_tok[oc % 2] = vtk
                prev_store[oc] = P.dma("y", x3_t[oc][:, c0:c0 + Tc], nf,
                                       deps=[vtk])
                P.wait("v", prev_store[oc])
                pool.release(pi, vtk)
            prev_vtok = vtk
        P.finish_pass()



def make_inputs(args):
    f32 = np.float32
    g = {k: np.asarray(v, dtype=f32) for k, v in args.items()}
    m = {}
    m["wq"] = np.ascontiguousarray(g["L0_attn_a_Wq"])
    m["wk"] = np.ascontiguousarray(g["L0_attn_a_Wk"])
    m["wv"] = np.ascontiguousarray(g["L0_attn_a_Wv"])
    m["wba"] = np.ascontiguousarray(np.concatenate(
        [g["L0_attn_a_Wb"], g["L0_attn_a_Wa"]], axis=1))
    m["ln1wb"] = np.ascontiguousarray(np.stack(
        [g["L0_attn_a_ln1_w"], g["L0_attn_a_ln1_b"]], axis=1))
    m["wo"] = np.ascontiguousarray(g["L0_attn_a_Wo"])
    m["w1"] = np.ascontiguousarray(g["L0_attn_a_W1"])
    m["b1"] = np.ascontiguousarray(g["L0_attn_a_b1"][:, None])
    m["w2"] = np.ascontiguousarray(g["L0_attn_a_W2"])
    m["w2s"] = np.ascontiguousarray(
        g["L0_attn_a_W2"].reshape(32, 128, 8, 128).transpose(2, 1, 0, 3)
        .reshape(1024, 4096))
    m["b2"] = np.ascontiguousarray(g["L0_attn_a_b2"][:, None])
    m["ln2wb"] = np.ascontiguousarray(np.stack(
        [g["L0_attn_a_ln2_w"], g["L0_attn_a_ln2_b"]], axis=1))
    m["wfcT"] = np.ascontiguousarray(g["L0_ffn_W_fc"].T)
    m["flnwb"] = np.ascontiguousarray(np.stack(
        [g["L0_ffn_ln_w"], g["L0_ffn_ln_b"]], axis=1))
    m["wprT"] = np.ascontiguousarray(g["L0_ffn_W_proj"].T)
    m["pqT"] = np.ascontiguousarray(g["L0_attn_b_q_proj_w"].T)
    m["pkT"] = np.ascontiguousarray(g["L0_attn_b_k_proj_w"].T)
    m["pvT"] = np.ascontiguousarray(g["L0_attn_b_v_proj_w"].T)
    m["pbT"] = np.ascontiguousarray(g["L0_attn_b_b_proj_w"].T)
    m["qcw"] = np.ascontiguousarray(g["L0_attn_b_q_conv_w"])
    m["kcw"] = np.ascontiguousarray(g["L0_attn_b_k_conv_w"])
    m["vcw"] = np.ascontiguousarray(g["L0_attn_b_v_conv_w"])
    m["firs"] = np.ascontiguousarray(g["L0_attn_b_fir_short_w"].reshape(D, 3))
    m["firl"] = np.ascontiguousarray(g["L0_attn_b_fir_long_w"].reshape(D, 31))
    fl = m["firl"]
    fd = np.zeros((128, 3 * 31 * 128), np.float32)
    arp = np.arange(128)
    for b_ in range(3):
        for j_ in range(31):
            fd[arp, (b_ * 31 + j_) * 128 + arp] = fl[b_ * 128 + arp, j_]
    m["fdiag"] = fd
    m["fgw1T"] = np.ascontiguousarray(g["L0_attn_b_fg_w1"].T)
    m["fgw1s"] = np.ascontiguousarray(
        g["L0_attn_b_fg_w1"].T.reshape(14, 128, 8, 128).transpose(2, 1, 0, 3)
        .reshape(1024, 1792))
    m["fgb1"] = np.ascontiguousarray(g["L0_attn_b_fg_b1"][:, None])
    m["fgw2T"] = np.ascontiguousarray(g["L0_attn_b_fg_w2"].T)
    m["ltemp"] = np.ascontiguousarray(g["L0_attn_b_logit_temp"][:, None])
    m["onw"] = np.ascontiguousarray(np.tile(g["L0_attn_b_o_norm_w"], 4)[:, None])
    m["oprT"] = np.ascontiguousarray(g["L0_attn_b_o_proj_w"].T)
    m["oprs"] = np.ascontiguousarray(
        g["L0_attn_b_o_proj_w"].T.reshape(8, 128, 8, 128).transpose(2, 1, 0, 3)
        .reshape(1024, 1024))
    one = np.ones
    tri_i = np.tril(one((128, 128), f32), 0)
    m["c_ones_col"] = one((128, 1), f32)
    m["c_ones_row"] = one((1, 128), f32)
    m["c_ones128"] = one((128, 128), f32)
    m["c_negtriu"] = -np.triu(one((128, 128), f32), 0)
    m["c_tri01i_x8"] = np.tile(np.triu(one((128, 128), f32), 0), (1, 8))
    maskb = np.where(np.triu(one((128, 128), bool), 1), 0.0, NEG).astype(f32)
    m["c_maskb_x8"] = np.tile(maskb, (1, 8))
    m["c_ident"] = np.eye(128, dtype=f32)
    m["c_identx8"] = np.tile(np.eye(128, dtype=f32), (1, 8))
    m["c_ident2x8"] = 2.0 * m["c_identx8"]
    m["c_mask01s_x4"] = np.tile(np.triu(one((128, 128), f32), 1), (1, 4))
    m["c_mask01i_x4"] = np.tile(np.triu(one((128, 128), f32), 0), (1, 4))
    m["c_identx4"] = np.tile(np.eye(128, dtype=f32), (1, 4))
    m["c_ident2x4"] = 2.0 * m["c_identx4"]
    m["c_tri_incl"] = np.tril(one((128, 128), f32), 0).T
    m["c_rev_strict"] = np.tril(one((128, 128), f32), -1)
    m["c_fgb2row"] = np.tile(g["L0_attn_b_fg_b2"][None, :], (128, 1))
    return m


def kernel(**inputs):
    nc = build_program()
    shared = make_inputs(inputs)
    x = np.asarray(inputs["L0_attn_a_x"], dtype=np.float32)
    B = x.shape[0]
    in_maps = []
    for b in range(B):
        im = dict(shared)
        im["x"] = np.ascontiguousarray(x[b].T)
        in_maps.append(im)
    res = run_bass_kernel_spmd(nc, in_maps, core_ids=list(range(B)))
    outs = [np.ascontiguousarray(r["out"].T) for r in res.results]
    return np.stack(outs, axis=0)
